# version 01
# baseline (speedup 1.0000x reference)
import contextlib
import numpy as np
import concourse.bass as bass
import concourse.mybir as mybir
from concourse.bass_utils import run_bass_kernel_spmd

F32 = mybir.dt.float32
BF16 = mybir.dt.bfloat16
I32 = mybir.dt.int32
AF = mybir.ActivationFunctionType
OP = mybir.AluOpType
AX = mybir.AxisListType

NCORE = 8
T = 4096
D = 1024
NT = T // 128
ND = D // 128


class Prog:

    def __init__(self):
        self.nc = bass.Bass("TRN2", target_bir_lowering=False)
        self.q = {"tensor": [], "vector": [], "scalar": [], "sync": [], "gpsimd": []}
        self.cnt = 0
        self.st = contextlib.ExitStack()
        self._names = {}
        self._groups = {}

    def name(self, base):
        i = self._names.get(base, 0)
        self._names[base] = i + 1
        return f"{base}{i}"

    def dram(self, name, shape, dtype, kind="ExternalInput"):
        return self.nc.dram_tensor(name, list(shape), dtype, kind=kind).ap()

    def _alloc(self, cm, grp):
        h = cm.__enter__()
        if grp is None:
            self.st.callback(cm.__exit__, None, None, None)
        else:
            self._groups.setdefault(grp, []).append(cm)
        return h.ap() if hasattr(h, "ap") else h

    def sb(self, shape, dtype, base="sb", grp=None):
        return self._alloc(self.nc.sbuf_tensor(self.name(base), list(shape), dtype), grp)

    def ps(self, shape, dtype=F32, base="ps", grp=None):
        return self._alloc(self.nc.psum_tensor(self.name(base), list(shape), dtype), grp)

    def free_grp(self, grp):
        for cm in reversed(self._groups.pop(grp, [])):
            cm.__exit__(None, None, None)

    def emit(self, eng, fn, inc=1, wait=True):
        w = self.cnt if wait else None
        self.q[eng].append((w, fn, inc))
        self.cnt += inc

    def dma(self, out, in_, wait=True):
        self.emit("sync", lambda e, o=out, i=in_: e.dma_start(out=o, in_=i), inc=16, wait=wait)

    def mm(self, out, lhsT, rhs, start=True, stop=True, wait=True, sgc=False):
        self.emit("tensor",
                  lambda e, o=out, l=lhsT, r=rhs, s0=start, s1=stop, g=sgc:
                  e.matmul(o, l, r, start=s0, stop=s1, skip_group_check=g),
                  wait=wait)

    def trans(self, out, in_, ident):
        self.emit("tensor", lambda e, o=out, i=in_, d=ident: e.transpose(o, i, d))

    def act(self, out, in_, func, bias=None, scale=None):
        if isinstance(bias, float):
            bias = np.float32(bias)
        if isinstance(scale, float):
            scale = np.float32(scale)

        def f(e, o=out, i=in_, fu=func, b=bias, s=scale):
            kw = {}
            if b is not None:
                kw["bias"] = b
            if s is not None:
                kw["scale"] = s
            return e.activation(o, i, fu, **kw)
        self.emit("scalar", f)

    def scopy(self, out, in_):
        self.emit("scalar", lambda e, o=out, i=in_: e.copy(o, i))

    def tt(self, out, a, b, op):
        self.emit("vector", lambda e, o=out, x=a, y=b, p=op: e.tensor_tensor(o, x, y, p))

    def ts(self, out, a, s1, op0, s2=None, op1=None):
        def f(e, o=out, x=a, a1=s1, a2=s2, p0=op0, p1=op1):
            if a2 is None:
                return e.tensor_scalar(o, x, a1, None, p0)
            return e.tensor_scalar(o, x, a1, a2, p0, p1)
        self.emit("vector", f)

    def stt(self, out, in0, scalar, in1, op0, op1):
        self.emit("vector",
                  lambda e, o=out, x=in0, s=scalar, y=in1, p0=op0, p1=op1:
                  e.scalar_tensor_tensor(o, x, s, y, p0, p1))

    def red(self, out, in_, op=OP.add):
        self.emit("vector", lambda e, o=out, i=in_, p=op: e.tensor_reduce(o, i, AX.X, p))

    def vcopy(self, out, in_):
        self.emit("vector", lambda e, o=out, i=in_: e.tensor_copy(o, i))

    def recip(self, out, in_):
        self.emit("vector", lambda e, o=out, i=in_: e.reciprocal(o, i))

    def memset(self, ap, const):
        self.emit("vector", lambda e, o=ap, c=const: e.memset(o, c))

    def build(self):
        nc = self.nc
        sem = self.st.enter_context(nc.semaphore("ser"))
        block = self.st.enter_context(nc.Block())
        total = self.cnt
        q = self.q

        def runner(eng_name):
            def run(eng):
                for w, fn, inc in q[eng_name]:
                    if w is not None and w > 0:
                        eng.wait_ge(sem, w)
                    inst = fn(eng)
                    inst.then_inc(sem, inc)
                if eng_name == "vector":
                    eng.wait_ge(sem, total)
            return run

        block.sync(runner("sync"))
        block.tensor(runner("tensor"))
        block.scalar(runner("scalar"))
        block.vector(runner("vector"))
        if q["gpsimd"]:
            block.gpsimd(runner("gpsimd"))
        for grp in reversed(list(self._groups)):
            self.free_grp(grp)
        self.st.close()
        return nc



def load_consts(p):
    c = {}
    ones = p.dram("c_ones", [1, 128], F32)
    ident = p.dram("c_ident", [128, 128], BF16)
    c["ones_sb"] = p.sb([1, 128], F32, "ones")
    c["id_sb"] = p.sb([128, 128], BF16, "ident")
    p.dma(c["ones_sb"][:, :], ones[:, :])
    p.dma(c["id_sb"][:, :], ident[:, :])
    return c


def const_inputs():
    return {
        "c_ones": np.ones((1, 128), np.float32),
        "c_ident": np.eye(128, dtype=np.float32).astype(bf16_np()),
    }


_BF16 = None


def bf16_np():
    global _BF16
    if _BF16 is None:
        import ml_dtypes
        _BF16 = ml_dtypes.bfloat16
    return _BF16


def replicate_row(p, c, dst_sb, row_sb, n, scale=None):
    pr = p.ps([128, 512], F32, "prep")
    for j in range(0, n, 512):
        w = min(512, n - j)
        p.mm(pr[:, :w], c["ones_sb"][:, :], row_sb[:, j:j + w])
        if scale is None:
            p.scopy(dst_sb[:, j:j + w], pr[:, :w])
        else:
            p.act(dst_sb[:, j:j + w], pr[:, :w], AF.Copy, scale=scale)


def load_row(p, hbm_vec, n, dtype=F32):
    r = p.sb([1, n], dtype, "row")
    src = hbm_vec.rearrange("(one n) -> one n", one=1) if len(hbm_vec.shape) == 1 else hbm_vec
    p.dma(r[:, :], src)
    return r


def layernorm_tile(p, dst, xt, tmp, m, v, width=None, gamma_rep=None, beta_rep=None, eps=1e-5):
    W = width or xt.shape[-1]
    p.red(m[:, 0:1], xt[:, :W])
    p.ts(m[:, 0:1], m[:, 0:1], 1.0 / W, OP.mult)
    p.act(tmp[:, :W], xt[:, :W], AF.Square)
    p.red(v[:, 0:1], tmp[:, :W])
    p.ts(v[:, 0:1], v[:, 0:1], 1.0 / W, OP.mult)
    p.stt(v[:, 0:1], m[:, 0:1], m[:, 0:1], v[:, 0:1], OP.mult, OP.subtract)
    p.act(v[:, 0:1], v[:, 0:1], AF.Ln, bias=eps, scale=-1.0)
    p.act(v[:, 0:1], v[:, 0:1], AF.Exp, scale=-0.5)
    p.ts(tmp[:, :W], xt[:, :W], m[:, 0:1], OP.subtract)
    if gamma_rep is None:
        p.ts(dst[:, :W], tmp[:, :W], v[:, 0:1], OP.mult)
    else:
        p.ts(tmp[:, :W], tmp[:, :W], v[:, 0:1], OP.mult)
        p.stt(dst[:, :W], tmp[:, :W], 1.0, gamma_rep[:, :W], OP.mult, OP.mult)
        p.tt(dst[:, :W], dst[:, :W], beta_rep[:, :W], OP.add)


def transpose_nat_to_T(p, c, dstT_list, nat_bf, tcol, pbf):
    for j in range(len(dstT_list)):
        p.trans(pbf[:, :128], nat_bf[:, j * 128:(j + 1) * 128], c["id_sb"][:, :])
        p.vcopy(dstT_list[j][:, tcol * 128:(tcol + 1) * 128], pbf[:, :128])



def build_l1_ffn():
    p = Prog()
    c = load_consts(p)
    x_h = p.dram("x", [T, D], F32)
    y_h = p.dram("y", [T, D], F32, kind="ExternalOutput")
    g_h = p.dram("ln_g", [D], F32)
    b_h = p.dram("ln_b", [D], F32)
    q1_h = p.dram("fc1_q", [D, 4096], I32)
    s1_h = p.dram("fc1_s", [8, 4096], F32)
    b1_h = p.dram("fc1_b", [4096], F32)
    q2_h = p.dram("fc2_q", [4096, D], I32)
    s2_h = p.dram("fc2_s", [32, D], F32)
    b2_h = p.dram("fc2_b", [D], F32)

    F = 4096
    NF = F // 128

    w1 = [p.sb([128, F], BF16, "w1") for _ in range(ND)]
    w2 = [p.sb([128, D], BF16, "w2") for _ in range(NF)]
    qs = p.sb([128, 2048], I32, "qscratch")
    qf = p.sb([128, F], F32, "qfscratch")
    srow = p.sb([1, 512], F32, "srow")
    prep = p.ps([128, 512], F32, "prep2")
    for g in range(ND):
        for h in range(2):
            p.dma(qs[:, :2048], q1_h[g * 128:(g + 1) * 128, h * 2048:(h + 1) * 2048])
            p.vcopy(qf[:, h * 2048:(h + 1) * 2048], qs[:, :2048])
        for j in range(F // 512):
            p.dma(srow[:, :512], s1_h[g:g + 1, j * 512:(j + 1) * 512])
            p.mm(prep[:, :], c["ones_sb"][:, :], srow[:, :512])
            p.stt(w1[g][:, j * 512:(j + 1) * 512], qf[:, j * 512:(j + 1) * 512],
                  -128.0, prep[:, :], OP.add, OP.mult)
    for g in range(F // 128):
        p.dma(qs[:, :D], q2_h[g * 128:(g + 1) * 128, :])
        p.vcopy(qf[:, :D], qs[:, :D])
        for j in range(D // 512):
            p.dma(srow[:, :512], s2_h[g:g + 1, j * 512:(j + 1) * 512])
            p.mm(prep[:, :], c["ones_sb"][:, :], srow[:, :512])
            p.stt(w2[g][:, j * 512:(j + 1) * 512], qf[:, j * 512:(j + 1) * 512],
                  -128.0, prep[:, :], OP.add, OP.mult)

    b1t = p.sb([128, NF], F32, "b1t")
    for ft in range(NF):
        p.dma(b1t[:, ft:ft + 1], b1_h[ft * 128:(ft + 1) * 128].rearrange("(p one) -> p one", one=1))
    b2rep = p.sb([128, D], F32, "b2rep")
    grep = p.sb([128, D], BF16, "grep")
    brep = p.sb([128, D], BF16, "brep")
    for (rep, h, sc) in ((b2rep, b2_h, 0.5), (grep, g_h, None), (brep, b_h, None)):
        for j in range(2):
            p.dma(srow[:, :512], h[j * 512:(j + 1) * 512].rearrange("(one n) -> one n", one=1))
            p.mm(prep[:, :], c["ones_sb"][:, :], srow[:, :512])
            if sc is None:
                p.scopy(rep[:, j * 512:(j + 1) * 512], prep[:, :])
            else:
                p.act(rep[:, j * 512:(j + 1) * 512], prep[:, :], AF.Copy, scale=sc)

    xt = qf[:, 0:1024]
    tmp = qf[:, 1024:2048]
    xres = qf[:, 2048:3072]
    ztmp = qf[:, 3072:3584]
    mcol = p.sb([128, 1], F32, "m")
    vcol = p.sb([128, 1], F32, "v")
    lnb = p.sb([128, D], BF16, "lnb")
    pbf = p.ps([128, 128], BF16, "ptr")
    Wc = 256
    ntc = Wc // 128
    lnT = [p.sb([128, Wc], BF16, "lnT") for _ in range(ND)]
    h1T = [p.sb([128, Wc], BF16, "h1T") for _ in range(NF)]
    tbuf = [p.sb([128, D], F32, "tbuf") for _ in range(ntc)]
    ph = p.ps([128, Wc], F32, "ph")
    pz = p.ps([128, 512], F32, "pz")

    for ch in range(T // Wc):
        for tt_ in range(ntc):
            t = ch * ntc + tt_
            p.dma(xt, x_h[t * 128:(t + 1) * 128, :])
            layernorm_tile(p, tbuf[tt_], xt, tmp, mcol, vcol)
            layernorm_tile(p, lnb, tbuf[tt_], tmp, mcol, vcol, gamma_rep=grep, beta_rep=brep)
            transpose_nat_to_T(p, c, lnT, lnb, tt_, pbf)
        for ft in range(NF):
            for k in range(ND):
                p.mm(ph[:, :], w1[k][:, ft * 128:(ft + 1) * 128], lnT[k][:, :],
                     start=(k == 0), stop=(k == ND - 1), wait=(k == 0))
            p.act(h1T[ft][:, :], ph[:, :], AF.Silu, bias=b1t[:, ft:ft + 1])
        for tt_ in range(ntc):
            t = ch * ntc + tt_
            p.dma(xres, x_h[t * 128:(t + 1) * 128, :])
            for dc in range(2):
                dcs = slice(dc * 512, (dc + 1) * 512)
                for k in range(NF):
                    p.mm(pz[:, :], h1T[k][:, tt_ * 128:(tt_ + 1) * 128], w2[k][:, dcs],
                         start=(k == 0), stop=(k == NF - 1), wait=(k == 0))
                p.stt(ztmp, pz[:, :], 0.5, b2rep[:, dcs], OP.mult, OP.add)
                p.tt(ztmp, ztmp, xres[:, dcs], OP.add)
                p.tt(ztmp, ztmp, tbuf[tt_][:, dcs], OP.add)
                p.dma(y_h[t * 128:(t + 1) * 128, dcs], ztmp)
    return p.build()



def bfly_build_M(p, c, uw_h, M_bf, wrow, wrep, X, Xn, t1, t2):
    uw_flat = uw_h.rearrange("s k a b -> s (k a b)")
    cur, nxt = X, Xn
    for s in range(7):
        s_ = 1 << s
        for q in range(4):
            p.dma(wrow[:, :], uw_flat[s:s + 1, q * 2048:(q + 1) * 2048])
            wv = wrow.rearrange("one (k a b) -> one k a b", a=2, b=2)
            for wi, (ai, bi) in enumerate(((0, 0), (0, 1), (1, 0), (1, 1))):
                p.mm(p._prep[:, :], c["ones_sb"][:, :], wv[:, :, ai, bi])
                p.scopy(wrep[wi][:, q * 512:(q + 1) * 512], p._prep[:, :])

        def av(buf):
            return buf.rearrange("p (nb two s) -> p nb two s", two=2, s=s_)
        a_c, b_c = av(cur)[:, :, 0, :], av(cur)[:, :, 1, :]
        a_n, b_n = av(nxt)[:, :, 0, :], av(nxt)[:, :, 1, :]
        wv3 = [w.rearrange("p (nb s) -> p nb s", s=s_) for w in wrep]
        t1v = t1.rearrange("p (nb s) -> p nb s", s=s_)
        t2v = t2.rearrange("p (nb s) -> p nb s", s=s_)
        p.tt(t1v, a_c, wv3[0], OP.mult)
        p.tt(t2v, b_c, wv3[1], OP.mult)
        p.tt(a_n, t1v, t2v, OP.add)
        p.tt(t1v, a_c, wv3[2], OP.mult)
        p.tt(t2v, b_c, wv3[3], OP.mult)
        p.tt(b_n, t1v, t2v, OP.add)
        cur, nxt = nxt, cur
    p.vcopy(M_bf[:, :], cur[:, :])


def bfly_high_cols(p, uw_h, dst):
    out = []
    for s in range(7, 12):
        cols = []
        npair = 4096 >> (s + 1)
        tstep = 1 << (s - 7)
        pairs = [ta for ta in range(32) if (ta // tstep) % 2 == 0]
        for pi, ta in enumerate(pairs):
            i0 = 128 * ta
            k0 = ((i0 >> (s + 1)) << s) | (i0 & ((1 << s) - 1))
            w4 = p.sb([128, 4], F32, "bw")
            p.dma(w4[:, :], uw_h[s:s + 1, k0:k0 + 128, :, :].rearrange("s k a b -> (s k) (a b)"))
            cols.append((ta, ta + tstep, w4))
        out.append(cols)
    return out


def bfly_high_apply(p, bufs_a, bufs_b, cols, W, tmp):
    cur, nxt = bufs_a, bufs_b
    for si in range(5):
        for (ta, tb, w4) in cols[si]:
            p.ts(tmp[:, :W], cur[tb][:, :W], w4[:, 1:2], OP.mult)
            p.stt(nxt[ta][:, :W], cur[ta][:, :W], w4[:, 0:1], tmp[:, :W], OP.mult, OP.add)
            p.ts(tmp[:, :W], cur[ta][:, :W], w4[:, 2:3], OP.mult)
            p.stt(nxt[tb][:, :W], cur[tb][:, :W], w4[:, 3:4], tmp[:, :W], OP.mult, OP.add)
        cur, nxt = nxt, cur
    return cur, nxt


def build_l0_ffn():
    p = Prog()
    c = load_consts(p)
    F = 4096
    NF = F // 128
    x_h = p.dram("x", [T, D], F32)
    y_h = p.dram("y", [T, D], F32, kind="ExternalOutput")
    wexp_h = p.dram("W_exp", [D, F], F32)
    bexp_h = p.dram("b_exp", [F], F32)
    uw_h = p.dram("up_w", [12, 2048, 2, 2], F32)
    ub_h = p.dram("up_b", [F], F32)
    dw_h = p.dram("down_w", [12, 2048, 2, 2], F32)
    wcon_h = p.dram("W_con", [F, D], F32)
    bcon_h = p.dram("b_con", [D], F32)
    db_h = p.dram("down_b", [D], F32)
    basis_h = p.dram("c_basis", [128, F], F32)

    p._prep = p.ps([128, 512], F32, "prep")

    Mup = p.sb([128, F], BF16, "Mup")
    Mdn = p.sb([128, F], BF16, "Mdn")
    wrow = p.sb([1, 2048], F32, "wrow", grp="bld")
    wrep = [p.sb([128, 2048], BF16, "wrep", grp="bld") for _ in range(4)]
    Xb = p.sb([128, F], F32, "Xb", grp="bld")
    Xb2 = p.sb([128, F], F32, "Xb2", grp="bld")
    t1 = p.sb([128, 2048], BF16, "bt1", grp="bld")
    t2 = p.sb([128, 2048], BF16, "bt2", grp="bld")
    p.dma(Xb[:, :], basis_h[:, :])
    bfly_build_M(p, c, uw_h, Mup, wrow, wrep, Xb, Xb2, t1, t2)
    p.dma(Xb[:, :], basis_h[:, :])
    bfly_build_M(p, c, dw_h, Mdn, wrow, wrep, Xb, Xb2, t1, t2)
    p.free_grp("bld")

    fscr = p.sb([128, 3584], F32, "fscr")
    wexp = [p.sb([128, F], BF16, "wexp") for _ in range(ND)]
    for k in range(ND):
        for hh in range(2):
            p.dma(fscr[:, :2048], wexp_h[k * 128:(k + 1) * 128, hh * 2048:(hh + 1) * 2048])
            p.vcopy(wexp[k][:, hh * 2048:(hh + 1) * 2048], fscr[:, :2048])
    wcon = [p.sb([128, D], BF16, "wcon") for _ in range(NF)]
    for k in range(NF):
        p.dma(fscr[:, :D], wcon_h[k * 128:(k + 1) * 128, :])
        p.vcopy(wcon[k][:, :], fscr[:, :D])

    upcols = bfly_high_cols(p, uw_h, None)
    dncols = bfly_high_cols(p, dw_h, None)

    bexp_c = p.sb([128, NF], F32, "bexpc")
    ub_c = p.sb([128, NF], F32, "ubc")
    for ft in range(NF):
        p.dma(bexp_c[:, ft:ft + 1], bexp_h[ft * 128:(ft + 1) * 128].rearrange("(p one) -> p one", one=1))
        p.dma(ub_c[:, ft:ft + 1], ub_h[ft * 128:(ft + 1) * 128].rearrange("(p one) -> p one", one=1))
    srow = p.sb([1, 512], F32, "srow")
    bc_rep = p.sb([128, D], BF16, "bcrep")
    for j in range(2):
        p.dma(srow[:, :], bcon_h[j * 512:(j + 1) * 512].rearrange("(one n) -> one n", one=1))
        p.mm(p._prep[:, :], c["ones_sb"][:, :], srow[:, :], start=True, stop=False)
        p.dma(srow[:, :], db_h[j * 512:(j + 1) * 512].rearrange("(one n) -> one n", one=1))
        p.mm(p._prep[:, :], c["ones_sb"][:, :], srow[:, :], start=False, stop=True)
        p.scopy(bc_rep[:, j * 512:(j + 1) * 512], p._prep[:, :])

    W = 256
    lnT = [p.sb([128, W], BF16, "lnT") for _ in range(ND)]
    bufA = [p.sb([128, W], BF16, "bufA") for _ in range(NF)]
    bufB = [p.sb([128, W], BF16, "bufB") for _ in range(NF)]
    htmp = p.sb([128, W], BF16, "htmp")
    xt = fscr[:, 0:1024]
    tmp = fscr[:, 1024:2048]
    xres = fscr[:, 2048:3072]
    ztmp = fscr[:, 3072:3584]
    mcol = p.sb([128, 1], F32, "m")
    vcol = p.sb([128, 1], F32, "v")
    lnb = p.sb([128, D], BF16, "lnb")
    pbf = p.ps([128, 128], BF16, "ptr")
    ph = p.ps([128, W], F32, "ph")
    pz = p.ps([128, 512], F32, "pz")

    for ch in range(T // W):
        for tt_ in range(W // 128):
            t = ch * (W // 128) + tt_
            p.dma(xt, x_h[t * 128:(t + 1) * 128, :])
            layernorm_tile(p, lnb, xt, tmp, mcol, vcol)
            transpose_nat_to_T(p, c, lnT, lnb, tt_, pbf)
        for ft in range(NF):
            for k in range(ND):
                p.mm(ph[:, :], wexp[k][:, ft * 128:(ft + 1) * 128], lnT[k][:, :],
                     start=(k == 0), stop=(k == ND - 1), wait=(k == 0))
            p.ts(bufB[ft][:, :], ph[:, :], bexp_c[:, ft:ft + 1], OP.add)
        for ft in range(NF):
            p.mm(ph[:, :], Mup[:, ft * 128:(ft + 1) * 128], bufB[ft][:, :])
            p.scopy(bufA[ft][:, :], ph[:, :])
        cur, nxt = bfly_high_apply(p, bufA, bufB, upcols, W, htmp)
        for ft in range(NF):
            p.act(nxt[ft][:, :], cur[ft][:, :], AF.Gelu, bias=ub_c[:, ft:ft + 1])
        cur, nxt = nxt, cur
        for ft in range(NF):
            p.mm(ph[:, :], Mdn[:, ft * 128:(ft + 1) * 128], cur[ft][:, :])
            p.scopy(nxt[ft][:, :], ph[:, :])
        cur, nxt = bfly_high_apply(p, nxt, cur, dncols, W, htmp)
        for tt_ in range(W // 128):
            t = ch * (W // 128) + tt_
            p.dma(xres, x_h[t * 128:(t + 1) * 128, :])
            for dc in range(2):
                dcs = slice(dc * 512, (dc + 1) * 512)
                for k in range(NF):
                    p.mm(pz[:, :], cur[k][:, tt_ * 128:(tt_ + 1) * 128], wcon[k][:, dcs],
                         start=(k == 0), stop=(k == NF - 1), wait=(k == 0))
                p.tt(ztmp, pz[:, :], bc_rep[:, dcs], OP.add)
                p.tt(ztmp, ztmp, xres[:, dcs], OP.add)
                p.dma(y_h[t * 128:(t + 1) * 128, dcs], ztmp)
    return p.build()


def run_l0_ffn(x, inp):
    nc = get_prog("l0_ffn", build_l0_ffn)
    cmn = const_inputs()
    basis = np.zeros((128, 4096), np.float32)
    for j in range(128):
        basis[j, j::128] = 1.0
    maps = []
    for i in range(NCORE):
        m = dict(cmn)
        m["c_basis"] = basis
        m["x"] = np.ascontiguousarray(x[i])
        m["W_exp"] = inp["L0_ffn_W_exp"]
        m["b_exp"] = inp["L0_ffn_b_exp"]
        m["up_w"] = inp["L0_ffn_up_weights"]
        m["up_b"] = inp["L0_ffn_up_bias"]
        m["down_w"] = inp["L0_ffn_down_weights"]
        m["W_con"] = inp["L0_ffn_W_con"]
        m["b_con"] = inp["L0_ffn_b_con"]
        m["down_b"] = inp["L0_ffn_down_bias"]
        maps.append(m)
    res = run_bass_kernel_spmd(nc, maps, core_ids=list(range(NCORE)))
    return np.stack([r["y"] for r in res.results], axis=0)



def load_w_bf16(p, dst_tiles, w_h, rows, cols, fscr):
    for k in range(rows // 128):
        for hh in range(0, cols, 2048):
            wdt = min(2048, cols - hh)
            p.dma(fscr[:, :wdt], w_h[k * 128:(k + 1) * 128, hh:hh + wdt])
            p.vcopy(dst_tiles[k][:, hh:hh + wdt], fscr[:, :wdt])


def load_w_bf16_T(p, c, dst_tiles, w_h, n, fscr, wbscr, pbf):
    nt = n // 128
    for k in range(nt):
        p.dma(fscr[:, :n], w_h[k * 128:(k + 1) * 128, :])
        p.vcopy(wbscr[:, :n], fscr[:, :n])
        for j in range(nt):
            p.trans(pbf[:, :128], wbscr[:, j * 128:(j + 1) * 128], c["id_sb"][:, :])
            p.vcopy(dst_tiles[j][:, k * 128:(k + 1) * 128], pbf[:, :128])


def build_attn(kind):
    p = Prog()
    c = load_consts(p)
    x_h = p.dram("x", [T, D], F32)
    y_h = p.dram("y", [T, D], F32, kind="ExternalOutput")
    H = 16
    HD = 64
    VA = HD + 1
    qT_hbm = p.dram("sc_qT", [D, T], BF16, kind="Internal")
    kT_hbm = p.dram("sc_kT", [D, T], BF16, kind="Internal")
    v_hbm = p.dram("sc_v", [T, H * VA], BF16, kind="Internal")
    if kind == "mla":
        wdkv_h = p.dram("W_DKV", [D, 256], F32)
        wuk_h = p.dram("W_UK", [256, D], F32)
        wuv_h = p.dram("W_UV", [256, D], F32)
        wuq_h = p.dram("W_UQ", [D, D], F32)
        wo_h = p.dram("W_O", [D, D], F32)
        mask_h = p.dram("c_diag", [128, 128], BF16)
    else:
        g_h = p.dram("ln_g", [D], F32)
        b_h = p.dram("ln_b", [D], F32)
        wq_h = p.dram("Wq", [D, D], F32)
        wk_h = p.dram("Wk", [D, D], F32)
        wv_h = p.dram("Wv", [D, D], F32)
        wo_h = p.dram("Wo", [D, D], F32)
        bq_h = p.dram("bq", [D], F32)
        bk_h = p.dram("bk", [D], F32)
        bv_h = p.dram("bv", [D], F32)
        bo_h = p.dram("bo", [D], F32)
        mu_h = p.dram("mu", [1, 1], F32)

    p._prep = p.ps([128, 512], F32, "prep")
    onat = [p.sb([128, D], BF16, "onat") for _ in range(NT)]
    wo = [p.sb([128, D], BF16, "wo") for _ in range(ND)]

    fscr = p.sb([128, 2048], F32, "fscr", grp="p1")
    if kind == "mla":
        wbscr = None
        wuq = [p.sb([128, D], BF16, "wuq", grp="p1") for _ in range(ND)]
        wdkv = [p.sb([128, 256], BF16, "wdkv", grp="p1") for _ in range(ND)]
        wuk = [p.sb([128, D], BF16, "wuk", grp="p1") for _ in range(2)]
        wuv = [p.sb([128, D], BF16, "wuv", grp="p1") for _ in range(2)]
        load_w_bf16(p, wuq, wuq_h, D, D, fscr)
        load_w_bf16(p, wdkv, wdkv_h, D, 256, fscr)
        load_w_bf16(p, wuk, wuk_h, 256, D, fscr)
        load_w_bf16(p, wuv, wuv_h, 256, D, fscr)
        load_w_bf16(p, wo, wo_h, D, D, fscr)
        ckvT = [p.sb([128, T], BF16, "ckvT", grp="p1") for _ in range(2)]
        grep = brep = None
        qb = kb = vb = None
    else:
        pbf0 = p.ps([128, 128], BF16, "ptr0", grp="p1")
        wbscr = p.sb([128, D], BF16, "wbscr", grp="p1")
        wqT = [p.sb([128, D], BF16, "wqT", grp="p1") for _ in range(ND)]
        wkT = [p.sb([128, D], BF16, "wkT", grp="p1") for _ in range(ND)]
        wvT = [p.sb([128, D], BF16, "wvT", grp="p1") for _ in range(ND)]
        load_w_bf16_T(p, c, wqT, wq_h, D, fscr, wbscr, pbf0)
        load_w_bf16_T(p, c, wkT, wk_h, D, fscr, wbscr, pbf0)
        load_w_bf16_T(p, c, wvT, wv_h, D, fscr, wbscr, pbf0)
        load_w_bf16_T(p, c, wo, wo_h, D, fscr, wbscr, pbf0)
        qb = p.sb([128, ND], F32, "qb", grp="p1")
        kb = p.sb([128, ND], F32, "kb", grp="p1")
        for ftl in range(ND):
            p.dma(qb[:, ftl:ftl + 1], bq_h[ftl * 128:(ftl + 1) * 128].rearrange("(p one) -> p one", one=1))
            p.dma(kb[:, ftl:ftl + 1], bk_h[ftl * 128:(ftl + 1) * 128].rearrange("(p one) -> p one", one=1))
        grep = p.sb([128, D], BF16, "grep", grp="p1")
        brep = p.sb([128, D], BF16, "brep", grp="p1")
        bvrep = p.sb([128, D], F32, "bvrep", grp="p1")
        srow = p.sb([1, 512], F32, "srow", grp="p1")
        for (rep, hh) in ((grep, g_h), (brep, b_h), (bvrep, bv_h)):
            for j in range(2):
                p.dma(srow[:, :], hh[j * 512:(j + 1) * 512].rearrange("(one n) -> one n", one=1))
                p.mm(p._prep[:, :], c["ones_sb"][:, :], srow[:, :])
                p.scopy(rep[:, j * 512:(j + 1) * 512], p._prep[:, :])

    lnT = [p.sb([128, 512], BF16, "lnTc", grp="p1") for _ in range(ND)]
    xt = p.sb([128, D], F32, "xt", grp="p1")
    tmp = p.sb([128, D], F32, "lntmp", grp="p1")
    t32 = p.sb([128, D], F32, "t32", grp="p1")
    mcol = p.sb([128, 1], F32, "m", grp="p1")
    vcol = p.sb([128, 1], F32, "v", grp="p1")
    lnb = p.sb([128, D], BF16, "lnb", grp="p1")
    pbf = p.ps([128, 128], BF16, "ptr", grp="p1")
    ph = p.ps([128, 512], F32, "ph", grp="p1")
    ev = p.sb([128, 512], BF16, "ev", grp="p1")
    vbuf = p.sb([128, H * VA], BF16, "vbuf", grp="p1")

    for ch in range(8):
        tok = slice(ch * 512, (ch + 1) * 512)
        for tt_ in range(4):
            t = ch * 4 + tt_
            p.dma(xt[:, :], x_h[t * 128:(t + 1) * 128, :])
            if kind == "mla":
                layernorm_tile(p, lnb, xt, tmp, mcol, vcol)
            else:
                layernorm_tile(p, t32, xt, tmp, mcol, vcol)
                layernorm_tile(p, lnb, t32, tmp, mcol, vcol,
                               gamma_rep=grep, beta_rep=brep, eps=1e-6)
            transpose_nat_to_T(p, c, lnT, lnb, tt_, pbf)
        if kind == "mla":
            for ntl in range(ND):
                for k in range(ND):
                    p.mm(ph[:, :], wuq[k][:, ntl * 128:(ntl + 1) * 128], lnT[k][:, :],
                         start=(k == 0), stop=(k == ND - 1), wait=(k == 0))
                p.scopy(ev[:, :], ph[:, :])
                p.dma(qT_hbm[ntl * 128:(ntl + 1) * 128, tok], ev[:, :])
            for ntl in range(2):
                for k in range(ND):
                    p.mm(ph[:, :], wdkv[k][:, ntl * 128:(ntl + 1) * 128], lnT[k][:, :],
                         start=(k == 0), stop=(k == ND - 1), wait=(k == 0))
                p.scopy(ckvT[ntl][:, tok], ph[:, :])
        else:
            for (wt, bias, dst_h) in ((wqT, qb, qT_hbm), (wkT, kb, kT_hbm)):
                for ntl in range(ND):
                    for k in range(ND):
                        p.mm(ph[:, :], wt[k][:, ntl * 128:(ntl + 1) * 128], lnT[k][:, :],
                             start=(k == 0), stop=(k == ND - 1), wait=(k == 0))
                    p.ts(ev[:, :], ph[:, :], bias[:, ntl:ntl + 1], OP.add)
                    p.dma(dst_h[ntl * 128:(ntl + 1) * 128, tok], ev[:, :])
            for tt_ in range(4):
                t = ch * 4 + tt_
                for nc2 in range(2):
                    for k in range(ND):
                        p.mm(ph[:, :], lnT[k][:, tt_ * 128:(tt_ + 1) * 128],
                             wvT[k][:, nc2 * 512:(nc2 + 1) * 512],
                             start=(k == 0), stop=(k == ND - 1), wait=(k == 0))
                    for h8 in range(8):
                        hd_ = nc2 * 8 + h8
                        p.tt(vbuf[:, hd_ * VA:hd_ * VA + HD], ph[:, h8 * 64:(h8 + 1) * 64],
                             bvrep[:, hd_ * 64:(hd_ + 1) * 64], OP.add)
                p.memset(vbuf.rearrange("p (h va) -> p h va", va=VA)[:, :, HD:HD + 1], 1.0)
                p.dma(v_hbm[t * 128:(t + 1) * 128, :], vbuf[:, :])
    if kind == "mla":
        for ntl in range(ND):
            for tc in range(8):
                for k in range(2):
                    p.mm(ph[:, :], wuk[k][:, ntl * 128:(ntl + 1) * 128],
                         ckvT[k][:, tc * 512:(tc + 1) * 512],
                         start=(k == 0), stop=(k == 1), wait=(k == 0))
                p.scopy(ev[:, :], ph[:, :])
                p.dma(kT_hbm[ntl * 128:(ntl + 1) * 128, tc * 512:(tc + 1) * 512], ev[:, :])
        for t in range(NT):
            for nc2 in range(2):
                for k in range(2):
                    p.mm(ph[:, :], ckvT[k][:, t * 128:(t + 1) * 128],
                         wuv[k][:, nc2 * 512:(nc2 + 1) * 512],
                         start=(k == 0), stop=(k == 1), wait=(k == 0))
                for h8 in range(8):
                    hd_ = nc2 * 8 + h8
                    p.vcopy(vbuf[:, hd_ * VA:hd_ * VA + HD], ph[:, h8 * 64:(h8 + 1) * 64])
            p.memset(vbuf.rearrange("p (h va) -> p h va", va=VA)[:, :, HD:HD + 1], 1.0)
            p.dma(v_hbm[t * 128:(t + 1) * 128, :], vbuf[:, :])
    p.free_grp("p1")

    kTh = p.sb([64, T], BF16, "kTh", grp="p2")
    qTh = p.sb([64, T], BF16, "qTh", grp="p2")
    vth = [p.sb([128, VA], BF16, "vth", grp="p2") for _ in range(NT)]
    PTs = p.sb([128, 512], BF16, "PTs", grp="p2")
    pS = p.ps([128, 512], F32, "pS", grp="p2")
    lrec = p.sb([128, 1], F32, "lrec", grp="p2")
    if kind == "mla":
        mask_sb = p.sb([128, 128], BF16, "msk", grp="p2")
        p.dma(mask_sb[:, :], mask_h[:, :])
        po = [p.ps([128, VA], F32, "po", grp="p2") for _ in range(4)]
        scale = 0.125
    else:
        macol = p.sb([128, 1], F32, "macol", grp="p2")
        murow = p.sb([1, 1], F32, "murow", grp="p2")
        p.dma(murow[:, :], mu_h[:, :])
        p.mm(p._prep[:, 0:1], c["ones_sb"][:, :], murow[:, :])
        p.act(macol[:, :], p._prep[:, 0:1], AF.Copy, scale=-1.0)
        sig = p.sb([128, 512], BF16, "sig", grp="p2")
        pot = p.ps([65, 512], F32, "pot", grp="p2")
        otu = p.sb([65, 512], BF16, "otu", grp="p2")
        pon = p.ps([128, VA], BF16, "pon", grp="p2")
        scale = 0.125

    for h in range(H):
        hrow = (h % 2) * 64
        ktile = h // 2
        p.dma(kTh[:, :], kT_hbm[ktile * 128 + hrow:ktile * 128 + hrow + 64, :])
        p.dma(qTh[:, :], qT_hbm[ktile * 128 + hrow:ktile * 128 + hrow + 64, :])
        for t in range(NT):
            p.dma(vth[t][:, :], v_hbm[t * 128:(t + 1) * 128, h * VA:(h + 1) * VA])
        for ic in range(8):
            njt = 4 * ic + 4 if kind == "mla" else NT
            if kind == "mla":
                for jt in range(njt):
                    p.mm(pS[:, :], kTh[:, jt * 128:(jt + 1) * 128], qTh[:, ic * 512:(ic + 1) * 512], sgc=True)
                    p.act(PTs[:, :], pS[:, :], AF.Exp, scale=scale)
                    if jt >= 4 * ic:
                        it4 = jt - 4 * ic
                        p.tt(PTs[:, it4 * 128:(it4 + 1) * 128], PTs[:, it4 * 128:(it4 + 1) * 128],
                             mask_sb[:, :], OP.mult)
                    first_pv = True
                    for it4 in range(4):
                        itile = 4 * ic + it4
                        if jt <= itile:
                            p.mm(po[it4][:, :], PTs[:, it4 * 128:(it4 + 1) * 128], vth[jt][:, :],
                                 start=(jt == 0), stop=(jt == itile), wait=first_pv, sgc=True)
                            first_pv = False
                for it4 in range(4):
                    itile = 4 * ic + it4
                    p.recip(lrec[:, :], po[it4][:, HD:HD + 1])
                    p.ts(onat[itile][:, h * HD:(h + 1) * HD], po[it4][:, 0:HD], lrec[:, 0:1], OP.mult)
            else:
                for jt in range(njt):
                    p.mm(pS[:, :], kTh[:, jt * 128:(jt + 1) * 128], qTh[:, ic * 512:(ic + 1) * 512], sgc=True)
                    p.act(PTs[:, :], pS[:, :], AF.Exp, scale=scale)
                    p.act(sig[:, :], pS[:, :], AF.Sigmoid, bias=macol[:, 0:1], scale=scale)
                    p.tt(PTs[:, :], PTs[:, :], sig[:, :], OP.mult)
                    p.mm(pot[:, :], vth[jt][:, :], PTs[:, :],
                         start=(jt == 0), stop=(jt == njt - 1), wait=True, sgc=True)
                p.vcopy(otu[:, :], pot[:, :])
                for it4 in range(4):
                    itile = 4 * ic + it4
                    p.trans(pon[:, :], otu[:, it4 * 128:(it4 + 1) * 128], c["id_sb"][0:VA, 0:VA])
                    p.recip(lrec[:, :], pon[0:128, HD:HD + 1])
                    p.ts(onat[itile][:, h * HD:(h + 1) * HD], pon[0:128, 0:HD], lrec[:, 0:1], OP.mult)
    p.free_grp("p2")

    xres = p.sb([128, D], F32, "xres", grp="p3")
    ztmp = p.sb([128, 512], F32, "zt3", grp="p3")
    otp = p.sb([128, 128], BF16, "otp", grp="p3")
    pbf3 = p.ps([128, 128], BF16, "ptr3", grp="p3")
    py = [p.ps([128, 512], F32, "py", grp="p3") for _ in range(2)]
    borep = None
    if kind != "mla":
        borep = p.sb([128, D], F32, "borep", grp="p3")
        srow3 = p.sb([1, 512], F32, "srow3", grp="p3")
        for j in range(2):
            p.dma(srow3[:, :], bo_h[j * 512:(j + 1) * 512].rearrange("(one n) -> one n", one=1))
            p.mm(p._prep[:, :], c["ones_sb"][:, :], srow3[:, :])
            p.scopy(borep[:, j * 512:(j + 1) * 512], p._prep[:, :])
    for t in range(NT):
        p.dma(xres[:, :], x_h[t * 128:(t + 1) * 128, :])
        for hv in range(ND):
            p.trans(pbf3[:, :], onat[t][:, hv * 128:(hv + 1) * 128], c["id_sb"][:, :])
            p.vcopy(otp[:, :], pbf3[:, :])
            for dc in range(2):
                p.mm(py[dc][:, :], otp[:, :], wo[hv][:, dc * 512:(dc + 1) * 512],
                     start=(hv == 0), stop=(hv == ND - 1), wait=(dc == 0), sgc=True)
        for dc in range(2):
            dcs = slice(dc * 512, (dc + 1) * 512)
            if borep is not None:
                p.tt(ztmp[:, :], py[dc][:, :], borep[:, dcs], OP.add)
                p.tt(ztmp[:, :], ztmp[:, :], xres[:, dcs], OP.add)
            else:
                p.tt(ztmp[:, :], py[dc][:, :], xres[:, dcs], OP.add)
            p.dma(y_h[t * 128:(t + 1) * 128, dcs], ztmp[:, :])
    return p.build()


def run_attn(kind, x, inp):
    nc = get_prog("attn_" + kind, lambda: build_attn(kind))
    cmn = const_inputs()
    maps = []
    for i in range(NCORE):
        m = dict(cmn)
        m["x"] = np.ascontiguousarray(x[i])
        if kind == "mla":
            jj = np.arange(128)
            m["c_diag"] = (jj[:, None] <= jj[None, :]).astype(bf16_np())
            m["W_DKV"] = inp["L1_attn_a_W_DKV"]
            m["W_UK"] = inp["L1_attn_a_W_UK"]
            m["W_UV"] = inp["L1_attn_a_W_UV"]
            m["W_UQ"] = inp["L1_attn_a_W_UQ"]
            m["W_O"] = inp["L1_attn_a_W_O"]
        else:
            m["ln_g"] = inp["L1_attn_b_ln_gamma"]
            m["ln_b"] = inp["L1_attn_b_ln_beta"]
            m["Wq"] = inp["L1_attn_b_Wq"]
            m["Wk"] = inp["L1_attn_b_Wk"]
            m["Wv"] = inp["L1_attn_b_Wv"]
            m["Wo"] = inp["L1_attn_b_Wo"]
            m["bq"] = inp["L1_attn_b_bq"]
            m["bk"] = inp["L1_attn_b_bk"]
            m["bv"] = inp["L1_attn_b_bv"]
            m["bo"] = inp["L1_attn_b_bo"]
            m["mu"] = np.asarray(inp["L1_attn_b_mu"], np.float32).reshape(1, 1)
        maps.append(m)
    res = run_bass_kernel_spmd(nc, maps, core_ids=list(range(NCORE)))
    return np.stack([r["y"] for r in res.results], axis=0)



def build_l0_attn_b():
    p = Prog()
    c = load_consts(p)
    x_h = p.dram("x", [T, D], F32)
    y_h = p.dram("y", [T, D], F32, kind="ExternalOutput")
    tmk_h = p.dram("tmk", [1, 1, D], F32)
    tmv_h = p.dram("tmv", [1, 1, D], F32)
    tmr_h = p.dram("tmr", [1, 1, D], F32)
    tmg_h = p.dram("tmg", [1, 1, D], F32)
    td_h = p.dram("td", [16], F32)
    tf_h = p.dram("tf", [16], F32)
    wr_h = p.dram("W_r", [D, D], F32)
    wk_h = p.dram("W_k", [D, D], F32)
    wv_h = p.dram("W_v", [D, D], F32)
    wg_h = p.dram("W_g", [D, D], F32)
    wo_h = p.dram("W_o", [D, D], F32)
    lnw_h = p.dram("ln_w", [D], F32)
    lnb_h = p.dram("ln_b", [D], F32)
    e1_h = p.dram("c_e1", [128, 128], F32)
    su_h = p.dram("c_su", [128, 128], F32)
    eye_h = p.dram("c_eye", [128, 128], F32)
    iota_h = p.dram("c_iota", [128, 2], F32)

    rT_hbm = p.dram("sc_rT", [D, T], BF16, kind="Internal")
    kT_hbm = p.dram("sc_kT", [D, T], BF16, kind="Internal")
    kn_hbm = p.dram("sc_kn", [T, D], BF16, kind="Internal")
    vn_hbm = p.dram("sc_vn", [T, D], BF16, kind="Internal")
    gn_hbm = p.dram("sc_gn", [T, D], BF16, kind="Internal")

    p._prep = p.ps([128, 512], F32, "prep")
    H = 16
    K = 64

    ynat = [p.sb([128, D], BF16, "ynat") for _ in range(NT)]
    woT = [p.sb([128, D], BF16, "woT") for _ in range(ND)]
    lnwrep = p.sb([128, D], F32, "lnwrep")
    lnbrep = p.sb([128, D], F32, "lnbrep")
    srow0 = p.sb([1, 512], F32, "srow0")
    for (rep, hh) in ((lnwrep, lnw_h), (lnbrep, lnb_h)):
        for j in range(2):
            p.dma(srow0[:, :], hh[j * 512:(j + 1) * 512].rearrange("(one n) -> one n", one=1))
            p.mm(p._prep[:, :], c["ones_sb"][:, :], srow0[:, :])
            p.scopy(rep[:, j * 512:(j + 1) * 512], p._prep[:, :])

    fscr = p.sb([128, D], F32, "fscr", grp="p1")
    wbscr = p.sb([128, D], BF16, "wbscr", grp="p1")
    pbf0 = p.ps([128, 128], BF16, "ptr0", grp="p1")
    wrT = [p.sb([128, D], BF16, "wrT", grp="p1") for _ in range(ND)]
    wkT = [p.sb([128, D], BF16, "wkT", grp="p1") for _ in range(ND)]
    wvT = [p.sb([128, D], BF16, "wvT", grp="p1") for _ in range(ND)]
    wgT = [p.sb([128, D], BF16, "wgT", grp="p1") for _ in range(ND)]
    load_w_bf16_T(p, c, wrT, wr_h, D, fscr, wbscr, pbf0)
    load_w_bf16_T(p, c, wkT, wk_h, D, fscr, wbscr, pbf0)
    load_w_bf16_T(p, c, wvT, wv_h, D, fscr, wbscr, pbf0)
    load_w_bf16_T(p, c, wgT, wg_h, D, fscr, wbscr, pbf0)
    load_w_bf16_T(p, c, woT, wo_h, D, fscr, wbscr, pbf0)
    tmcols = {}
    for nm, hh in (("r", tmr_h), ("k", tmk_h), ("v", tmv_h), ("g", tmg_h)):
        col = p.sb([128, ND], F32, "tmc", grp="p1")
        for ftl in range(ND):
            p.dma(col[:, ftl:ftl + 1],
                  hh.rearrange("a b d -> (a b) d")[0:1, ftl * 128:(ftl + 1) * 128]
                  .rearrange("one (pp o) -> (one pp) o", o=1))
        tmcols[nm] = col

    lnT = [p.sb([128, 512], BF16, "lnTc", grp="p1") for _ in range(ND)]
    xmT = [p.sb([128, 512], BF16, "xmT", grp="p1") for _ in range(ND)]
    dcol = p.sb([128, 512], F32, "dcol", grp="p1")
    pcol = p.sb([128, ND], BF16, "pcol", grp="p1")
    xt = p.sb([128, D], F32, "xt", grp="p1")
    tmp = p.sb([128, D], F32, "lntmp", grp="p1")
    mcol = p.sb([128, 1], F32, "m", grp="p1")
    vcol = p.sb([128, 1], F32, "v", grp="p1")
    lnb2 = p.sb([128, D], BF16, "lnb2", grp="p1")
    pbf = p.ps([128, 128], BF16, "ptr", grp="p1")
    ph = p.ps([128, 512], F32, "ph", grp="p1")
    ev = p.sb([128, 512], BF16, "ev", grp="p1")
    p.memset(pcol[:, :], 0.0)

    for ch in range(8):
        tok = slice(ch * 512, (ch + 1) * 512)
        for tt_ in range(4):
            t = ch * 4 + tt_
            p.dma(xt[:, :], x_h[t * 128:(t + 1) * 128, :])
            layernorm_tile(p, lnb2, xt, tmp, mcol, vcol)
            transpose_nat_to_T(p, c, lnT, lnb2, tt_, pbf)
        for nm, dsts in (("r", ("rT",)), ("k", ("kT", "kn")), ("v", ("vn",)), ("g", ("gn",))):
            tmc = tmcols[nm]
            for k in range(ND):
                p.tt(dcol[:, 1:512], lnT[k][:, 1:512], lnT[k][:, 0:511], OP.subtract)
                p.stt(xmT[k][:, 1:512], dcol[:, 1:512], tmc[:, k:k + 1], lnT[k][:, 0:511],
                      OP.mult, OP.add)
                p.tt(dcol[:, 0:1], lnT[k][:, 0:1], pcol[:, k:k + 1], OP.subtract)
                p.stt(xmT[k][:, 0:1], dcol[:, 0:1], tmc[:, k:k + 1], pcol[:, k:k + 1],
                      OP.mult, OP.add)
            if nm == "g":
                for k in range(ND):
                    p.vcopy(pcol[:, k:k + 1], lnT[k][:, 511:512])
            wT = {"r": wrT, "k": wkT, "v": wvT, "g": wgT}[nm]
            for d in dsts:
                if d in ("rT", "kT"):
                    hbm = rT_hbm if d == "rT" else kT_hbm
                    for ntl in range(ND):
                        for k in range(ND):
                            p.mm(ph[:, :], wT[k][:, ntl * 128:(ntl + 1) * 128], xmT[k][:, :],
                                 start=(k == 0), stop=(k == ND - 1), wait=(k == 0))
                        p.scopy(ev[:, :], ph[:, :])
                        p.dma(hbm[ntl * 128:(ntl + 1) * 128, tok], ev[:, :])
                else:
                    hbm = {"kn": kn_hbm, "vn": vn_hbm, "gn": gn_hbm}[d]
                    for tt_ in range(4):
                        t = ch * 4 + tt_
                        for nc2 in range(2):
                            for k in range(ND):
                                p.mm(ph[:, :], xmT[k][:, tt_ * 128:(tt_ + 1) * 128],
                                     wT[k][:, nc2 * 512:(nc2 + 1) * 512],
                                     start=(k == 0), stop=(k == ND - 1), wait=(k == 0))
                            if d == "gn":
                                p.act(ev[:, :], ph[:, :], AF.Silu)
                            else:
                                p.scopy(ev[:, :], ph[:, :])
                            p.dma(hbm[t * 128:(t + 1) * 128, nc2 * 512:(nc2 + 1) * 512], ev[:, :])
    p.free_grp("p1")

    e1c = p.sb([128, 128], F32, "e1c", grp="p2")
    suc = p.sb([128, 128], F32, "suc", grp="p2")
    eyec = p.sb([128, 128], F32, "eyec", grp="p2")
    iotac = p.sb([128, 2], F32, "iotac", grp="p2")
    p.dma(e1c[:, :], e1_h[:, :])
    p.dma(suc[:, :], su_h[:, :])
    p.dma(eyec[:, :], eye_h[:, :])
    p.dma(iotac[:, :], iota_h[:, :])
    tdrow = p.sb([1, 16], F32, "tdrow", grp="p2")
    tfrow = p.sb([1, 16], F32, "tfrow", grp="p2")
    p.dma(tdrow[:, :], td_h.rearrange("(one h) -> one h", one=1))
    p.dma(tfrow[:, :], tf_h.rearrange("(one h) -> one h", one=1))
    ldrep = p.sb([128, 16], F32, "ldrep", grp="p2")
    urep = p.sb([128, 16], F32, "urep", grp="p2")
    p.mm(p._prep[:, 0:16], c["ones_sb"][:, :], tdrow[:, :])
    p.act(ldrep[:, :], p._prep[:, 0:16], AF.Exp)
    p.ts(ldrep[:, :], ldrep[:, :], -1.0, OP.mult)
    p.mm(p._prep[:, 0:16], c["ones_sb"][:, :], tfrow[:, :])
    p.scopy(urep[:, :], p._prep[:, 0:16])

    wmT = p.sb([128, 128], BF16, "wmT", grp="p2")
    wmt1 = p.sb([128, 128], F32, "wmt1", grp="p2")
    wbcol = p.sb([128, 1], F32, "wbcol", grp="p2")
    wkcol = p.sb([128, 1], F32, "wkcol", grp="p2")
    wscol = p.sb([64, 1], F32, "wscol", grp="p2")
    s_st = p.sb([64, 128], F32, "s_st", grp="p2")
    s_bf = p.sb([64, K], BF16, "s_bf", grp="p2")
    kwk = p.sb([128, K], BF16, "kwk", grp="p2")
    PTc = p.sb([128, 128], BF16, "PTc", grp="p2")
    y1 = p.sb([128, K], F32, "y1", grp="p2")
    y2 = p.sb([128, K], F32, "y2", grp="p2")
    gm = p.sb([128, 1], F32, "gm", grp="p2")
    gv = p.sb([128, 1], F32, "gv", grp="p2")
    rTh = p.sb([64, T], BF16, "rTh", grp="p2")
    kTh = p.sb([64, T], BF16, "kTh", grp="p2")
    knh = [p.sb([128, K], BF16, "knh", grp="p2") for _ in range(NT)]
    vnh = [p.sb([128, K], BF16, "vnh", grp="p2") for _ in range(NT)]
    pA = p.ps([128, 128], F32, "pA", grp="p2")
    pO = p.ps([128, K], F32, "pO", grp="p2")
    pC = p.ps([128, K], F32, "pC", grp="p2")
    pS3 = p.ps([64, K], F32, "pS3", grp="p2")

    for h in range(H):
        hrow = (h % 2) * 64
        ktile = h // 2
        p.dma(rTh[:, :], rT_hbm[ktile * 128 + hrow:ktile * 128 + hrow + 64, :])
        p.dma(kTh[:, :], kT_hbm[ktile * 128 + hrow:ktile * 128 + hrow + 64, :])
        for t in range(NT):
            p.dma(knh[t][:, :], kn_hbm[t * 128:(t + 1) * 128, h * K:(h + 1) * K])
            p.dma(vnh[t][:, :], vn_hbm[t * 128:(t + 1) * 128, h * K:(h + 1) * K])
        ldc = ldrep[:, h:h + 1]
        p.ts(wmt1[:, :], e1c[:, :], ldc, OP.mult)
        p.act(wmt1[:, :], wmt1[:, :], AF.Exp)
        p.tt(wmt1[:, :], wmt1[:, :], suc[:, :], OP.mult)
        p.stt(wmT[:, :], eyec[:, :], urep[:, h:h + 1], wmt1[:, :], OP.mult, OP.add)
        p.ts(wbcol[:, :], iotac[:, 0:1], ldc, OP.mult)
        p.act(wbcol[:, :], wbcol[:, :], AF.Exp)
        p.ts(wkcol[:, :], iotac[:, 1:2], ldc, OP.mult)
        p.act(wkcol[:, :], wkcol[:, :], AF.Exp)
        p.act(wscol[:, :], ldrep[0:64, h:h + 1], AF.Exp, scale=128.0)
        p.memset(s_st[:, 0:K], 0.0)
        p.memset(s_bf[:, :], 0.0)
        for t in range(NT):
            p.mm(pA[:, :], kTh[:, t * 128:(t + 1) * 128], rTh[:, t * 128:(t + 1) * 128], sgc=True)
            p.tt(PTc[:, :], pA[:, :], wmT[:, :], OP.mult)
            p.mm(pO[:, :], PTc[:, :], vnh[t][:, :], sgc=True)
            p.mm(pC[:, :], rTh[:, t * 128:(t + 1) * 128], s_bf[:, :], sgc=True)
            p.ts(y1[:, :], pC[:, :], wbcol[:, 0:1], OP.mult)
            p.tt(y1[:, :], y1[:, :], pO[:, :], OP.add)
            p.ts(kwk[:, :], knh[t][:, :], wkcol[:, 0:1], OP.mult)
            p.mm(pS3[:, :], kwk[:, :], vnh[t][:, :], sgc=True)
            p.ts(s_st[:, 0:K], s_st[:, 0:K], wscol[:, 0:1], OP.mult)
            p.tt(s_st[:, 0:K], s_st[:, 0:K], pS3[:, :], OP.add)
            p.vcopy(s_bf[:, :], s_st[:, 0:K])
            p.ts(y2[:, :], y1[:, :], 0.125, OP.mult)
            p.red(gm[:, 0:1], y2[:, :])
            p.ts(gm[:, 0:1], gm[:, 0:1], 1.0 / K, OP.mult)
            p.act(y1[:, :], y2[:, :], AF.Square)
            p.red(gv[:, 0:1], y1[:, :])
            p.ts(gv[:, 0:1], gv[:, 0:1], 1.0 / K, OP.mult)
            p.stt(gv[:, 0:1], gm[:, 0:1], gm[:, 0:1], gv[:, 0:1], OP.mult, OP.subtract)
            p.act(gv[:, 0:1], gv[:, 0:1], AF.Ln, bias=1e-5, scale=-1.0)
            p.act(gv[:, 0:1], gv[:, 0:1], AF.Exp, scale=-0.5)
            p.ts(y2[:, :], y2[:, :], gm[:, 0:1], OP.subtract)
            p.ts(y2[:, :], y2[:, :], gv[:, 0:1], OP.mult)
            p.tt(y2[:, :], y2[:, :], lnwrep[:, h * K:(h + 1) * K], OP.mult)
            p.tt(ynat[t][:, h * K:(h + 1) * K], y2[:, :], lnbrep[:, h * K:(h + 1) * K], OP.add)
    p.free_grp("p2")

    xres = p.sb([128, D], F32, "xres", grp="p3")
    gt = p.sb([128, D], BF16, "gt", grp="p3")
    yg = p.sb([128, D], BF16, "yg", grp="p3")
    ztmp = p.sb([128, 512], F32, "zt3", grp="p3")
    otp = p.sb([128, 128], BF16, "otp", grp="p3")
    pbf3 = p.ps([128, 128], BF16, "ptr3", grp="p3")
    py = [p.ps([128, 512], F32, "py", grp="p3") for _ in range(2)]
    for t in range(NT):
        p.dma(xres[:, :], x_h[t * 128:(t + 1) * 128, :])
        p.dma(gt[:, :], gn_hbm[t * 128:(t + 1) * 128, :])
        p.tt(yg[:, :], ynat[t][:, :], gt[:, :], OP.mult)
        for hv in range(ND):
            p.trans(pbf3[:, :], yg[:, hv * 128:(hv + 1) * 128], c["id_sb"][:, :])
            p.vcopy(otp[:, :], pbf3[:, :])
            for dc in range(2):
                p.mm(py[dc][:, :], otp[:, :], woT[hv][:, dc * 512:(dc + 1) * 512],
                     start=(hv == 0), stop=(hv == ND - 1), wait=(dc == 0), sgc=True)
        for dc in range(2):
            dcs = slice(dc * 512, (dc + 1) * 512)
            p.tt(ztmp[:, :], py[dc][:, :], xres[:, dcs], OP.add)
            p.dma(y_h[t * 128:(t + 1) * 128, dcs], ztmp[:, :])
    return p.build()


def run_l0_attn_b(x, inp):
    nc = get_prog("l0_attn_b", build_l0_attn_b)
    cmn = const_inputs()
    jj = np.arange(128)
    e1 = np.maximum(jj[None, :] - jj[:, None] - 1, 0).astype(np.float32)
    su = (jj[None, :] > jj[:, None]).astype(np.float32)
    iota = np.stack([jj.astype(np.float32), (127 - jj).astype(np.float32)], axis=1)
    iota = np.ascontiguousarray(iota)
    maps = []
    for i in range(NCORE):
        m = dict(cmn)
        m["c_e1"] = e1
        m["c_su"] = su
        m["c_eye"] = np.eye(128, dtype=np.float32)
        m["c_iota"] = iota
        m["x"] = np.ascontiguousarray(x[i])
        m["tmk"] = inp["L0_attn_b_time_mix_k"]
        m["tmv"] = inp["L0_attn_b_time_mix_v"]
        m["tmr"] = inp["L0_attn_b_time_mix_r"]
        m["tmg"] = inp["L0_attn_b_time_mix_g"]
        m["td"] = inp["L0_attn_b_time_decay"]
        m["tf"] = inp["L0_attn_b_time_faaaa"]
        m["W_r"] = inp["L0_attn_b_W_r"]
        m["W_k"] = inp["L0_attn_b_W_k"]
        m["W_v"] = inp["L0_attn_b_W_v"]
        m["W_g"] = inp["L0_attn_b_W_g"]
        m["W_o"] = inp["L0_attn_b_W_o"]
        m["ln_w"] = inp["L0_attn_b_ln_w"]
        m["ln_b"] = inp["L0_attn_b_ln_b"]
        maps.append(m)
    res = run_bass_kernel_spmd(nc, maps, core_ids=list(range(NCORE)))
    return np.stack([r["y"] for r in res.results], axis=0)



def build_l0_attn_a():
    p = Prog()
    c = load_consts(p)
    x_h = p.dram("x", [T, D], F32)
    y_h = p.dram("y", [T, D], F32, kind="ExternalOutput")
    wsel_h = p.dram("W_sel", [D, 160], F32)
    dtw_h = p.dram("dt_W", [32, 512], F32)
    dtb_h = p.dram("dt_b", [512], F32)
    wk_h = p.dram("Wk", [D, 512], F32)
    wv_h = p.dram("Wv", [D, 512], F32)
    wr_h = p.dram("Wr", [D, 512], F32)
    wg_h = p.dram("Wg", [D, 512], F32)
    wo_h = p.dram("Wo", [512, D], F32)
    bd_h = p.dram("base_decay", [8, 64], F32)
    tf_h = p.dram("time_first", [8, 64], F32)
    tri_h = p.dram("c_tri", [128, 128], F32)
    sl_h = p.dram("c_sl", [128, 128], F32)
    msk_h = p.dram("c_msk", [128, 128], BF16)
    rm_h = p.dram("c_rm", [128, 4], F32)
    sel8_h = p.dram("c_sel8", [8, 512], BF16)

    ld_hbm = p.dram("sc_ld", [T, 512], F32, kind="Internal")
    kn_hbm = p.dram("sc_kn", [T, 512], BF16, kind="Internal")
    kT_hbm = p.dram("sc_kT", [512, T], BF16, kind="Internal")
    vn_hbm = p.dram("sc_vn", [T, 512], BF16, kind="Internal")
    rg_hbm = p.dram("sc_rg", [T, 512], BF16, kind="Internal")
    ex_hbm = p.dram("sc_ex", [T, 512], BF16, kind="Internal")
    bs_hbm = p.dram("sc_bs", [T], F32, kind="Internal")

    wo = [p.sb([128, D], BF16, "wo") for _ in range(4)]

    prep = p.ps([128, 512], F32, "prep", grp="p1")
    fscr = p.sb([128, 512], F32, "fscr", grp="p1")
    wsel = [p.sb([128, 160], BF16, "wsel", grp="p1") for _ in range(ND)]
    wk = [p.sb([128, 512], BF16, "wk", grp="p1") for _ in range(ND)]
    wv = [p.sb([128, 512], BF16, "wv", grp="p1") for _ in range(ND)]
    wr = [p.sb([128, 512], BF16, "wr", grp="p1") for _ in range(ND)]
    wg = [p.sb([128, 512], BF16, "wg", grp="p1") for _ in range(ND)]
    for k in range(ND):
        p.dma(fscr[:, 0:160], wsel_h[k * 128:(k + 1) * 128, :])
        p.vcopy(wsel[k][:, :], fscr[:, 0:160])
        for (dst, hh) in ((wk, wk_h), (wv, wv_h), (wr, wr_h), (wg, wg_h)):
            p.dma(fscr[:, :], hh[k * 128:(k + 1) * 128, :])
            p.vcopy(dst[k][:, :], fscr[:, :])
    for k in range(4):
        p.dma(fscr[:, :], wo_h[k * 128:(k + 1) * 128, 0:512])
        p.vcopy(wo[k][:, 0:512], fscr[:, :])
        p.dma(fscr[:, :], wo_h[k * 128:(k + 1) * 128, 512:1024])
        p.vcopy(wo[k][:, 512:1024], fscr[:, :])
    dtw = p.sb([32, 512], BF16, "dtw", grp="p1")
    p.dma(fscr[0:32, :], dtw_h[:, :])
    p.vcopy(dtw[:, :], fscr[0:32, :])
    srow = p.sb([1, 512], F32, "srow", grp="p1")
    dtbrep = p.sb([128, 512], F32, "dtbrep", grp="p1")
    nbdrep = p.sb([128, 512], F32, "nbdrep", grp="p1")
    tfrep = p.sb([128, 512], BF16, "tfrep", grp="p1")
    p.dma(srow[:, :], dtb_h.rearrange("(one n) -> one n", one=1))
    p.mm(prep[:, :], c["ones_sb"][:, :], srow[:, :])
    p.scopy(dtbrep[:, :], prep[:, :])
    p.dma(srow[:, :], bd_h.rearrange("h k -> (h k)").rearrange("(one n) -> one n", one=1))
    p.mm(prep[:, :], c["ones_sb"][:, :], srow[:, :])
    p.act(nbdrep[:, :], prep[:, :], AF.Copy, scale=-1.0)
    p.dma(srow[:, :], tf_h.rearrange("h k -> (h k)").rearrange("(one n) -> one n", one=1))
    p.mm(prep[:, :], c["ones_sb"][:, :], srow[:, :])
    p.scopy(tfrep[:, :], prep[:, :])
    sel8 = p.sb([8, 512], BF16, "sel8", grp="p1")
    p.dma(sel8[:, :], sel8_h[:, :])

    lnT = [p.sb([128, 512], BF16, "lnTc", grp="p1") for _ in range(ND)]
    dtT = p.sb([32, T], BF16, "dtT", grp="p1")
    xt = p.sb([128, D], F32, "xt", grp="p1")
    tmp = p.sb([128, D], F32, "lntmp", grp="p1")
    mcol = p.sb([128, 1], F32, "m", grp="p1")
    vcol = p.sb([128, 1], F32, "v", grp="p1")
    lnb2 = p.sb([128, D], BF16, "lnb2", grp="p1")
    pbf = p.ps([128, 128], BF16, "ptr", grp="p1")
    ph = p.ps([128, 512], F32, "ph", grp="p1")
    pse = p.ps([128, 160], F32, "pse", grp="p1")
    ptr32 = p.ps([32, 128], BF16, "ptr32", grp="p1")
    bsc = p.sb([128, 1], F32, "bsc", grp="p1")
    dtrb = p.sb([128, 32], BF16, "dtrb", grp="p1")
    ktile = p.sb([128, 512], BF16, "ktile", grp="p1")
    vtile = p.sb([128, 512], BF16, "vtile", grp="p1")
    rs = p.sb([128, 512], BF16, "rs", grp="p1")
    gs = p.sb([128, 512], BF16, "gs", grp="p1")
    e1b = p.sb([128, 512], BF16, "e1b", grp="p1")
    sqf = p.sb([128, 512], F32, "sqf", grp="p1")
    rs8 = p.sb([128, 8], BF16, "rs8", grp="p1")
    rs8T = p.sb([8, 128], BF16, "rs8T", grp="p1")
    x32 = p.sb([128, 512], F32, "x32", grp="p1")
    ptr8 = p.ps([8, 128], BF16, "ptr8", grp="p1")

    for ch in range(8):
        tok = slice(ch * 512, (ch + 1) * 512)
        for tt_ in range(4):
            t = ch * 4 + tt_
            p.dma(xt[:, :], x_h[t * 128:(t + 1) * 128, :])
            layernorm_tile(p, lnb2, xt, tmp, mcol, vcol)
            transpose_nat_to_T(p, c, lnT, lnb2, tt_, pbf)
        for tt_ in range(4):
            t = ch * 4 + tt_
            tsl = slice(tt_ * 128, (tt_ + 1) * 128)
            for k in range(ND):
                p.mm(pse[:, :], lnT[k][:, tsl], wsel[k][:, :],
                     start=(k == 0), stop=(k == ND - 1), wait=(k == 0))
            p.red(bsc[:, 0:1], pse[:, 32:96])
            p.ts(bsc[:, 0:1], bsc[:, 0:1], 1.0 / 64.0, OP.mult)
            p.dma(bs_hbm[t * 128:(t + 1) * 128].rearrange("(pp o) -> pp o", o=1), bsc[:, 0:1])
            p.scopy(dtrb[:, :], pse[:, 0:32])
            p.trans(ptr32[:, :], dtrb[:, :], c["id_sb"][:, :])
            p.vcopy(dtT[:, t * 128:(t + 1) * 128], ptr32[:, :])
            for k in range(ND):
                p.mm(ph[:, :], lnT[k][:, tsl], wk[k][:, :],
                     start=(k == 0), stop=(k == ND - 1), wait=(k == 0))
            p.scopy(ktile[:, :], ph[:, :])
            p.dma(kn_hbm[t * 128:(t + 1) * 128, :], ktile[:, :])
            for k in range(ND):
                p.mm(ph[:, :], lnT[k][:, tsl], wv[k][:, :],
                     start=(k == 0), stop=(k == ND - 1), wait=(k == 0))
            p.scopy(vtile[:, :], ph[:, :])
            p.dma(vn_hbm[t * 128:(t + 1) * 128, :], vtile[:, :])
            for k in range(ND):
                p.mm(ph[:, :], lnT[k][:, tsl], wr[k][:, :],
                     start=(k == 0), stop=(k == ND - 1), wait=(k == 0))
            p.act(rs[:, :], ph[:, :], AF.Sigmoid)
            for k in range(ND):
                p.mm(ph[:, :], lnT[k][:, tsl], wg[k][:, :],
                     start=(k == 0), stop=(k == ND - 1), wait=(k == 0))
            p.act(gs[:, :], ph[:, :], AF.Sigmoid)
            p.tt(rs[:, :], rs[:, :], gs[:, :], OP.mult)
            p.dma(rg_hbm[t * 128:(t + 1) * 128, :], rs[:, :])
            p.tt(e1b[:, :], ktile[:, :], vtile[:, :], OP.mult)
            p.tt(e1b[:, :], e1b[:, :], tfrep[:, :], OP.mult)
            p.act(sqf[:, :], ktile[:, :], AF.Square)
            for h8 in range(8):
                p.red(mcol[:, 0:1], sqf[:, h8 * 64:(h8 + 1) * 64])
                p.ts(rs8[:, h8:h8 + 1], mcol[:, 0:1], bsc[:, 0:1], OP.mult)
            p.trans(ptr8[:, :], rs8[:, :], c["id_sb"][:, :])
            p.vcopy(rs8T[:, :], ptr8[:, :])
            p.mm(ph[:, :], rs8T[:, :], sel8[:, :], sgc=True)
            p.tt(x32[:, :], vtile[:, :], ph[:, :], OP.mult)
            p.tt(e1b[:, :], e1b[:, :], x32[:, :], OP.add)
            p.dma(ex_hbm[t * 128:(t + 1) * 128, :], e1b[:, :])
            p.mm(ph[:, :], dtT[:, t * 128:(t + 1) * 128], dtw[:, :], sgc=True)
            p.tt(x32[:, :], ph[:, :], dtbrep[:, :], OP.add)
            p.act(x32[:, :], x32[:, :], AF.Exp)
            p.act(x32[:, :], x32[:, :], AF.Ln, bias=1.0)
            p.tt(x32[:, :], x32[:, :], nbdrep[:, :], OP.mult)
            p.dma(ld_hbm[t * 128:(t + 1) * 128, :], x32[:, :])
        for ntl in range(4):
            for k in range(ND):
                p.mm(ph[:, :], wk[k][:, ntl * 128:(ntl + 1) * 128], lnT[k][:, :],
                     start=(k == 0), stop=(k == ND - 1), wait=(k == 0))
            p.scopy(x32[:, :], ph[:, :])
            p.vcopy(rs[:, :], x32[:, :])
            p.dma(kT_hbm[ntl * 128:(ntl + 1) * 128, tok], rs[:, :])
    p.free_grp("p1")

    tric = p.sb([128, 128], F32, "tric", grp="p2")
    slc = p.sb([128, 128], F32, "slc", grp="p2")
    mskc = p.sb([128, 128], BF16, "mskc", grp="p2")
    rmc = p.sb([128, 4], F32, "rmc", grp="p2")
    p.dma(tric[:, :], tri_h[:, :])
    p.dma(slc[:, :], sl_h[:, :])
    p.dma(mskc[:, :], msk_h[:, :])
    p.dma(rmc[:, :], rm_h[:, :])
    ldt = p.sb([128, 512], F32, "ldt", grp="p2")
    knt = p.sb([128, 512], BF16, "knt", grp="p2")
    vnt = p.sb([128, 512], BF16, "vnt", grp="p2")
    vp = p.sb([128, 512], BF16, "vp", grp="p2")
    ext = p.sb([128, 512], BF16, "ext", grp="p2")
    rgt = p.sb([128, 512], BF16, "rgt", grp="p2")
    kTt = [p.sb([128, 128], BF16, "kTt", grp="p2") for _ in range(4)]
    bscol = p.sb([128, 1], F32, "bscol", grp="p2")
    xrest = p.sb([128, D], F32, "xrest", grp="p2")
    qTh = [p.sb([64, 128], F32, "qTh", grp="p2") for _ in range(8)]
    uTh = [p.sb([64, 128], F32, "uTh", grp="p2") for _ in range(8)]
    qe = p.sb([128, 128], F32, "qe", grp="p2")
    Ecol = [p.sb([128, 4], F32, "Ecol", grp="p2") for _ in range(4)]
    er = p.sb([128, 512], F32, "er", grp="p2")
    wt = p.sb([128, 512], BF16, "wt", grp="p2")
    wtc = p.sb([128, 128], BF16, "wtc", grp="p2")
    S = [p.sb([128, 128], F32, "S", grp="p2") for _ in range(4)]
    Sstk = [p.sb([64, 256], F32, "Sstk", grp="p2") for _ in range(8)]
    PTm = p.sb([128, 128], BF16, "PTm", grp="p2")
    ya = p.sb([128, 64], F32, "ya", grp="p2")
    yb = p.sb([128, 64], F32, "yb", grp="p2")
    yacc = p.sb([128, 512], F32, "yacc", grp="p2")
    ysf = p.sb([128, 512], BF16, "ysf", grp="p2")
    otp = p.sb([128, 128], BF16, "otp", grp="p2")
    pL = p.ps([128, 512], F32, "pL", grp="p2")
    pT = p.ps([128, 128], F32, "pT", grp="p2")
    pPT = p.ps([128, 128], F32, "pPT", grp="p2")
    pO = p.ps([128, 512], F32, "pO", grp="p2")
    pC = p.ps([128, 256], F32, "pC", grp="p2")
    ptr2 = p.ps([128, 128], BF16, "ptr2", grp="p2")
    py = [p.ps([128, 512], F32, "py", grp="p2") for _ in range(2)]

    for kc in range(4):
        p.memset(S[kc][:, :], 0.0)
    for t in range(NT):
        p.dma(ldt[:, :], ld_hbm[t * 128:(t + 1) * 128, :])
        p.dma(knt[:, :], kn_hbm[t * 128:(t + 1) * 128, :])
        p.dma(vnt[:, :], vn_hbm[t * 128:(t + 1) * 128, :])
        p.dma(ext[:, :], ex_hbm[t * 128:(t + 1) * 128, :])
        p.dma(rgt[:, :], rg_hbm[t * 128:(t + 1) * 128, :])
        p.dma(bscol[:, 0:1], bs_hbm[t * 128:(t + 1) * 128].rearrange("(pp o) -> pp o", o=1))
        p.dma(xrest[:, :], x_h[t * 128:(t + 1) * 128, :])
        for kc in range(4):
            p.dma(kTt[kc][:, :], kT_hbm[kc * 128:(kc + 1) * 128, t * 128:(t + 1) * 128])
        for kc in range(4):
            p.mm(pL[:, kc * 128:(kc + 1) * 128], ldt[:, kc * 128:(kc + 1) * 128], tric[:, :],
                 sgc=True, wait=(kc == 0), start=(kc == 0), stop=(kc == 3))
        for kc in range(4):
            psl = pL[:, kc * 128:(kc + 1) * 128]
            p.act(qe[:, :], psl, AF.Exp)
            for hf in range(2):
                p.tt(qTh[2 * kc + hf][:, :], kTt[kc][hf * 64:(hf + 1) * 64, :],
                     qe[hf * 64:(hf + 1) * 64, :], OP.mult)
            p.ts(qe[:, :], psl, -1.0, OP.mult)
            p.ts(qe[:, :], qe[:, :], 80.0, OP.min)
            p.act(qe[:, :], qe[:, :], AF.Exp)
            for hf in range(2):
                p.tt(uTh[2 * kc + hf][:, :], kTt[kc][hf * 64:(hf + 1) * 64, :],
                     qe[hf * 64:(hf + 1) * 64, :], OP.mult)
            for cc in range(4):
                p.act(Ecol[kc][:, cc:cc + 1], psl[:, 32 * cc + 31:32 * cc + 32], AF.Exp)
        p.mm(pL[:, :], slc[:, :], ldt[:, :], sgc=True)
        p.act(er[:, :], pL[:, :], AF.Exp)
        p.tt(wt[:, :], knt[:, :], er[:, :], OP.mult)
        p.ts(vp[:, :], vnt[:, :], bscol[:, 0:1], OP.mult)
        for cc in range(4):
            for kc in range(4):
                p.vcopy(Sstk[2 * kc][:, cc * 64:(cc + 1) * 64], S[kc][0:64, 0:64])
                p.vcopy(Sstk[2 * kc + 1][:, cc * 64:(cc + 1) * 64], S[kc][64:128, 64:128])
            for kc in range(4):
                p.ts(wtc[:, :], wt[:, kc * 128:(kc + 1) * 128], rmc[:, cc:cc + 1], OP.mult)
                p.mm(pT[:, :], wtc[:, :], vp[:, kc * 128:(kc + 1) * 128], sgc=True)
                p.ts(S[kc][:, :], S[kc][:, :], Ecol[kc][:, cc:cc + 1], OP.mult)
                p.tt(S[kc][:, :], S[kc][:, :], pT[:, :], OP.add)
        for h in range(8):
            p.mm(pC[:, :], qTh[h][:, :], Sstk[h][:, :], sgc=True)
            p.mm(pPT[:, :], uTh[h][:, :], qTh[h][:, :], sgc=True)
            p.tt(PTm[:, :], pPT[:, :], mskc[:, :], OP.mult)
            p.mm(pO[:, h * 64:(h + 1) * 64], PTm[:, :], vp[:, h * 64:(h + 1) * 64], sgc=True,
                 start=(h == 0), stop=(h == 7))
            p.ts(ya[:, :], pC[:, 0:64], rmc[:, 0:1], OP.mult)
            p.stt(yb[:, :], pC[:, 64:128], rmc[:, 1:2], ya[:, :], OP.mult, OP.add)
            p.stt(ya[:, :], pC[:, 128:192], rmc[:, 2:3], yb[:, :], OP.mult, OP.add)
            p.stt(yb[:, :], pC[:, 192:256], rmc[:, 3:4], ya[:, :], OP.mult, OP.add)
            p.tt(yacc[:, h * 64:(h + 1) * 64], yb[:, :], pO[:, h * 64:(h + 1) * 64], OP.add)
        p.tt(yacc[:, :], yacc[:, :], ext[:, :], OP.add)
        p.tt(ysf[:, :], yacc[:, :], rgt[:, :], OP.mult)
        for kc in range(4):
            p.trans(ptr2[:, :], ysf[:, kc * 128:(kc + 1) * 128], c["id_sb"][:, :])
            p.vcopy(otp[:, :], ptr2[:, :])
            for dc in range(2):
                p.mm(py[dc][:, :], otp[:, :], wo[kc][:, dc * 512:(dc + 1) * 512],
                     start=(kc == 0), stop=(kc == 3), wait=(dc == 0), sgc=True)
        for dc in range(2):
            dcs = slice(dc * 512, (dc + 1) * 512)
            p.tt(er[:, :], py[dc][:, :], xrest[:, dcs], OP.add)
            p.dma(y_h[t * 128:(t + 1) * 128, dcs], er[:, :])
    return p.build()


def run_l0_attn_a(x, inp):
    nc = get_prog("l0_attn_a", build_l0_attn_a)
    cmn = const_inputs()
    jj = np.arange(128)
    same = (jj[:, None] // 32) == (jj[None, :] // 32)
    tri = ((jj[:, None] <= jj[None, :]) & same).astype(np.float32)
    sl = ((jj[:, None] > jj[None, :]) & same).astype(np.float32)
    msk = ((jj[:, None] < jj[None, :]) & same).astype(bf16_np())
    rm = np.zeros((128, 4), np.float32)
    for cc in range(4):
        rm[cc * 32:(cc + 1) * 32, cc] = 1.0
    sel8 = np.zeros((8, 512), np.float32)
    for h8 in range(8):
        sel8[h8, h8 * 64:(h8 + 1) * 64] = 1.0
    maps = []
    for i in range(NCORE):
        m = dict(cmn)
        m["c_tri"] = tri
        m["c_sl"] = sl
        m["c_msk"] = msk
        m["c_rm"] = rm
        m["c_sel8"] = sel8.astype(bf16_np())
        m["x"] = np.ascontiguousarray(x[i])
        m["W_sel"] = inp["L0_attn_a_W_sel"]
        m["dt_W"] = inp["L0_attn_a_dt_W"]
        m["dt_b"] = inp["L0_attn_a_dt_b"]
        m["Wk"] = inp["L0_attn_a_Wk"]
        m["Wv"] = inp["L0_attn_a_Wv"]
        m["Wr"] = inp["L0_attn_a_Wr"]
        m["Wg"] = inp["L0_attn_a_Wg"]
        m["Wo"] = inp["L0_attn_a_Wo"]
        m["base_decay"] = inp["L0_attn_a_base_decay"]
        m["time_first"] = inp["L0_attn_a_time_first"]
        maps.append(m)
    res = run_bass_kernel_spmd(nc, maps, core_ids=list(range(NCORE)))
    return np.stack([r["y"] for r in res.results], axis=0)


_PROG_CACHE = {}


def get_prog(key, builder):
    if key not in _PROG_CACHE:
        _PROG_CACHE[key] = builder()
    return _PROG_CACHE[key]


def run_l1_ffn(x, inp):
    nc = get_prog("l1_ffn", build_l1_ffn)
    cmn = const_inputs()
    maps = []
    for i in range(NCORE):
        m = dict(cmn)
        m["x"] = np.ascontiguousarray(x[i])
        m["ln_g"] = inp["L1_ffn_ln_gamma"]
        m["ln_b"] = inp["L1_ffn_ln_beta"]
        m["fc1_q"] = inp["L1_ffn_fc1_q"]
        m["fc1_s"] = inp["L1_ffn_fc1_scales"]
        m["fc1_b"] = inp["L1_ffn_fc1_bias"]
        m["fc2_q"] = inp["L1_ffn_fc2_q"]
        m["fc2_s"] = inp["L1_ffn_fc2_scales"]
        m["fc2_b"] = inp["L1_ffn_fc2_bias"]
        maps.append(m)
    res = run_bass_kernel_spmd(nc, maps, core_ids=list(range(NCORE)))
    return np.stack([r["y"] for r in res.results], axis=0)


def kernel(**inp):
    x = np.ascontiguousarray(np.asarray(inp["L0_attn_a_x"], np.float32))
    x = run_l0_attn_a(x, inp)
    x = run_l0_ffn(x, inp)
    x = run_l0_attn_b(x, inp)
    x = run_l0_ffn(x, inp)
    x = run_attn("mla", x, inp)
    x = run_l1_ffn(x, inp)
    x = run_attn("sig", x, inp)
    x = run_l1_ffn(x, inp)
    return np.asarray(x, np.float32)


# version 03
# speedup vs baseline: 1.0096x; 1.0096x over previous
import contextlib
import numpy as np
import concourse.bass as bass
import concourse.mybir as mybir
from concourse.bass_utils import run_bass_kernel_spmd

F32 = mybir.dt.float32
BF16 = mybir.dt.bfloat16
I32 = mybir.dt.int32
AF = mybir.ActivationFunctionType
OP = mybir.AluOpType
AX = mybir.AxisListType

NCORE = 8
T = 4096
D = 1024
NT = T // 128
ND = D // 128


class Prog:

    def __init__(self):
        self.nc = bass.Bass("TRN2", target_bir_lowering=False)
        self.q = {"tensor": [], "vector": [], "scalar": [], "sync": [], "gpsimd": []}
        self.cnt = 0
        self.st = contextlib.ExitStack()
        self._names = {}
        self._groups = {}
        self.pref = ""
        self.stage = ""


    def name(self, base):
        i = self._names.get(base, 0)
        self._names[base] = i + 1
        return f"{base}{i}"

    def dram(self, name, shape, dtype, kind="ExternalInput"):
        return self.nc.dram_tensor(self.pref + name, list(shape), dtype, kind=kind).ap()

    def _alloc(self, cm, grp):
        h = cm.__enter__()
        if grp is None:
            self.st.callback(cm.__exit__, None, None, None)
        else:
            self._groups.setdefault(grp, []).append(cm)
        return h.ap() if hasattr(h, "ap") else h

    def sb(self, shape, dtype, base="sb", grp=None):
        g = self.stage + ":" + (grp or "main")
        return self._alloc(self.nc.sbuf_tensor(self.name(self.pref + base), list(shape), dtype), g)

    def ps(self, shape, dtype=F32, base="ps", grp=None):
        g = self.stage + ":" + (grp or "main")
        return self._alloc(self.nc.psum_tensor(self.name(self.pref + base), list(shape), dtype), g)

    def free_grp(self, grp):
        for cm in reversed(self._groups.pop(self.stage + ":" + grp, [])):
            cm.__exit__(None, None, None)

    def end_stage(self):
        ks = [k for k in self._groups if k.startswith(self.stage + ":")]
        for k in reversed(ks):
            for cm in reversed(self._groups.pop(k)):
                cm.__exit__(None, None, None)

    EPOCH = 50000

    def emit(self, eng, fn, inc=1, wait=True):
        if not hasattr(self, "_epoch"):
            self._epoch = 0
            self._etotals = []
        if self.cnt >= self.EPOCH:
            self._etotals.append(self.cnt)
            self._epoch += 1
            self.cnt = 0
            wait = True
        w = self.cnt if wait else None
        self.q[eng].append((self._epoch, w, fn, inc))
        self.cnt += inc

    def dma(self, out, in_, wait=True):
        self.emit("sync", lambda e, o=out, i=in_: e.dma_start(out=o, in_=i), inc=16, wait=wait)

    def mm(self, out, lhsT, rhs, start=True, stop=True, wait=True, sgc=False):
        self.emit("tensor",
                  lambda e, o=out, l=lhsT, r=rhs, s0=start, s1=stop, g=sgc:
                  e.matmul(o, l, r, start=s0, stop=s1, skip_group_check=g),
                  wait=wait)

    def trans(self, out, in_, ident):
        self.emit("tensor", lambda e, o=out, i=in_, d=ident: e.transpose(o, i, d))

    def act(self, out, in_, func, bias=None, scale=None):
        if isinstance(bias, float):
            bias = np.float32(bias)
        if isinstance(scale, float):
            scale = np.float32(scale)

        def f(e, o=out, i=in_, fu=func, b=bias, s=scale):
            kw = {}
            if b is not None:
                kw["bias"] = b
            if s is not None:
                kw["scale"] = s
            return e.activation(o, i, fu, **kw)
        self.emit("scalar", f)

    def scopy(self, out, in_):
        self.emit("scalar", lambda e, o=out, i=in_: e.copy(o, i))

    def tt(self, out, a, b, op):
        self.emit("vector", lambda e, o=out, x=a, y=b, p=op: e.tensor_tensor(o, x, y, p))

    def ts(self, out, a, s1, op0, s2=None, op1=None):
        def f(e, o=out, x=a, a1=s1, a2=s2, p0=op0, p1=op1):
            if a2 is None:
                return e.tensor_scalar(o, x, a1, None, p0)
            return e.tensor_scalar(o, x, a1, a2, p0, p1)
        self.emit("vector", f)

    def stt(self, out, in0, scalar, in1, op0, op1):
        self.emit("vector",
                  lambda e, o=out, x=in0, s=scalar, y=in1, p0=op0, p1=op1:
                  e.scalar_tensor_tensor(o, x, s, y, p0, p1))

    def red(self, out, in_, op=OP.add):
        self.emit("vector", lambda e, o=out, i=in_, p=op: e.tensor_reduce(o, i, AX.X, p))

    def vcopy(self, out, in_):
        self.emit("vector", lambda e, o=out, i=in_: e.tensor_copy(o, i))

    def recip(self, out, in_):
        self.emit("vector", lambda e, o=out, i=in_: e.reciprocal(o, i))

    def memset(self, ap, const):
        self.emit("vector", lambda e, o=ap, c=const: e.memset(o, c))

    def build(self):
        nc = self.nc
        if not hasattr(self, "_epoch"):
            self._epoch = 0
            self._etotals = []
        nsem = self._epoch + 1
        sems = [self.st.enter_context(nc.semaphore(f"ser{i}")) for i in range(nsem)]
        block = self.st.enter_context(nc.Block())
        total = self.cnt
        etot = self._etotals
        q = self.q

        def runner(eng_name):
            def run(eng):
                cur_ep = 0
                for ep, w, fn, inc in q[eng_name]:
                    while cur_ep < ep:
                        eng.wait_ge(sems[cur_ep], etot[cur_ep])
                        cur_ep += 1
                    if w is not None and w > 0:
                        eng.wait_ge(sems[ep], w)
                    inst = fn(eng)
                    inst.then_inc(sems[ep], inc)
                if eng_name == "vector":
                    while cur_ep < len(etot):
                        eng.wait_ge(sems[cur_ep], etot[cur_ep])
                        cur_ep += 1
                    eng.wait_ge(sems[-1], total)
            return run

        block.sync(runner("sync"))
        block.tensor(runner("tensor"))
        block.scalar(runner("scalar"))
        block.vector(runner("vector"))
        if q["gpsimd"]:
            block.gpsimd(runner("gpsimd"))
        for k in reversed(list(self._groups)):
            for cm in reversed(self._groups.pop(k)):
                cm.__exit__(None, None, None)
        self.st.close()
        return nc



def load_consts(p):
    c = {}
    ones = p.dram("c_ones", [1, 128], F32)
    ident = p.dram("c_ident", [128, 128], BF16)
    c["ones_sb"] = p.sb([1, 128], F32, "ones")
    c["id_sb"] = p.sb([128, 128], BF16, "ident")
    p.dma(c["ones_sb"][:, :], ones[:, :])
    p.dma(c["id_sb"][:, :], ident[:, :])
    return c


def const_inputs():
    return {
        "c_ones": np.ones((1, 128), np.float32),
        "c_ident": np.eye(128, dtype=np.float32).astype(bf16_np()),
    }


_BF16 = None


def bf16_np():
    global _BF16
    if _BF16 is None:
        import ml_dtypes
        _BF16 = ml_dtypes.bfloat16
    return _BF16


def replicate_row(p, c, dst_sb, row_sb, n, scale=None):
    pr = p.ps([128, 512], F32, "prep")
    for j in range(0, n, 512):
        w = min(512, n - j)
        p.mm(pr[:, :w], c["ones_sb"][:, :], row_sb[:, j:j + w])
        if scale is None:
            p.scopy(dst_sb[:, j:j + w], pr[:, :w])
        else:
            p.act(dst_sb[:, j:j + w], pr[:, :w], AF.Copy, scale=scale)


def load_row(p, hbm_vec, n, dtype=F32):
    r = p.sb([1, n], dtype, "row")
    src = hbm_vec.rearrange("(one n) -> one n", one=1) if len(hbm_vec.shape) == 1 else hbm_vec
    p.dma(r[:, :], src)
    return r


def layernorm_tile(p, dst, xt, tmp, m, v, width=None, gamma_rep=None, beta_rep=None, eps=1e-5):
    W = width or xt.shape[-1]
    p.red(m[:, 0:1], xt[:, :W])
    p.ts(m[:, 0:1], m[:, 0:1], 1.0 / W, OP.mult)
    p.act(tmp[:, :W], xt[:, :W], AF.Square)
    p.red(v[:, 0:1], tmp[:, :W])
    p.ts(v[:, 0:1], v[:, 0:1], 1.0 / W, OP.mult)
    p.stt(v[:, 0:1], m[:, 0:1], m[:, 0:1], v[:, 0:1], OP.mult, OP.subtract)
    p.act(v[:, 0:1], v[:, 0:1], AF.Ln, bias=eps, scale=-1.0)
    p.act(v[:, 0:1], v[:, 0:1], AF.Exp, scale=-0.5)
    p.ts(tmp[:, :W], xt[:, :W], m[:, 0:1], OP.subtract)
    if gamma_rep is None:
        p.ts(dst[:, :W], tmp[:, :W], v[:, 0:1], OP.mult)
    else:
        p.ts(tmp[:, :W], tmp[:, :W], v[:, 0:1], OP.mult)
        p.stt(dst[:, :W], tmp[:, :W], 1.0, gamma_rep[:, :W], OP.mult, OP.mult)
        p.tt(dst[:, :W], dst[:, :W], beta_rep[:, :W], OP.add)


def transpose_nat_to_T(p, c, dstT_list, nat_bf, tcol, pbf):
    for j in range(len(dstT_list)):
        p.trans(pbf[:, :128], nat_bf[:, j * 128:(j + 1) * 128], c["id_sb"][:, :])
        p.vcopy(dstT_list[j][:, tcol * 128:(tcol + 1) * 128], pbf[:, :128])



def build_l1_ffn(p=None, c=None, xin=None, yout=None):
    standalone = p is None
    if standalone:
        p = Prog()
        c = load_consts(p)
    x_h = xin if xin is not None else p.dram("x", [T, D], F32)
    y_h = yout if yout is not None else p.dram("y", [T, D], F32, kind="ExternalOutput")
    g_h = p.dram("ln_g", [D], F32)
    b_h = p.dram("ln_b", [D], F32)
    q1_h = p.dram("fc1_q", [D, 4096], I32)
    s1_h = p.dram("fc1_s", [8, 4096], F32)
    b1_h = p.dram("fc1_b", [4096], F32)
    q2_h = p.dram("fc2_q", [4096, D], I32)
    s2_h = p.dram("fc2_s", [32, D], F32)
    b2_h = p.dram("fc2_b", [D], F32)

    F = 4096
    NF = F // 128

    w1 = [p.sb([128, F], BF16, "w1") for _ in range(ND)]
    w2 = [p.sb([128, D], BF16, "w2") for _ in range(NF)]
    qs = p.sb([128, 2048], I32, "qscratch")
    qf = p.sb([128, F], F32, "qfscratch")
    srow = p.sb([1, 512], F32, "srow")
    prep = p.ps([128, 512], F32, "prep2")
    for g in range(ND):
        for h in range(2):
            p.dma(qs[:, :2048], q1_h[g * 128:(g + 1) * 128, h * 2048:(h + 1) * 2048])
            p.vcopy(qf[:, h * 2048:(h + 1) * 2048], qs[:, :2048])
        for j in range(F // 512):
            p.dma(srow[:, :512], s1_h[g:g + 1, j * 512:(j + 1) * 512])
            p.mm(prep[:, :], c["ones_sb"][:, :], srow[:, :512])
            p.stt(w1[g][:, j * 512:(j + 1) * 512], qf[:, j * 512:(j + 1) * 512],
                  -128.0, prep[:, :], OP.add, OP.mult)
    for g in range(F // 128):
        p.dma(qs[:, :D], q2_h[g * 128:(g + 1) * 128, :])
        p.vcopy(qf[:, :D], qs[:, :D])
        for j in range(D // 512):
            p.dma(srow[:, :512], s2_h[g:g + 1, j * 512:(j + 1) * 512])
            p.mm(prep[:, :], c["ones_sb"][:, :], srow[:, :512])
            p.stt(w2[g][:, j * 512:(j + 1) * 512], qf[:, j * 512:(j + 1) * 512],
                  -128.0, prep[:, :], OP.add, OP.mult)

    b1t = p.sb([128, NF], F32, "b1t")
    for ft in range(NF):
        p.dma(b1t[:, ft:ft + 1], b1_h[ft * 128:(ft + 1) * 128].rearrange("(p one) -> p one", one=1))
    b2rep = p.sb([128, D], F32, "b2rep")
    grep = p.sb([128, D], BF16, "grep")
    brep = p.sb([128, D], BF16, "brep")
    for (rep, h, sc) in ((b2rep, b2_h, 0.5), (grep, g_h, None), (brep, b_h, None)):
        for j in range(2):
            p.dma(srow[:, :512], h[j * 512:(j + 1) * 512].rearrange("(one n) -> one n", one=1))
            p.mm(prep[:, :], c["ones_sb"][:, :], srow[:, :512])
            if sc is None:
                p.scopy(rep[:, j * 512:(j + 1) * 512], prep[:, :])
            else:
                p.act(rep[:, j * 512:(j + 1) * 512], prep[:, :], AF.Copy, scale=sc)

    xt = qf[:, 0:1024]
    tmp = qf[:, 1024:2048]
    xres = qf[:, 2048:3072]
    ztmp = qf[:, 3072:3584]
    mcol = p.sb([128, 1], F32, "m")
    vcol = p.sb([128, 1], F32, "v")
    lnb = p.sb([128, D], BF16, "lnb")
    pbf = p.ps([128, 128], BF16, "ptr")
    Wc = 256
    ntc = Wc // 128
    lnT = [p.sb([128, Wc], BF16, "lnT") for _ in range(ND)]
    h1T = [p.sb([128, Wc], BF16, "h1T") for _ in range(NF)]
    tbuf = [p.sb([128, D], F32, "tbuf") for _ in range(ntc)]
    ph = p.ps([128, Wc], F32, "ph")
    pz = p.ps([128, 512], F32, "pz")

    for ch in range(T // Wc):
        for tt_ in range(ntc):
            t = ch * ntc + tt_
            p.dma(xt, x_h[t * 128:(t + 1) * 128, :])
            layernorm_tile(p, tbuf[tt_], xt, tmp, mcol, vcol)
            layernorm_tile(p, lnb, tbuf[tt_], tmp, mcol, vcol, gamma_rep=grep, beta_rep=brep)
            transpose_nat_to_T(p, c, lnT, lnb, tt_, pbf)
        for ft in range(NF):
            for k in range(ND):
                p.mm(ph[:, :], w1[k][:, ft * 128:(ft + 1) * 128], lnT[k][:, :],
                     start=(k == 0), stop=(k == ND - 1), wait=(k == 0))
            p.act(h1T[ft][:, :], ph[:, :], AF.Silu, bias=b1t[:, ft:ft + 1])
        for tt_ in range(ntc):
            t = ch * ntc + tt_
            p.dma(xres, x_h[t * 128:(t + 1) * 128, :])
            for dc in range(2):
                dcs = slice(dc * 512, (dc + 1) * 512)
                for k in range(NF):
                    p.mm(pz[:, :], h1T[k][:, tt_ * 128:(tt_ + 1) * 128], w2[k][:, dcs],
                         start=(k == 0), stop=(k == NF - 1), wait=(k == 0))
                p.stt(ztmp, pz[:, :], 0.5, b2rep[:, dcs], OP.mult, OP.add)
                p.tt(ztmp, ztmp, xres[:, dcs], OP.add)
                p.tt(ztmp, ztmp, tbuf[tt_][:, dcs], OP.add)
                p.dma(y_h[t * 128:(t + 1) * 128, dcs], ztmp)
    return p.build() if standalone else None



def bfly_build_M(p, c, uw_h, M_bf, wrow, wrep, X, Xn, t1, t2):
    uw_flat = uw_h.rearrange("s k a b -> s (k a b)")
    cur, nxt = X, Xn
    for s in range(7):
        s_ = 1 << s
        for q in range(4):
            p.dma(wrow[:, :], uw_flat[s:s + 1, q * 2048:(q + 1) * 2048])
            wv = wrow.rearrange("one (k a b) -> one k a b", a=2, b=2)
            for wi, (ai, bi) in enumerate(((0, 0), (0, 1), (1, 0), (1, 1))):
                p.mm(p._prep[:, :], c["ones_sb"][:, :], wv[:, :, ai, bi])
                p.scopy(wrep[wi][:, q * 512:(q + 1) * 512], p._prep[:, :])

        def av(buf):
            return buf.rearrange("p (nb two s) -> p nb two s", two=2, s=s_)
        a_c, b_c = av(cur)[:, :, 0, :], av(cur)[:, :, 1, :]
        a_n, b_n = av(nxt)[:, :, 0, :], av(nxt)[:, :, 1, :]
        wv3 = [w.rearrange("p (nb s) -> p nb s", s=s_) for w in wrep]
        t1v = t1.rearrange("p (nb s) -> p nb s", s=s_)
        t2v = t2.rearrange("p (nb s) -> p nb s", s=s_)
        p.tt(t1v, a_c, wv3[0], OP.mult)
        p.tt(t2v, b_c, wv3[1], OP.mult)
        p.tt(a_n, t1v, t2v, OP.add)
        p.tt(t1v, a_c, wv3[2], OP.mult)
        p.tt(t2v, b_c, wv3[3], OP.mult)
        p.tt(b_n, t1v, t2v, OP.add)
        cur, nxt = nxt, cur
    p.vcopy(M_bf[:, :], cur[:, :])


def bfly_high_cols(p, uw_h, dst):
    out = []
    for s in range(7, 12):
        cols = []
        npair = 4096 >> (s + 1)
        tstep = 1 << (s - 7)
        pairs = [ta for ta in range(32) if (ta // tstep) % 2 == 0]
        for pi, ta in enumerate(pairs):
            i0 = 128 * ta
            k0 = ((i0 >> (s + 1)) << s) | (i0 & ((1 << s) - 1))
            w4 = p.sb([128, 4], F32, "bw")
            p.dma(w4[:, :], uw_h[s:s + 1, k0:k0 + 128, :, :].rearrange("s k a b -> (s k) (a b)"))
            cols.append((ta, ta + tstep, w4))
        out.append(cols)
    return out


def bfly_high_apply(p, bufs_a, bufs_b, cols, W, tmp):
    cur, nxt = bufs_a, bufs_b
    for si in range(5):
        for (ta, tb, w4) in cols[si]:
            p.ts(tmp[:, :W], cur[tb][:, :W], w4[:, 1:2], OP.mult)
            p.stt(nxt[ta][:, :W], cur[ta][:, :W], w4[:, 0:1], tmp[:, :W], OP.mult, OP.add)
            p.ts(tmp[:, :W], cur[ta][:, :W], w4[:, 2:3], OP.mult)
            p.stt(nxt[tb][:, :W], cur[tb][:, :W], w4[:, 3:4], tmp[:, :W], OP.mult, OP.add)
        cur, nxt = nxt, cur
    return cur, nxt


def build_l0_ffn(p=None, c=None, xin=None, yout=None):
    standalone = p is None
    if standalone:
        p = Prog()
        c = load_consts(p)
    F = 4096
    NF = F // 128
    x_h = xin if xin is not None else p.dram("x", [T, D], F32)
    y_h = yout if yout is not None else p.dram("y", [T, D], F32, kind="ExternalOutput")
    wexp_h = p.dram("W_exp", [D, F], F32)
    bexp_h = p.dram("b_exp", [F], F32)
    uw_h = p.dram("up_w", [12, 2048, 2, 2], F32)
    ub_h = p.dram("up_b", [F], F32)
    dw_h = p.dram("down_w", [12, 2048, 2, 2], F32)
    wcon_h = p.dram("W_con", [F, D], F32)
    bcon_h = p.dram("b_con", [D], F32)
    db_h = p.dram("down_b", [D], F32)
    basis_h = p.dram("c_basis", [128, F], F32)

    p._prep = p.ps([128, 512], F32, "prep")

    Mup = p.sb([128, F], BF16, "Mup")
    Mdn = p.sb([128, F], BF16, "Mdn")
    wrow = p.sb([1, 2048], F32, "wrow", grp="bld")
    wrep = [p.sb([128, 2048], BF16, "wrep", grp="bld") for _ in range(4)]
    Xb = p.sb([128, F], F32, "Xb", grp="bld")
    Xb2 = p.sb([128, F], F32, "Xb2", grp="bld")
    t1 = p.sb([128, 2048], BF16, "bt1", grp="bld")
    t2 = p.sb([128, 2048], BF16, "bt2", grp="bld")
    p.dma(Xb[:, :], basis_h[:, :])
    bfly_build_M(p, c, uw_h, Mup, wrow, wrep, Xb, Xb2, t1, t2)
    p.dma(Xb[:, :], basis_h[:, :])
    bfly_build_M(p, c, dw_h, Mdn, wrow, wrep, Xb, Xb2, t1, t2)
    p.free_grp("bld")

    fscr = p.sb([128, 3584], F32, "fscr")
    wexp = [p.sb([128, F], BF16, "wexp") for _ in range(ND)]
    for k in range(ND):
        for hh in range(2):
            p.dma(fscr[:, :2048], wexp_h[k * 128:(k + 1) * 128, hh * 2048:(hh + 1) * 2048])
            p.vcopy(wexp[k][:, hh * 2048:(hh + 1) * 2048], fscr[:, :2048])
    wcon = [p.sb([128, D], BF16, "wcon") for _ in range(NF)]
    for k in range(NF):
        p.dma(fscr[:, :D], wcon_h[k * 128:(k + 1) * 128, :])
        p.vcopy(wcon[k][:, :], fscr[:, :D])

    upcols = bfly_high_cols(p, uw_h, None)
    dncols = bfly_high_cols(p, dw_h, None)

    bexp_c = p.sb([128, NF], F32, "bexpc")
    ub_c = p.sb([128, NF], F32, "ubc")
    for ft in range(NF):
        p.dma(bexp_c[:, ft:ft + 1], bexp_h[ft * 128:(ft + 1) * 128].rearrange("(p one) -> p one", one=1))
        p.dma(ub_c[:, ft:ft + 1], ub_h[ft * 128:(ft + 1) * 128].rearrange("(p one) -> p one", one=1))
    srow = p.sb([1, 512], F32, "srow")
    bc_rep = p.sb([128, D], BF16, "bcrep")
    for j in range(2):
        p.dma(srow[:, :], bcon_h[j * 512:(j + 1) * 512].rearrange("(one n) -> one n", one=1))
        p.mm(p._prep[:, :], c["ones_sb"][:, :], srow[:, :], start=True, stop=False)
        p.dma(srow[:, :], db_h[j * 512:(j + 1) * 512].rearrange("(one n) -> one n", one=1))
        p.mm(p._prep[:, :], c["ones_sb"][:, :], srow[:, :], start=False, stop=True)
        p.scopy(bc_rep[:, j * 512:(j + 1) * 512], p._prep[:, :])

    W = 256
    lnT = [p.sb([128, W], BF16, "lnT") for _ in range(ND)]
    bufA = [p.sb([128, W], BF16, "bufA") for _ in range(NF)]
    bufB = [p.sb([128, W], BF16, "bufB") for _ in range(NF)]
    htmp = p.sb([128, W], BF16, "htmp")
    xt = fscr[:, 0:1024]
    tmp = fscr[:, 1024:2048]
    xres = fscr[:, 2048:3072]
    ztmp = fscr[:, 3072:3584]
    mcol = p.sb([128, 1], F32, "m")
    vcol = p.sb([128, 1], F32, "v")
    lnb = p.sb([128, D], BF16, "lnb")
    pbf = p.ps([128, 128], BF16, "ptr")
    ph = p.ps([128, W], F32, "ph")
    pz = p.ps([128, 512], F32, "pz")

    for ch in range(T // W):
        for tt_ in range(W // 128):
            t = ch * (W // 128) + tt_
            p.dma(xt, x_h[t * 128:(t + 1) * 128, :])
            layernorm_tile(p, lnb, xt, tmp, mcol, vcol)
            transpose_nat_to_T(p, c, lnT, lnb, tt_, pbf)
        for ft in range(NF):
            for k in range(ND):
                p.mm(ph[:, :], wexp[k][:, ft * 128:(ft + 1) * 128], lnT[k][:, :],
                     start=(k == 0), stop=(k == ND - 1), wait=(k == 0))
            p.ts(bufB[ft][:, :], ph[:, :], bexp_c[:, ft:ft + 1], OP.add)
        for ft in range(NF):
            p.mm(ph[:, :], Mup[:, ft * 128:(ft + 1) * 128], bufB[ft][:, :])
            p.scopy(bufA[ft][:, :], ph[:, :])
        cur, nxt = bfly_high_apply(p, bufA, bufB, upcols, W, htmp)
        for ft in range(NF):
            p.act(nxt[ft][:, :], cur[ft][:, :], AF.Gelu, bias=ub_c[:, ft:ft + 1])
        cur, nxt = nxt, cur
        for ft in range(NF):
            p.mm(ph[:, :], Mdn[:, ft * 128:(ft + 1) * 128], cur[ft][:, :])
            p.scopy(nxt[ft][:, :], ph[:, :])
        cur, nxt = bfly_high_apply(p, nxt, cur, dncols, W, htmp)
        for tt_ in range(W // 128):
            t = ch * (W // 128) + tt_
            p.dma(xres, x_h[t * 128:(t + 1) * 128, :])
            for dc in range(2):
                dcs = slice(dc * 512, (dc + 1) * 512)
                for k in range(NF):
                    p.mm(pz[:, :], cur[k][:, tt_ * 128:(tt_ + 1) * 128], wcon[k][:, dcs],
                         start=(k == 0), stop=(k == NF - 1), wait=(k == 0))
                p.tt(ztmp, pz[:, :], bc_rep[:, dcs], OP.add)
                p.tt(ztmp, ztmp, xres[:, dcs], OP.add)
                p.dma(y_h[t * 128:(t + 1) * 128, dcs], ztmp)
    return p.build() if standalone else None


def run_l0_ffn(x, inp):
    nc = get_prog("l0_ffn", build_l0_ffn)
    cmn = const_inputs()
    basis = np.zeros((128, 4096), np.float32)
    for j in range(128):
        basis[j, j::128] = 1.0
    maps = []
    for i in range(NCORE):
        m = dict(cmn)
        m["c_basis"] = basis
        m["x"] = np.ascontiguousarray(x[i])
        m["W_exp"] = inp["L0_ffn_W_exp"]
        m["b_exp"] = inp["L0_ffn_b_exp"]
        m["up_w"] = inp["L0_ffn_up_weights"]
        m["up_b"] = inp["L0_ffn_up_bias"]
        m["down_w"] = inp["L0_ffn_down_weights"]
        m["W_con"] = inp["L0_ffn_W_con"]
        m["b_con"] = inp["L0_ffn_b_con"]
        m["down_b"] = inp["L0_ffn_down_bias"]
        maps.append(m)
    res = run_bass_kernel_spmd(nc, maps, core_ids=list(range(NCORE)))
    return np.stack([r["y"] for r in res.results], axis=0)



def load_w_bf16(p, dst_tiles, w_h, rows, cols, fscr):
    for k in range(rows // 128):
        for hh in range(0, cols, 2048):
            wdt = min(2048, cols - hh)
            p.dma(fscr[:, :wdt], w_h[k * 128:(k + 1) * 128, hh:hh + wdt])
            p.vcopy(dst_tiles[k][:, hh:hh + wdt], fscr[:, :wdt])


def load_w_bf16_T(p, c, dst_tiles, w_h, n, fscr, wbscr, pbf):
    nt = n // 128
    for k in range(nt):
        p.dma(fscr[:, :n], w_h[k * 128:(k + 1) * 128, :])
        p.vcopy(wbscr[:, :n], fscr[:, :n])
        for j in range(nt):
            p.trans(pbf[:, :128], wbscr[:, j * 128:(j + 1) * 128], c["id_sb"][:, :])
            p.vcopy(dst_tiles[j][:, k * 128:(k + 1) * 128], pbf[:, :128])


def build_attn(kind, p=None, c=None, xin=None, yout=None):
    standalone = p is None
    if standalone:
        p = Prog()
        c = load_consts(p)
    x_h = xin if xin is not None else p.dram("x", [T, D], F32)
    y_h = yout if yout is not None else p.dram("y", [T, D], F32, kind="ExternalOutput")
    H = 16
    HD = 64
    VA = HD + 1
    qT_hbm = p.dram("sc_qT", [D, T], BF16, kind="Internal")
    kT_hbm = p.dram("sc_kT", [D, T], BF16, kind="Internal")
    v_hbm = p.dram("sc_v", [T, H * VA], BF16, kind="Internal")
    if kind == "mla":
        wdkv_h = p.dram("W_DKV", [D, 256], F32)
        wuk_h = p.dram("W_UK", [256, D], F32)
        wuv_h = p.dram("W_UV", [256, D], F32)
        wuq_h = p.dram("W_UQ", [D, D], F32)
        wo_h = p.dram("W_O", [D, D], F32)
        mask_h = p.dram("c_diag", [128, 128], BF16)
    else:
        g_h = p.dram("ln_g", [D], F32)
        b_h = p.dram("ln_b", [D], F32)
        wq_h = p.dram("Wq", [D, D], F32)
        wk_h = p.dram("Wk", [D, D], F32)
        wv_h = p.dram("Wv", [D, D], F32)
        wo_h = p.dram("Wo", [D, D], F32)
        bq_h = p.dram("bq", [D], F32)
        bk_h = p.dram("bk", [D], F32)
        bv_h = p.dram("bv", [D], F32)
        bo_h = p.dram("bo", [D], F32)
        mu_h = p.dram("mu", [1, 1], F32)

    p._prep = p.ps([128, 512], F32, "prep")
    onat = [p.sb([128, D], BF16, "onat") for _ in range(NT)]
    wo = [p.sb([128, D], BF16, "wo") for _ in range(ND)]

    fscr = p.sb([128, 2048], F32, "fscr", grp="p1")
    if kind == "mla":
        wbscr = None
        wuq = [p.sb([128, D], BF16, "wuq", grp="p1") for _ in range(ND)]
        wdkv = [p.sb([128, 256], BF16, "wdkv", grp="p1") for _ in range(ND)]
        wuk = [p.sb([128, D], BF16, "wuk", grp="p1") for _ in range(2)]
        wuv = [p.sb([128, D], BF16, "wuv", grp="p1") for _ in range(2)]
        load_w_bf16(p, wuq, wuq_h, D, D, fscr)
        load_w_bf16(p, wdkv, wdkv_h, D, 256, fscr)
        load_w_bf16(p, wuk, wuk_h, 256, D, fscr)
        load_w_bf16(p, wuv, wuv_h, 256, D, fscr)
        load_w_bf16(p, wo, wo_h, D, D, fscr)
        ckvT = [p.sb([128, T], BF16, "ckvT", grp="p1") for _ in range(2)]
        grep = brep = None
        qb = kb = vb = None
    else:
        pbf0 = p.ps([128, 128], BF16, "ptr0", grp="p1")
        wbscr = p.sb([128, D], BF16, "wbscr", grp="p1")
        wqT = [p.sb([128, D], BF16, "wqT", grp="p1") for _ in range(ND)]
        wkT = [p.sb([128, D], BF16, "wkT", grp="p1") for _ in range(ND)]
        wvT = [p.sb([128, D], BF16, "wvT", grp="p1") for _ in range(ND)]
        load_w_bf16_T(p, c, wqT, wq_h, D, fscr, wbscr, pbf0)
        load_w_bf16_T(p, c, wkT, wk_h, D, fscr, wbscr, pbf0)
        load_w_bf16_T(p, c, wvT, wv_h, D, fscr, wbscr, pbf0)
        load_w_bf16_T(p, c, wo, wo_h, D, fscr, wbscr, pbf0)
        qb = p.sb([128, ND], F32, "qb", grp="p1")
        kb = p.sb([128, ND], F32, "kb", grp="p1")
        for ftl in range(ND):
            p.dma(qb[:, ftl:ftl + 1], bq_h[ftl * 128:(ftl + 1) * 128].rearrange("(p one) -> p one", one=1))
            p.dma(kb[:, ftl:ftl + 1], bk_h[ftl * 128:(ftl + 1) * 128].rearrange("(p one) -> p one", one=1))
        grep = p.sb([128, D], BF16, "grep", grp="p1")
        brep = p.sb([128, D], BF16, "brep", grp="p1")
        bvrep = p.sb([128, D], F32, "bvrep", grp="p1")
        srow = p.sb([1, 512], F32, "srow", grp="p1")
        for (rep, hh) in ((grep, g_h), (brep, b_h), (bvrep, bv_h)):
            for j in range(2):
                p.dma(srow[:, :], hh[j * 512:(j + 1) * 512].rearrange("(one n) -> one n", one=1))
                p.mm(p._prep[:, :], c["ones_sb"][:, :], srow[:, :])
                p.scopy(rep[:, j * 512:(j + 1) * 512], p._prep[:, :])

    lnT = [p.sb([128, 512], BF16, "lnTc", grp="p1") for _ in range(ND)]
    xt = p.sb([128, D], F32, "xt", grp="p1")
    tmp = p.sb([128, D], F32, "lntmp", grp="p1")
    t32 = p.sb([128, D], F32, "t32", grp="p1")
    mcol = p.sb([128, 1], F32, "m", grp="p1")
    vcol = p.sb([128, 1], F32, "v", grp="p1")
    lnb = p.sb([128, D], BF16, "lnb", grp="p1")
    pbf = p.ps([128, 128], BF16, "ptr", grp="p1")
    ph = p.ps([128, 512], F32, "ph", grp="p1")
    ev = p.sb([128, 512], BF16, "ev", grp="p1")
    vbuf = p.sb([128, H * VA], BF16, "vbuf", grp="p1")

    for ch in range(8):
        tok = slice(ch * 512, (ch + 1) * 512)
        for tt_ in range(4):
            t = ch * 4 + tt_
            p.dma(xt[:, :], x_h[t * 128:(t + 1) * 128, :])
            if kind == "mla":
                layernorm_tile(p, lnb, xt, tmp, mcol, vcol)
            else:
                layernorm_tile(p, t32, xt, tmp, mcol, vcol)
                layernorm_tile(p, lnb, t32, tmp, mcol, vcol,
                               gamma_rep=grep, beta_rep=brep, eps=1e-6)
            transpose_nat_to_T(p, c, lnT, lnb, tt_, pbf)
        if kind == "mla":
            for ntl in range(ND):
                for k in range(ND):
                    p.mm(ph[:, :], wuq[k][:, ntl * 128:(ntl + 1) * 128], lnT[k][:, :],
                         start=(k == 0), stop=(k == ND - 1), wait=(k == 0))
                p.scopy(ev[:, :], ph[:, :])
                p.dma(qT_hbm[ntl * 128:(ntl + 1) * 128, tok], ev[:, :])
            for ntl in range(2):
                for k in range(ND):
                    p.mm(ph[:, :], wdkv[k][:, ntl * 128:(ntl + 1) * 128], lnT[k][:, :],
                         start=(k == 0), stop=(k == ND - 1), wait=(k == 0))
                p.scopy(ckvT[ntl][:, tok], ph[:, :])
        else:
            for (wt, bias, dst_h) in ((wqT, qb, qT_hbm), (wkT, kb, kT_hbm)):
                for ntl in range(ND):
                    for k in range(ND):
                        p.mm(ph[:, :], wt[k][:, ntl * 128:(ntl + 1) * 128], lnT[k][:, :],
                             start=(k == 0), stop=(k == ND - 1), wait=(k == 0))
                    p.ts(ev[:, :], ph[:, :], bias[:, ntl:ntl + 1], OP.add)
                    p.dma(dst_h[ntl * 128:(ntl + 1) * 128, tok], ev[:, :])
            for tt_ in range(4):
                t = ch * 4 + tt_
                for nc2 in range(2):
                    for k in range(ND):
                        p.mm(ph[:, :], lnT[k][:, tt_ * 128:(tt_ + 1) * 128],
                             wvT[k][:, nc2 * 512:(nc2 + 1) * 512],
                             start=(k == 0), stop=(k == ND - 1), wait=(k == 0))
                    for h8 in range(8):
                        hd_ = nc2 * 8 + h8
                        p.tt(vbuf[:, hd_ * VA:hd_ * VA + HD], ph[:, h8 * 64:(h8 + 1) * 64],
                             bvrep[:, hd_ * 64:(hd_ + 1) * 64], OP.add)
                p.memset(vbuf.rearrange("p (h va) -> p h va", va=VA)[:, :, HD:HD + 1], 1.0)
                p.dma(v_hbm[t * 128:(t + 1) * 128, :], vbuf[:, :])
    if kind == "mla":
        for ntl in range(ND):
            for tc in range(8):
                for k in range(2):
                    p.mm(ph[:, :], wuk[k][:, ntl * 128:(ntl + 1) * 128],
                         ckvT[k][:, tc * 512:(tc + 1) * 512],
                         start=(k == 0), stop=(k == 1), wait=(k == 0))
                p.scopy(ev[:, :], ph[:, :])
                p.dma(kT_hbm[ntl * 128:(ntl + 1) * 128, tc * 512:(tc + 1) * 512], ev[:, :])
        for t in range(NT):
            for nc2 in range(2):
                for k in range(2):
                    p.mm(ph[:, :], ckvT[k][:, t * 128:(t + 1) * 128],
                         wuv[k][:, nc2 * 512:(nc2 + 1) * 512],
                         start=(k == 0), stop=(k == 1), wait=(k == 0))
                for h8 in range(8):
                    hd_ = nc2 * 8 + h8
                    p.vcopy(vbuf[:, hd_ * VA:hd_ * VA + HD], ph[:, h8 * 64:(h8 + 1) * 64])
            p.memset(vbuf.rearrange("p (h va) -> p h va", va=VA)[:, :, HD:HD + 1], 1.0)
            p.dma(v_hbm[t * 128:(t + 1) * 128, :], vbuf[:, :])
    p.free_grp("p1")

    kTh = p.sb([64, T], BF16, "kTh", grp="p2")
    qTh = p.sb([64, T], BF16, "qTh", grp="p2")
    vth = [p.sb([128, VA], BF16, "vth", grp="p2") for _ in range(NT)]
    PTs = p.sb([128, 512], BF16, "PTs", grp="p2")
    pS = p.ps([128, 512], F32, "pS", grp="p2")
    lrec = p.sb([128, 1], F32, "lrec", grp="p2")
    if kind == "mla":
        mask_sb = p.sb([128, 128], BF16, "msk", grp="p2")
        p.dma(mask_sb[:, :], mask_h[:, :])
        po = [p.ps([128, VA], F32, "po", grp="p2") for _ in range(4)]
        scale = 0.125
    else:
        macol = p.sb([128, 1], F32, "macol", grp="p2")
        murow = p.sb([1, 1], F32, "murow", grp="p2")
        p.dma(murow[:, :], mu_h[:, :])
        p.mm(p._prep[:, 0:1], c["ones_sb"][:, :], murow[:, :])
        p.act(macol[:, :], p._prep[:, 0:1], AF.Copy, scale=-1.0)
        sig = p.sb([128, 512], BF16, "sig", grp="p2")
        pot = p.ps([65, 512], F32, "pot", grp="p2")
        otu = p.sb([65, 512], BF16, "otu", grp="p2")
        pon = p.ps([128, VA], BF16, "pon", grp="p2")
        scale = 0.125

    for h in range(H):
        hrow = (h % 2) * 64
        ktile = h // 2
        p.dma(kTh[:, :], kT_hbm[ktile * 128 + hrow:ktile * 128 + hrow + 64, :])
        p.dma(qTh[:, :], qT_hbm[ktile * 128 + hrow:ktile * 128 + hrow + 64, :])
        for t in range(NT):
            p.dma(vth[t][:, :], v_hbm[t * 128:(t + 1) * 128, h * VA:(h + 1) * VA])
        for ic in range(8):
            njt = 4 * ic + 4 if kind == "mla" else NT
            if kind == "mla":
                for jt in range(njt):
                    p.mm(pS[:, :], kTh[:, jt * 128:(jt + 1) * 128], qTh[:, ic * 512:(ic + 1) * 512], sgc=True)
                    p.act(PTs[:, :], pS[:, :], AF.Exp, scale=scale)
                    if jt >= 4 * ic:
                        it4 = jt - 4 * ic
                        p.tt(PTs[:, it4 * 128:(it4 + 1) * 128], PTs[:, it4 * 128:(it4 + 1) * 128],
                             mask_sb[:, :], OP.mult)
                    first_pv = True
                    for it4 in range(4):
                        itile = 4 * ic + it4
                        if jt <= itile:
                            p.mm(po[it4][:, :], PTs[:, it4 * 128:(it4 + 1) * 128], vth[jt][:, :],
                                 start=(jt == 0), stop=(jt == itile), wait=first_pv, sgc=True)
                            first_pv = False
                for it4 in range(4):
                    itile = 4 * ic + it4
                    p.recip(lrec[:, :], po[it4][:, HD:HD + 1])
                    p.ts(onat[itile][:, h * HD:(h + 1) * HD], po[it4][:, 0:HD], lrec[:, 0:1], OP.mult)
            else:
                for jt in range(njt):
                    p.mm(pS[:, :], kTh[:, jt * 128:(jt + 1) * 128], qTh[:, ic * 512:(ic + 1) * 512], sgc=True)
                    p.act(PTs[:, :], pS[:, :], AF.Exp, scale=scale)
                    p.act(sig[:, :], pS[:, :], AF.Sigmoid, bias=macol[:, 0:1], scale=scale)
                    p.tt(PTs[:, :], PTs[:, :], sig[:, :], OP.mult)
                    p.mm(pot[:, :], vth[jt][:, :], PTs[:, :],
                         start=(jt == 0), stop=(jt == njt - 1), wait=True, sgc=True)
                p.vcopy(otu[:, :], pot[:, :])
                for it4 in range(4):
                    itile = 4 * ic + it4
                    p.trans(pon[:, :], otu[:, it4 * 128:(it4 + 1) * 128], c["id_sb"][0:VA, 0:VA])
                    p.recip(lrec[:, :], pon[0:128, HD:HD + 1])
                    p.ts(onat[itile][:, h * HD:(h + 1) * HD], pon[0:128, 0:HD], lrec[:, 0:1], OP.mult)
    p.free_grp("p2")

    xres = p.sb([128, D], F32, "xres", grp="p3")
    ztmp = p.sb([128, 512], F32, "zt3", grp="p3")
    otp = p.sb([128, 128], BF16, "otp", grp="p3")
    pbf3 = p.ps([128, 128], BF16, "ptr3", grp="p3")
    py = [p.ps([128, 512], F32, "py", grp="p3") for _ in range(2)]
    borep = None
    if kind != "mla":
        borep = p.sb([128, D], F32, "borep", grp="p3")
        srow3 = p.sb([1, 512], F32, "srow3", grp="p3")
        for j in range(2):
            p.dma(srow3[:, :], bo_h[j * 512:(j + 1) * 512].rearrange("(one n) -> one n", one=1))
            p.mm(p._prep[:, :], c["ones_sb"][:, :], srow3[:, :])
            p.scopy(borep[:, j * 512:(j + 1) * 512], p._prep[:, :])
    for t in range(NT):
        p.dma(xres[:, :], x_h[t * 128:(t + 1) * 128, :])
        for hv in range(ND):
            p.trans(pbf3[:, :], onat[t][:, hv * 128:(hv + 1) * 128], c["id_sb"][:, :])
            p.vcopy(otp[:, :], pbf3[:, :])
            for dc in range(2):
                p.mm(py[dc][:, :], otp[:, :], wo[hv][:, dc * 512:(dc + 1) * 512],
                     start=(hv == 0), stop=(hv == ND - 1), wait=(dc == 0), sgc=True)
        for dc in range(2):
            dcs = slice(dc * 512, (dc + 1) * 512)
            if borep is not None:
                p.tt(ztmp[:, :], py[dc][:, :], borep[:, dcs], OP.add)
                p.tt(ztmp[:, :], ztmp[:, :], xres[:, dcs], OP.add)
            else:
                p.tt(ztmp[:, :], py[dc][:, :], xres[:, dcs], OP.add)
            p.dma(y_h[t * 128:(t + 1) * 128, dcs], ztmp[:, :])
    return p.build() if standalone else None


def run_attn(kind, x, inp):
    nc = get_prog("attn_" + kind, lambda: build_attn(kind))
    cmn = const_inputs()
    maps = []
    for i in range(NCORE):
        m = dict(cmn)
        m["x"] = np.ascontiguousarray(x[i])
        if kind == "mla":
            jj = np.arange(128)
            m["c_diag"] = (jj[:, None] <= jj[None, :]).astype(bf16_np())
            m["W_DKV"] = inp["L1_attn_a_W_DKV"]
            m["W_UK"] = inp["L1_attn_a_W_UK"]
            m["W_UV"] = inp["L1_attn_a_W_UV"]
            m["W_UQ"] = inp["L1_attn_a_W_UQ"]
            m["W_O"] = inp["L1_attn_a_W_O"]
        else:
            m["ln_g"] = inp["L1_attn_b_ln_gamma"]
            m["ln_b"] = inp["L1_attn_b_ln_beta"]
            m["Wq"] = inp["L1_attn_b_Wq"]
            m["Wk"] = inp["L1_attn_b_Wk"]
            m["Wv"] = inp["L1_attn_b_Wv"]
            m["Wo"] = inp["L1_attn_b_Wo"]
            m["bq"] = inp["L1_attn_b_bq"]
            m["bk"] = inp["L1_attn_b_bk"]
            m["bv"] = inp["L1_attn_b_bv"]
            m["bo"] = inp["L1_attn_b_bo"]
            m["mu"] = np.asarray(inp["L1_attn_b_mu"], np.float32).reshape(1, 1)
        maps.append(m)
    res = run_bass_kernel_spmd(nc, maps, core_ids=list(range(NCORE)))
    return np.stack([r["y"] for r in res.results], axis=0)



def build_l0_attn_b(p=None, c=None, xin=None, yout=None):
    standalone = p is None
    if standalone:
        p = Prog()
        c = load_consts(p)
    x_h = xin if xin is not None else p.dram("x", [T, D], F32)
    y_h = yout if yout is not None else p.dram("y", [T, D], F32, kind="ExternalOutput")
    tmk_h = p.dram("tmk", [1, 1, D], F32)
    tmv_h = p.dram("tmv", [1, 1, D], F32)
    tmr_h = p.dram("tmr", [1, 1, D], F32)
    tmg_h = p.dram("tmg", [1, 1, D], F32)
    td_h = p.dram("td", [16], F32)
    tf_h = p.dram("tf", [16], F32)
    wr_h = p.dram("W_r", [D, D], F32)
    wk_h = p.dram("W_k", [D, D], F32)
    wv_h = p.dram("W_v", [D, D], F32)
    wg_h = p.dram("W_g", [D, D], F32)
    wo_h = p.dram("W_o", [D, D], F32)
    lnw_h = p.dram("ln_w", [D], F32)
    lnb_h = p.dram("ln_b", [D], F32)
    e1_h = p.dram("c_e1", [128, 128], F32)
    su_h = p.dram("c_su", [128, 128], F32)
    eye_h = p.dram("c_eye", [128, 128], F32)
    iota_h = p.dram("c_iota", [128, 2], F32)

    rT_hbm = p.dram("sc_rT", [D, T], BF16, kind="Internal")
    kT_hbm = p.dram("sc_kT", [D, T], BF16, kind="Internal")
    kn_hbm = p.dram("sc_kn", [T, D], BF16, kind="Internal")
    vn_hbm = p.dram("sc_vn", [T, D], BF16, kind="Internal")
    gn_hbm = p.dram("sc_gn", [T, D], BF16, kind="Internal")

    p._prep = p.ps([128, 512], F32, "prep")
    H = 16
    K = 64

    ynat = [p.sb([128, D], BF16, "ynat") for _ in range(NT)]
    woT = [p.sb([128, D], BF16, "woT") for _ in range(ND)]
    lnwrep = p.sb([128, D], F32, "lnwrep")
    lnbrep = p.sb([128, D], F32, "lnbrep")
    srow0 = p.sb([1, 512], F32, "srow0")
    for (rep, hh) in ((lnwrep, lnw_h), (lnbrep, lnb_h)):
        for j in range(2):
            p.dma(srow0[:, :], hh[j * 512:(j + 1) * 512].rearrange("(one n) -> one n", one=1))
            p.mm(p._prep[:, :], c["ones_sb"][:, :], srow0[:, :])
            p.scopy(rep[:, j * 512:(j + 1) * 512], p._prep[:, :])

    fscr = p.sb([128, D], F32, "fscr", grp="p1")
    wbscr = p.sb([128, D], BF16, "wbscr", grp="p1")
    pbf0 = p.ps([128, 128], BF16, "ptr0", grp="p1")
    wrT = [p.sb([128, D], BF16, "wrT", grp="p1") for _ in range(ND)]
    wkT = [p.sb([128, D], BF16, "wkT", grp="p1") for _ in range(ND)]
    wvT = [p.sb([128, D], BF16, "wvT", grp="p1") for _ in range(ND)]
    wgT = [p.sb([128, D], BF16, "wgT", grp="p1") for _ in range(ND)]
    load_w_bf16_T(p, c, wrT, wr_h, D, fscr, wbscr, pbf0)
    load_w_bf16_T(p, c, wkT, wk_h, D, fscr, wbscr, pbf0)
    load_w_bf16_T(p, c, wvT, wv_h, D, fscr, wbscr, pbf0)
    load_w_bf16_T(p, c, wgT, wg_h, D, fscr, wbscr, pbf0)
    load_w_bf16_T(p, c, woT, wo_h, D, fscr, wbscr, pbf0)
    tmcols = {}
    for nm, hh in (("r", tmr_h), ("k", tmk_h), ("v", tmv_h), ("g", tmg_h)):
        col = p.sb([128, ND], F32, "tmc", grp="p1")
        for ftl in range(ND):
            p.dma(col[:, ftl:ftl + 1],
                  hh.rearrange("a b d -> (a b) d")[0:1, ftl * 128:(ftl + 1) * 128]
                  .rearrange("one (pp o) -> (one pp) o", o=1))
        tmcols[nm] = col

    lnT = [p.sb([128, 512], BF16, "lnTc", grp="p1") for _ in range(ND)]
    xmT = [p.sb([128, 512], BF16, "xmT", grp="p1") for _ in range(ND)]
    dcol = p.sb([128, 512], F32, "dcol", grp="p1")
    pcol = p.sb([128, ND], BF16, "pcol", grp="p1")
    xt = p.sb([128, D], F32, "xt", grp="p1")
    tmp = p.sb([128, D], F32, "lntmp", grp="p1")
    mcol = p.sb([128, 1], F32, "m", grp="p1")
    vcol = p.sb([128, 1], F32, "v", grp="p1")
    lnb2 = p.sb([128, D], BF16, "lnb2", grp="p1")
    pbf = p.ps([128, 128], BF16, "ptr", grp="p1")
    ph = p.ps([128, 512], F32, "ph", grp="p1")
    ev = p.sb([128, 512], BF16, "ev", grp="p1")
    p.memset(pcol[:, :], 0.0)

    for ch in range(8):
        tok = slice(ch * 512, (ch + 1) * 512)
        for tt_ in range(4):
            t = ch * 4 + tt_
            p.dma(xt[:, :], x_h[t * 128:(t + 1) * 128, :])
            layernorm_tile(p, lnb2, xt, tmp, mcol, vcol)
            transpose_nat_to_T(p, c, lnT, lnb2, tt_, pbf)
        for nm, dsts in (("r", ("rT",)), ("k", ("kT", "kn")), ("v", ("vn",)), ("g", ("gn",))):
            tmc = tmcols[nm]
            for k in range(ND):
                p.tt(dcol[:, 1:512], lnT[k][:, 1:512], lnT[k][:, 0:511], OP.subtract)
                p.stt(xmT[k][:, 1:512], dcol[:, 1:512], tmc[:, k:k + 1], lnT[k][:, 0:511],
                      OP.mult, OP.add)
                p.tt(dcol[:, 0:1], lnT[k][:, 0:1], pcol[:, k:k + 1], OP.subtract)
                p.stt(xmT[k][:, 0:1], dcol[:, 0:1], tmc[:, k:k + 1], pcol[:, k:k + 1],
                      OP.mult, OP.add)
            if nm == "g":
                for k in range(ND):
                    p.vcopy(pcol[:, k:k + 1], lnT[k][:, 511:512])
            wT = {"r": wrT, "k": wkT, "v": wvT, "g": wgT}[nm]
            for d in dsts:
                if d in ("rT", "kT"):
                    hbm = rT_hbm if d == "rT" else kT_hbm
                    for ntl in range(ND):
                        for k in range(ND):
                            p.mm(ph[:, :], wT[k][:, ntl * 128:(ntl + 1) * 128], xmT[k][:, :],
                                 start=(k == 0), stop=(k == ND - 1), wait=(k == 0))
                        p.scopy(ev[:, :], ph[:, :])
                        p.dma(hbm[ntl * 128:(ntl + 1) * 128, tok], ev[:, :])
                else:
                    hbm = {"kn": kn_hbm, "vn": vn_hbm, "gn": gn_hbm}[d]
                    for tt_ in range(4):
                        t = ch * 4 + tt_
                        for nc2 in range(2):
                            for k in range(ND):
                                p.mm(ph[:, :], xmT[k][:, tt_ * 128:(tt_ + 1) * 128],
                                     wT[k][:, nc2 * 512:(nc2 + 1) * 512],
                                     start=(k == 0), stop=(k == ND - 1), wait=(k == 0))
                            if d == "gn":
                                p.act(ev[:, :], ph[:, :], AF.Silu)
                            else:
                                p.scopy(ev[:, :], ph[:, :])
                            p.dma(hbm[t * 128:(t + 1) * 128, nc2 * 512:(nc2 + 1) * 512], ev[:, :])
    p.free_grp("p1")

    e1c = p.sb([128, 128], F32, "e1c", grp="p2")
    suc = p.sb([128, 128], F32, "suc", grp="p2")
    eyec = p.sb([128, 128], F32, "eyec", grp="p2")
    iotac = p.sb([128, 2], F32, "iotac", grp="p2")
    p.dma(e1c[:, :], e1_h[:, :])
    p.dma(suc[:, :], su_h[:, :])
    p.dma(eyec[:, :], eye_h[:, :])
    p.dma(iotac[:, :], iota_h[:, :])
    tdrow = p.sb([1, 16], F32, "tdrow", grp="p2")
    tfrow = p.sb([1, 16], F32, "tfrow", grp="p2")
    p.dma(tdrow[:, :], td_h.rearrange("(one h) -> one h", one=1))
    p.dma(tfrow[:, :], tf_h.rearrange("(one h) -> one h", one=1))
    ldrep = p.sb([128, 16], F32, "ldrep", grp="p2")
    urep = p.sb([128, 16], F32, "urep", grp="p2")
    p.mm(p._prep[:, 0:16], c["ones_sb"][:, :], tdrow[:, :])
    p.act(ldrep[:, :], p._prep[:, 0:16], AF.Exp)
    p.ts(ldrep[:, :], ldrep[:, :], -1.0, OP.mult)
    p.mm(p._prep[:, 0:16], c["ones_sb"][:, :], tfrow[:, :])
    p.scopy(urep[:, :], p._prep[:, 0:16])

    wmT = p.sb([128, 128], BF16, "wmT", grp="p2")
    wmt1 = p.sb([128, 128], F32, "wmt1", grp="p2")
    wbcol = p.sb([128, 1], F32, "wbcol", grp="p2")
    wkcol = p.sb([128, 1], F32, "wkcol", grp="p2")
    wscol = p.sb([64, 1], F32, "wscol", grp="p2")
    s_st = p.sb([64, 128], F32, "s_st", grp="p2")
    s_bf = p.sb([64, K], BF16, "s_bf", grp="p2")
    kwk = p.sb([128, K], BF16, "kwk", grp="p2")
    PTc = p.sb([128, 128], BF16, "PTc", grp="p2")
    y1 = p.sb([128, K], F32, "y1", grp="p2")
    y2 = p.sb([128, K], F32, "y2", grp="p2")
    gm = p.sb([128, 1], F32, "gm", grp="p2")
    gv = p.sb([128, 1], F32, "gv", grp="p2")
    rTh = p.sb([64, T], BF16, "rTh", grp="p2")
    kTh = p.sb([64, T], BF16, "kTh", grp="p2")
    knh = [p.sb([128, K], BF16, "knh", grp="p2") for _ in range(NT)]
    vnh = [p.sb([128, K], BF16, "vnh", grp="p2") for _ in range(NT)]
    pA = p.ps([128, 128], F32, "pA", grp="p2")
    pO = p.ps([128, K], F32, "pO", grp="p2")
    pC = p.ps([128, K], F32, "pC", grp="p2")
    pS3 = p.ps([64, K], F32, "pS3", grp="p2")

    for h in range(H):
        hrow = (h % 2) * 64
        ktile = h // 2
        p.dma(rTh[:, :], rT_hbm[ktile * 128 + hrow:ktile * 128 + hrow + 64, :])
        p.dma(kTh[:, :], kT_hbm[ktile * 128 + hrow:ktile * 128 + hrow + 64, :])
        for t in range(NT):
            p.dma(knh[t][:, :], kn_hbm[t * 128:(t + 1) * 128, h * K:(h + 1) * K])
            p.dma(vnh[t][:, :], vn_hbm[t * 128:(t + 1) * 128, h * K:(h + 1) * K])
        ldc = ldrep[:, h:h + 1]
        p.ts(wmt1[:, :], e1c[:, :], ldc, OP.mult)
        p.act(wmt1[:, :], wmt1[:, :], AF.Exp)
        p.tt(wmt1[:, :], wmt1[:, :], suc[:, :], OP.mult)
        p.stt(wmT[:, :], eyec[:, :], urep[:, h:h + 1], wmt1[:, :], OP.mult, OP.add)
        p.ts(wbcol[:, :], iotac[:, 0:1], ldc, OP.mult)
        p.act(wbcol[:, :], wbcol[:, :], AF.Exp)
        p.ts(wkcol[:, :], iotac[:, 1:2], ldc, OP.mult)
        p.act(wkcol[:, :], wkcol[:, :], AF.Exp)
        p.act(wscol[:, :], ldrep[0:64, h:h + 1], AF.Exp, scale=128.0)
        p.memset(s_st[:, 0:K], 0.0)
        p.memset(s_bf[:, :], 0.0)
        for t in range(NT):
            p.mm(pA[:, :], kTh[:, t * 128:(t + 1) * 128], rTh[:, t * 128:(t + 1) * 128], sgc=True)
            p.tt(PTc[:, :], pA[:, :], wmT[:, :], OP.mult)
            p.mm(pO[:, :], PTc[:, :], vnh[t][:, :], sgc=True)
            p.mm(pC[:, :], rTh[:, t * 128:(t + 1) * 128], s_bf[:, :], sgc=True)
            p.ts(y1[:, :], pC[:, :], wbcol[:, 0:1], OP.mult)
            p.tt(y1[:, :], y1[:, :], pO[:, :], OP.add)
            p.ts(kwk[:, :], knh[t][:, :], wkcol[:, 0:1], OP.mult)
            p.mm(pS3[:, :], kwk[:, :], vnh[t][:, :], sgc=True)
            p.ts(s_st[:, 0:K], s_st[:, 0:K], wscol[:, 0:1], OP.mult)
            p.tt(s_st[:, 0:K], s_st[:, 0:K], pS3[:, :], OP.add)
            p.vcopy(s_bf[:, :], s_st[:, 0:K])
            p.ts(y2[:, :], y1[:, :], 0.125, OP.mult)
            p.red(gm[:, 0:1], y2[:, :])
            p.ts(gm[:, 0:1], gm[:, 0:1], 1.0 / K, OP.mult)
            p.act(y1[:, :], y2[:, :], AF.Square)
            p.red(gv[:, 0:1], y1[:, :])
            p.ts(gv[:, 0:1], gv[:, 0:1], 1.0 / K, OP.mult)
            p.stt(gv[:, 0:1], gm[:, 0:1], gm[:, 0:1], gv[:, 0:1], OP.mult, OP.subtract)
            p.act(gv[:, 0:1], gv[:, 0:1], AF.Ln, bias=1e-5, scale=-1.0)
            p.act(gv[:, 0:1], gv[:, 0:1], AF.Exp, scale=-0.5)
            p.ts(y2[:, :], y2[:, :], gm[:, 0:1], OP.subtract)
            p.ts(y2[:, :], y2[:, :], gv[:, 0:1], OP.mult)
            p.tt(y2[:, :], y2[:, :], lnwrep[:, h * K:(h + 1) * K], OP.mult)
            p.tt(ynat[t][:, h * K:(h + 1) * K], y2[:, :], lnbrep[:, h * K:(h + 1) * K], OP.add)
    p.free_grp("p2")

    xres = p.sb([128, D], F32, "xres", grp="p3")
    gt = p.sb([128, D], BF16, "gt", grp="p3")
    yg = p.sb([128, D], BF16, "yg", grp="p3")
    ztmp = p.sb([128, 512], F32, "zt3", grp="p3")
    otp = p.sb([128, 128], BF16, "otp", grp="p3")
    pbf3 = p.ps([128, 128], BF16, "ptr3", grp="p3")
    py = [p.ps([128, 512], F32, "py", grp="p3") for _ in range(2)]
    for t in range(NT):
        p.dma(xres[:, :], x_h[t * 128:(t + 1) * 128, :])
        p.dma(gt[:, :], gn_hbm[t * 128:(t + 1) * 128, :])
        p.tt(yg[:, :], ynat[t][:, :], gt[:, :], OP.mult)
        for hv in range(ND):
            p.trans(pbf3[:, :], yg[:, hv * 128:(hv + 1) * 128], c["id_sb"][:, :])
            p.vcopy(otp[:, :], pbf3[:, :])
            for dc in range(2):
                p.mm(py[dc][:, :], otp[:, :], woT[hv][:, dc * 512:(dc + 1) * 512],
                     start=(hv == 0), stop=(hv == ND - 1), wait=(dc == 0), sgc=True)
        for dc in range(2):
            dcs = slice(dc * 512, (dc + 1) * 512)
            p.tt(ztmp[:, :], py[dc][:, :], xres[:, dcs], OP.add)
            p.dma(y_h[t * 128:(t + 1) * 128, dcs], ztmp[:, :])
    return p.build() if standalone else None


def run_l0_attn_b(x, inp):
    nc = get_prog("l0_attn_b", build_l0_attn_b)
    cmn = const_inputs()
    jj = np.arange(128)
    e1 = np.maximum(jj[None, :] - jj[:, None] - 1, 0).astype(np.float32)
    su = (jj[None, :] > jj[:, None]).astype(np.float32)
    iota = np.stack([jj.astype(np.float32), (127 - jj).astype(np.float32)], axis=1)
    iota = np.ascontiguousarray(iota)
    maps = []
    for i in range(NCORE):
        m = dict(cmn)
        m["c_e1"] = e1
        m["c_su"] = su
        m["c_eye"] = np.eye(128, dtype=np.float32)
        m["c_iota"] = iota
        m["x"] = np.ascontiguousarray(x[i])
        m["tmk"] = inp["L0_attn_b_time_mix_k"]
        m["tmv"] = inp["L0_attn_b_time_mix_v"]
        m["tmr"] = inp["L0_attn_b_time_mix_r"]
        m["tmg"] = inp["L0_attn_b_time_mix_g"]
        m["td"] = inp["L0_attn_b_time_decay"]
        m["tf"] = inp["L0_attn_b_time_faaaa"]
        m["W_r"] = inp["L0_attn_b_W_r"]
        m["W_k"] = inp["L0_attn_b_W_k"]
        m["W_v"] = inp["L0_attn_b_W_v"]
        m["W_g"] = inp["L0_attn_b_W_g"]
        m["W_o"] = inp["L0_attn_b_W_o"]
        m["ln_w"] = inp["L0_attn_b_ln_w"]
        m["ln_b"] = inp["L0_attn_b_ln_b"]
        maps.append(m)
    res = run_bass_kernel_spmd(nc, maps, core_ids=list(range(NCORE)))
    return np.stack([r["y"] for r in res.results], axis=0)



def build_l0_attn_a(p=None, c=None, xin=None, yout=None):
    standalone = p is None
    if standalone:
        p = Prog()
        c = load_consts(p)
    x_h = xin if xin is not None else p.dram("x", [T, D], F32)
    y_h = yout if yout is not None else p.dram("y", [T, D], F32, kind="ExternalOutput")
    wsel_h = p.dram("W_sel", [D, 160], F32)
    dtw_h = p.dram("dt_W", [32, 512], F32)
    dtb_h = p.dram("dt_b", [512], F32)
    wk_h = p.dram("Wk", [D, 512], F32)
    wv_h = p.dram("Wv", [D, 512], F32)
    wr_h = p.dram("Wr", [D, 512], F32)
    wg_h = p.dram("Wg", [D, 512], F32)
    wo_h = p.dram("Wo", [512, D], F32)
    bd_h = p.dram("base_decay", [8, 64], F32)
    tf_h = p.dram("time_first", [8, 64], F32)
    tri_h = p.dram("c_tri", [128, 128], F32)
    sl_h = p.dram("c_sl", [128, 128], F32)
    msk_h = p.dram("c_msk", [128, 128], BF16)
    rm_h = p.dram("c_rm", [128, 4], F32)
    sel8_h = p.dram("c_sel8", [8, 512], BF16)

    ld_hbm = p.dram("sc_ld", [T, 512], F32, kind="Internal")
    kn_hbm = p.dram("sc_kn", [T, 512], BF16, kind="Internal")
    kT_hbm = p.dram("sc_kT", [512, T], BF16, kind="Internal")
    vn_hbm = p.dram("sc_vn", [T, 512], BF16, kind="Internal")
    rg_hbm = p.dram("sc_rg", [T, 512], BF16, kind="Internal")
    ex_hbm = p.dram("sc_ex", [T, 512], BF16, kind="Internal")
    bs_hbm = p.dram("sc_bs", [T], F32, kind="Internal")

    wo = [p.sb([128, D], BF16, "wo") for _ in range(4)]

    prep = p.ps([128, 512], F32, "prep", grp="p1")
    fscr = p.sb([128, 512], F32, "fscr", grp="p1")
    wsel = [p.sb([128, 160], BF16, "wsel", grp="p1") for _ in range(ND)]
    wk = [p.sb([128, 512], BF16, "wk", grp="p1") for _ in range(ND)]
    wv = [p.sb([128, 512], BF16, "wv", grp="p1") for _ in range(ND)]
    wr = [p.sb([128, 512], BF16, "wr", grp="p1") for _ in range(ND)]
    wg = [p.sb([128, 512], BF16, "wg", grp="p1") for _ in range(ND)]
    for k in range(ND):
        p.dma(fscr[:, 0:160], wsel_h[k * 128:(k + 1) * 128, :])
        p.vcopy(wsel[k][:, :], fscr[:, 0:160])
        for (dst, hh) in ((wk, wk_h), (wv, wv_h), (wr, wr_h), (wg, wg_h)):
            p.dma(fscr[:, :], hh[k * 128:(k + 1) * 128, :])
            p.vcopy(dst[k][:, :], fscr[:, :])
    for k in range(4):
        p.dma(fscr[:, :], wo_h[k * 128:(k + 1) * 128, 0:512])
        p.vcopy(wo[k][:, 0:512], fscr[:, :])
        p.dma(fscr[:, :], wo_h[k * 128:(k + 1) * 128, 512:1024])
        p.vcopy(wo[k][:, 512:1024], fscr[:, :])
    dtw = p.sb([32, 512], BF16, "dtw", grp="p1")
    p.dma(fscr[0:32, :], dtw_h[:, :])
    p.vcopy(dtw[:, :], fscr[0:32, :])
    srow = p.sb([1, 512], F32, "srow", grp="p1")
    dtbrep = p.sb([128, 512], F32, "dtbrep", grp="p1")
    nbdrep = p.sb([128, 512], F32, "nbdrep", grp="p1")
    tfrep = p.sb([128, 512], BF16, "tfrep", grp="p1")
    p.dma(srow[:, :], dtb_h.rearrange("(one n) -> one n", one=1))
    p.mm(prep[:, :], c["ones_sb"][:, :], srow[:, :])
    p.scopy(dtbrep[:, :], prep[:, :])
    p.dma(srow[:, :], bd_h.rearrange("h k -> (h k)").rearrange("(one n) -> one n", one=1))
    p.mm(prep[:, :], c["ones_sb"][:, :], srow[:, :])
    p.act(nbdrep[:, :], prep[:, :], AF.Copy, scale=-1.0)
    p.dma(srow[:, :], tf_h.rearrange("h k -> (h k)").rearrange("(one n) -> one n", one=1))
    p.mm(prep[:, :], c["ones_sb"][:, :], srow[:, :])
    p.scopy(tfrep[:, :], prep[:, :])
    sel8 = p.sb([8, 512], BF16, "sel8", grp="p1")
    p.dma(sel8[:, :], sel8_h[:, :])

    lnT = [p.sb([128, 512], BF16, "lnTc", grp="p1") for _ in range(ND)]
    dtT = p.sb([32, T], BF16, "dtT", grp="p1")
    xt = p.sb([128, D], F32, "xt", grp="p1")
    tmp = p.sb([128, D], F32, "lntmp", grp="p1")
    mcol = p.sb([128, 1], F32, "m", grp="p1")
    vcol = p.sb([128, 1], F32, "v", grp="p1")
    lnb2 = p.sb([128, D], BF16, "lnb2", grp="p1")
    pbf = p.ps([128, 128], BF16, "ptr", grp="p1")
    ph = p.ps([128, 512], F32, "ph", grp="p1")
    pse = p.ps([128, 160], F32, "pse", grp="p1")
    ptr32 = p.ps([32, 128], BF16, "ptr32", grp="p1")
    bsc = p.sb([128, 1], F32, "bsc", grp="p1")
    dtrb = p.sb([128, 32], BF16, "dtrb", grp="p1")
    ktile = p.sb([128, 512], BF16, "ktile", grp="p1")
    vtile = p.sb([128, 512], BF16, "vtile", grp="p1")
    rs = p.sb([128, 512], BF16, "rs", grp="p1")
    gs = p.sb([128, 512], BF16, "gs", grp="p1")
    e1b = p.sb([128, 512], BF16, "e1b", grp="p1")
    sqf = p.sb([128, 512], F32, "sqf", grp="p1")
    rs8 = p.sb([128, 8], BF16, "rs8", grp="p1")
    rs8T = p.sb([8, 128], BF16, "rs8T", grp="p1")
    x32 = p.sb([128, 512], F32, "x32", grp="p1")
    ptr8 = p.ps([8, 128], BF16, "ptr8", grp="p1")

    for ch in range(8):
        tok = slice(ch * 512, (ch + 1) * 512)
        for tt_ in range(4):
            t = ch * 4 + tt_
            p.dma(xt[:, :], x_h[t * 128:(t + 1) * 128, :])
            layernorm_tile(p, lnb2, xt, tmp, mcol, vcol)
            transpose_nat_to_T(p, c, lnT, lnb2, tt_, pbf)
        for tt_ in range(4):
            t = ch * 4 + tt_
            tsl = slice(tt_ * 128, (tt_ + 1) * 128)
            for k in range(ND):
                p.mm(pse[:, :], lnT[k][:, tsl], wsel[k][:, :],
                     start=(k == 0), stop=(k == ND - 1), wait=(k == 0))
            p.red(bsc[:, 0:1], pse[:, 32:96])
            p.ts(bsc[:, 0:1], bsc[:, 0:1], 1.0 / 64.0, OP.mult)
            p.dma(bs_hbm[t * 128:(t + 1) * 128].rearrange("(pp o) -> pp o", o=1), bsc[:, 0:1])
            p.scopy(dtrb[:, :], pse[:, 0:32])
            p.trans(ptr32[:, :], dtrb[:, :], c["id_sb"][:, :])
            p.vcopy(dtT[:, t * 128:(t + 1) * 128], ptr32[:, :])
            for k in range(ND):
                p.mm(ph[:, :], lnT[k][:, tsl], wk[k][:, :],
                     start=(k == 0), stop=(k == ND - 1), wait=(k == 0))
            p.scopy(ktile[:, :], ph[:, :])
            p.dma(kn_hbm[t * 128:(t + 1) * 128, :], ktile[:, :])
            for k in range(ND):
                p.mm(ph[:, :], lnT[k][:, tsl], wv[k][:, :],
                     start=(k == 0), stop=(k == ND - 1), wait=(k == 0))
            p.scopy(vtile[:, :], ph[:, :])
            p.dma(vn_hbm[t * 128:(t + 1) * 128, :], vtile[:, :])
            for k in range(ND):
                p.mm(ph[:, :], lnT[k][:, tsl], wr[k][:, :],
                     start=(k == 0), stop=(k == ND - 1), wait=(k == 0))
            p.act(rs[:, :], ph[:, :], AF.Sigmoid)
            for k in range(ND):
                p.mm(ph[:, :], lnT[k][:, tsl], wg[k][:, :],
                     start=(k == 0), stop=(k == ND - 1), wait=(k == 0))
            p.act(gs[:, :], ph[:, :], AF.Sigmoid)
            p.tt(rs[:, :], rs[:, :], gs[:, :], OP.mult)
            p.dma(rg_hbm[t * 128:(t + 1) * 128, :], rs[:, :])
            p.tt(e1b[:, :], ktile[:, :], vtile[:, :], OP.mult)
            p.tt(e1b[:, :], e1b[:, :], tfrep[:, :], OP.mult)
            p.act(sqf[:, :], ktile[:, :], AF.Square)
            for h8 in range(8):
                p.red(mcol[:, 0:1], sqf[:, h8 * 64:(h8 + 1) * 64])
                p.ts(rs8[:, h8:h8 + 1], mcol[:, 0:1], bsc[:, 0:1], OP.mult)
            p.trans(ptr8[:, :], rs8[:, :], c["id_sb"][:, :])
            p.vcopy(rs8T[:, :], ptr8[:, :])
            p.mm(ph[:, :], rs8T[:, :], sel8[:, :], sgc=True)
            p.tt(x32[:, :], vtile[:, :], ph[:, :], OP.mult)
            p.tt(e1b[:, :], e1b[:, :], x32[:, :], OP.add)
            p.dma(ex_hbm[t * 128:(t + 1) * 128, :], e1b[:, :])
            p.mm(ph[:, :], dtT[:, t * 128:(t + 1) * 128], dtw[:, :], sgc=True)
            p.tt(x32[:, :], ph[:, :], dtbrep[:, :], OP.add)
            p.act(x32[:, :], x32[:, :], AF.Exp)
            p.act(x32[:, :], x32[:, :], AF.Ln, bias=1.0)
            p.tt(x32[:, :], x32[:, :], nbdrep[:, :], OP.mult)
            p.dma(ld_hbm[t * 128:(t + 1) * 128, :], x32[:, :])
        for ntl in range(4):
            for k in range(ND):
                p.mm(ph[:, :], wk[k][:, ntl * 128:(ntl + 1) * 128], lnT[k][:, :],
                     start=(k == 0), stop=(k == ND - 1), wait=(k == 0))
            p.scopy(x32[:, :], ph[:, :])
            p.vcopy(rs[:, :], x32[:, :])
            p.dma(kT_hbm[ntl * 128:(ntl + 1) * 128, tok], rs[:, :])
    p.free_grp("p1")

    tric = p.sb([128, 128], F32, "tric", grp="p2")
    slc = p.sb([128, 128], F32, "slc", grp="p2")
    mskc = p.sb([128, 128], BF16, "mskc", grp="p2")
    rmc = p.sb([128, 4], F32, "rmc", grp="p2")
    p.dma(tric[:, :], tri_h[:, :])
    p.dma(slc[:, :], sl_h[:, :])
    p.dma(mskc[:, :], msk_h[:, :])
    p.dma(rmc[:, :], rm_h[:, :])
    ldt = p.sb([128, 512], F32, "ldt", grp="p2")
    knt = p.sb([128, 512], BF16, "knt", grp="p2")
    vnt = p.sb([128, 512], BF16, "vnt", grp="p2")
    vp = p.sb([128, 512], BF16, "vp", grp="p2")
    ext = p.sb([128, 512], BF16, "ext", grp="p2")
    rgt = p.sb([128, 512], BF16, "rgt", grp="p2")
    kTt = [p.sb([128, 128], BF16, "kTt", grp="p2") for _ in range(4)]
    bscol = p.sb([128, 1], F32, "bscol", grp="p2")
    xrest = p.sb([128, D], F32, "xrest", grp="p2")
    qTh = [p.sb([64, 128], F32, "qTh", grp="p2") for _ in range(8)]
    uTh = [p.sb([64, 128], F32, "uTh", grp="p2") for _ in range(8)]
    qe = p.sb([128, 128], F32, "qe", grp="p2")
    Ecol = [p.sb([128, 4], F32, "Ecol", grp="p2") for _ in range(4)]
    er = p.sb([128, 512], F32, "er", grp="p2")
    wt = p.sb([128, 512], BF16, "wt", grp="p2")
    wtc = p.sb([128, 128], BF16, "wtc", grp="p2")
    S = [p.sb([128, 128], F32, "S", grp="p2") for _ in range(4)]
    Sstk = [p.sb([64, 256], F32, "Sstk", grp="p2") for _ in range(8)]
    PTm = p.sb([128, 128], BF16, "PTm", grp="p2")
    ya = p.sb([128, 64], F32, "ya", grp="p2")
    yb = p.sb([128, 64], F32, "yb", grp="p2")
    yacc = p.sb([128, 512], F32, "yacc", grp="p2")
    ysf = p.sb([128, 512], BF16, "ysf", grp="p2")
    otp = p.sb([128, 128], BF16, "otp", grp="p2")
    pL = p.ps([128, 512], F32, "pL", grp="p2")
    pT = p.ps([128, 128], F32, "pT", grp="p2")
    pPT = p.ps([128, 128], F32, "pPT", grp="p2")
    pO = p.ps([128, 512], F32, "pO", grp="p2")
    pC = p.ps([128, 256], F32, "pC", grp="p2")
    ptr2 = p.ps([128, 128], BF16, "ptr2", grp="p2")
    py = [p.ps([128, 512], F32, "py", grp="p2") for _ in range(2)]

    for kc in range(4):
        p.memset(S[kc][:, :], 0.0)
    for t in range(NT):
        p.dma(ldt[:, :], ld_hbm[t * 128:(t + 1) * 128, :])
        p.dma(knt[:, :], kn_hbm[t * 128:(t + 1) * 128, :])
        p.dma(vnt[:, :], vn_hbm[t * 128:(t + 1) * 128, :])
        p.dma(ext[:, :], ex_hbm[t * 128:(t + 1) * 128, :])
        p.dma(rgt[:, :], rg_hbm[t * 128:(t + 1) * 128, :])
        p.dma(bscol[:, 0:1], bs_hbm[t * 128:(t + 1) * 128].rearrange("(pp o) -> pp o", o=1))
        p.dma(xrest[:, :], x_h[t * 128:(t + 1) * 128, :])
        for kc in range(4):
            p.dma(kTt[kc][:, :], kT_hbm[kc * 128:(kc + 1) * 128, t * 128:(t + 1) * 128])
        for kc in range(4):
            p.mm(pL[:, kc * 128:(kc + 1) * 128], ldt[:, kc * 128:(kc + 1) * 128], tric[:, :],
                 sgc=True, wait=(kc == 0), start=(kc == 0), stop=(kc == 3))
        for kc in range(4):
            psl = pL[:, kc * 128:(kc + 1) * 128]
            p.act(qe[:, :], psl, AF.Exp)
            for hf in range(2):
                p.tt(qTh[2 * kc + hf][:, :], kTt[kc][hf * 64:(hf + 1) * 64, :],
                     qe[hf * 64:(hf + 1) * 64, :], OP.mult)
            p.ts(qe[:, :], psl, -1.0, OP.mult)
            p.ts(qe[:, :], qe[:, :], 80.0, OP.min)
            p.act(qe[:, :], qe[:, :], AF.Exp)
            for hf in range(2):
                p.tt(uTh[2 * kc + hf][:, :], kTt[kc][hf * 64:(hf + 1) * 64, :],
                     qe[hf * 64:(hf + 1) * 64, :], OP.mult)
            for cc in range(4):
                p.act(Ecol[kc][:, cc:cc + 1], psl[:, 32 * cc + 31:32 * cc + 32], AF.Exp)
        p.mm(pL[:, :], slc[:, :], ldt[:, :], sgc=True)
        p.act(er[:, :], pL[:, :], AF.Exp)
        p.tt(wt[:, :], knt[:, :], er[:, :], OP.mult)
        p.ts(vp[:, :], vnt[:, :], bscol[:, 0:1], OP.mult)
        for cc in range(4):
            for kc in range(4):
                p.vcopy(Sstk[2 * kc][:, cc * 64:(cc + 1) * 64], S[kc][0:64, 0:64])
                p.vcopy(Sstk[2 * kc + 1][:, cc * 64:(cc + 1) * 64], S[kc][64:128, 64:128])
            for kc in range(4):
                p.ts(wtc[:, :], wt[:, kc * 128:(kc + 1) * 128], rmc[:, cc:cc + 1], OP.mult)
                p.mm(pT[:, :], wtc[:, :], vp[:, kc * 128:(kc + 1) * 128], sgc=True)
                p.ts(S[kc][:, :], S[kc][:, :], Ecol[kc][:, cc:cc + 1], OP.mult)
                p.tt(S[kc][:, :], S[kc][:, :], pT[:, :], OP.add)
        for h in range(8):
            p.mm(pC[:, :], qTh[h][:, :], Sstk[h][:, :], sgc=True)
            p.mm(pPT[:, :], uTh[h][:, :], qTh[h][:, :], sgc=True)
            p.tt(PTm[:, :], pPT[:, :], mskc[:, :], OP.mult)
            p.mm(pO[:, h * 64:(h + 1) * 64], PTm[:, :], vp[:, h * 64:(h + 1) * 64], sgc=True,
                 start=(h == 0), stop=(h == 7))
            p.ts(ya[:, :], pC[:, 0:64], rmc[:, 0:1], OP.mult)
            p.stt(yb[:, :], pC[:, 64:128], rmc[:, 1:2], ya[:, :], OP.mult, OP.add)
            p.stt(ya[:, :], pC[:, 128:192], rmc[:, 2:3], yb[:, :], OP.mult, OP.add)
            p.stt(yb[:, :], pC[:, 192:256], rmc[:, 3:4], ya[:, :], OP.mult, OP.add)
            p.tt(yacc[:, h * 64:(h + 1) * 64], yb[:, :], pO[:, h * 64:(h + 1) * 64], OP.add)
        p.tt(yacc[:, :], yacc[:, :], ext[:, :], OP.add)
        p.tt(ysf[:, :], yacc[:, :], rgt[:, :], OP.mult)
        for kc in range(4):
            p.trans(ptr2[:, :], ysf[:, kc * 128:(kc + 1) * 128], c["id_sb"][:, :])
            p.vcopy(otp[:, :], ptr2[:, :])
            for dc in range(2):
                p.mm(py[dc][:, :], otp[:, :], wo[kc][:, dc * 512:(dc + 1) * 512],
                     start=(kc == 0), stop=(kc == 3), wait=(dc == 0), sgc=True)
        for dc in range(2):
            dcs = slice(dc * 512, (dc + 1) * 512)
            p.tt(er[:, :], py[dc][:, :], xrest[:, dcs], OP.add)
            p.dma(y_h[t * 128:(t + 1) * 128, dcs], er[:, :])
    return p.build() if standalone else None


def run_l0_attn_a(x, inp):
    nc = get_prog("l0_attn_a", build_l0_attn_a)
    cmn = const_inputs()
    jj = np.arange(128)
    same = (jj[:, None] // 32) == (jj[None, :] // 32)
    tri = ((jj[:, None] <= jj[None, :]) & same).astype(np.float32)
    sl = ((jj[:, None] > jj[None, :]) & same).astype(np.float32)
    msk = ((jj[:, None] < jj[None, :]) & same).astype(bf16_np())
    rm = np.zeros((128, 4), np.float32)
    for cc in range(4):
        rm[cc * 32:(cc + 1) * 32, cc] = 1.0
    sel8 = np.zeros((8, 512), np.float32)
    for h8 in range(8):
        sel8[h8, h8 * 64:(h8 + 1) * 64] = 1.0
    maps = []
    for i in range(NCORE):
        m = dict(cmn)
        m["c_tri"] = tri
        m["c_sl"] = sl
        m["c_msk"] = msk
        m["c_rm"] = rm
        m["c_sel8"] = sel8.astype(bf16_np())
        m["x"] = np.ascontiguousarray(x[i])
        m["W_sel"] = inp["L0_attn_a_W_sel"]
        m["dt_W"] = inp["L0_attn_a_dt_W"]
        m["dt_b"] = inp["L0_attn_a_dt_b"]
        m["Wk"] = inp["L0_attn_a_Wk"]
        m["Wv"] = inp["L0_attn_a_Wv"]
        m["Wr"] = inp["L0_attn_a_Wr"]
        m["Wg"] = inp["L0_attn_a_Wg"]
        m["Wo"] = inp["L0_attn_a_Wo"]
        m["base_decay"] = inp["L0_attn_a_base_decay"]
        m["time_first"] = inp["L0_attn_a_time_first"]
        maps.append(m)
    res = run_bass_kernel_spmd(nc, maps, core_ids=list(range(NCORE)))
    return np.stack([r["y"] for r in res.results], axis=0)


_PROG_CACHE = {}


def get_prog(key, builder):
    if key not in _PROG_CACHE:
        _PROG_CACHE[key] = builder()
    return _PROG_CACHE[key]


def run_l1_ffn(x, inp):
    nc = get_prog("l1_ffn", build_l1_ffn)
    cmn = const_inputs()
    maps = []
    for i in range(NCORE):
        m = dict(cmn)
        m["x"] = np.ascontiguousarray(x[i])
        m["ln_g"] = inp["L1_ffn_ln_gamma"]
        m["ln_b"] = inp["L1_ffn_ln_beta"]
        m["fc1_q"] = inp["L1_ffn_fc1_q"]
        m["fc1_s"] = inp["L1_ffn_fc1_scales"]
        m["fc1_b"] = inp["L1_ffn_fc1_bias"]
        m["fc2_q"] = inp["L1_ffn_fc2_q"]
        m["fc2_s"] = inp["L1_ffn_fc2_scales"]
        m["fc2_b"] = inp["L1_ffn_fc2_bias"]
        maps.append(m)
    res = run_bass_kernel_spmd(nc, maps, core_ids=list(range(NCORE)))
    return np.stack([r["y"] for r in res.results], axis=0)





def inputs_l0_attn_a(inp):
    jj = np.arange(128)
    same = (jj[:, None] // 32) == (jj[None, :] // 32)
    rm = np.zeros((128, 4), np.float32)
    for cc in range(4):
        rm[cc * 32:(cc + 1) * 32, cc] = 1.0
    sel8 = np.zeros((8, 512), np.float32)
    for h8 in range(8):
        sel8[h8, h8 * 64:(h8 + 1) * 64] = 1.0
    return {
        "c_tri": ((jj[:, None] <= jj[None, :]) & same).astype(np.float32),
        "c_sl": ((jj[:, None] > jj[None, :]) & same).astype(np.float32),
        "c_msk": ((jj[:, None] < jj[None, :]) & same).astype(bf16_np()),
        "c_rm": rm, "c_sel8": sel8.astype(bf16_np()),
        "W_sel": inp["L0_attn_a_W_sel"], "dt_W": inp["L0_attn_a_dt_W"],
        "dt_b": inp["L0_attn_a_dt_b"], "Wk": inp["L0_attn_a_Wk"],
        "Wv": inp["L0_attn_a_Wv"], "Wr": inp["L0_attn_a_Wr"],
        "Wg": inp["L0_attn_a_Wg"], "Wo": inp["L0_attn_a_Wo"],
        "base_decay": inp["L0_attn_a_base_decay"], "time_first": inp["L0_attn_a_time_first"],
    }


def inputs_l0_ffn(inp):
    basis = np.zeros((128, 4096), np.float32)
    for j in range(128):
        basis[j, j::128] = 1.0
    return {
        "c_basis": basis,
        "W_exp": inp["L0_ffn_W_exp"], "b_exp": inp["L0_ffn_b_exp"],
        "up_w": inp["L0_ffn_up_weights"], "up_b": inp["L0_ffn_up_bias"],
        "down_w": inp["L0_ffn_down_weights"], "W_con": inp["L0_ffn_W_con"],
        "b_con": inp["L0_ffn_b_con"], "down_b": inp["L0_ffn_down_bias"],
    }


def inputs_l0_attn_b(inp):
    jj = np.arange(128)
    return {
        "c_e1": np.maximum(jj[None, :] - jj[:, None] - 1, 0).astype(np.float32),
        "c_su": (jj[None, :] > jj[:, None]).astype(np.float32),
        "c_eye": np.eye(128, dtype=np.float32),
        "c_iota": np.ascontiguousarray(
            np.stack([jj.astype(np.float32), (127 - jj).astype(np.float32)], axis=1)),
        "tmk": inp["L0_attn_b_time_mix_k"], "tmv": inp["L0_attn_b_time_mix_v"],
        "tmr": inp["L0_attn_b_time_mix_r"], "tmg": inp["L0_attn_b_time_mix_g"],
        "td": inp["L0_attn_b_time_decay"], "tf": inp["L0_attn_b_time_faaaa"],
        "W_r": inp["L0_attn_b_W_r"], "W_k": inp["L0_attn_b_W_k"],
        "W_v": inp["L0_attn_b_W_v"], "W_g": inp["L0_attn_b_W_g"],
        "W_o": inp["L0_attn_b_W_o"], "ln_w": inp["L0_attn_b_ln_w"],
        "ln_b": inp["L0_attn_b_ln_b"],
    }


def inputs_attn(kind, inp):
    if kind == "mla":
        jj = np.arange(128)
        return {
            "c_diag": (jj[:, None] <= jj[None, :]).astype(bf16_np()),
            "W_DKV": inp["L1_attn_a_W_DKV"], "W_UK": inp["L1_attn_a_W_UK"],
            "W_UV": inp["L1_attn_a_W_UV"], "W_UQ": inp["L1_attn_a_W_UQ"],
            "W_O": inp["L1_attn_a_W_O"],
        }
    return {
        "ln_g": inp["L1_attn_b_ln_gamma"], "ln_b": inp["L1_attn_b_ln_beta"],
        "Wq": inp["L1_attn_b_Wq"], "Wk": inp["L1_attn_b_Wk"],
        "Wv": inp["L1_attn_b_Wv"], "Wo": inp["L1_attn_b_Wo"],
        "bq": inp["L1_attn_b_bq"], "bk": inp["L1_attn_b_bk"],
        "bv": inp["L1_attn_b_bv"], "bo": inp["L1_attn_b_bo"],
        "mu": np.asarray(inp["L1_attn_b_mu"], np.float32).reshape(1, 1),
    }


def inputs_l1_ffn(inp):
    return {
        "ln_g": inp["L1_ffn_ln_gamma"], "ln_b": inp["L1_ffn_ln_beta"],
        "fc1_q": inp["L1_ffn_fc1_q"], "fc1_s": inp["L1_ffn_fc1_scales"],
        "fc1_b": inp["L1_ffn_fc1_bias"], "fc2_q": inp["L1_ffn_fc2_q"],
        "fc2_s": inp["L1_ffn_fc2_scales"], "fc2_b": inp["L1_ffn_fc2_bias"],
    }



_STAGES = [
    ("s0", lambda **kw: build_l0_attn_a(**kw), inputs_l0_attn_a),
    ("s1", lambda **kw: build_l0_ffn(**kw), inputs_l0_ffn),
    ("s2", lambda **kw: build_l0_attn_b(**kw), inputs_l0_attn_b),
    ("s3", lambda **kw: build_l0_ffn(**kw), inputs_l0_ffn),
    ("s4", lambda **kw: build_attn("mla", **kw), lambda inp: inputs_attn("mla", inp)),
    ("s5", lambda **kw: build_l1_ffn(**kw), inputs_l1_ffn),
    ("s6", lambda **kw: build_attn("sig", **kw), lambda inp: inputs_attn("sig", inp)),
    ("s7", lambda **kw: build_l1_ffn(**kw), inputs_l1_ffn),
]


def build_fused():
    p = Prog()
    c = load_consts(p)
    x_h = p.dram("x", [T, D], F32)
    y_h = p.dram("y", [T, D], F32, kind="ExternalOutput")
    bufs = [p.dram("xchain0", [T, D], F32, kind="Internal"),
            p.dram("xchain1", [T, D], F32, kind="Internal")]
    cur = x_h
    for i, (nm, fn, _) in enumerate(_STAGES):
        out = y_h if i == len(_STAGES) - 1 else bufs[i % 2]
        p.stage = nm
        p.pref = nm + "_"
        fn(p=p, c=c, xin=cur, yout=out)
        p.end_stage()
        cur = out
    p.stage = ""
    p.pref = ""
    return p.build()


def kernel_fused(inp):
    nc = get_prog("fused", build_fused)
    base = const_inputs()
    for nm, _, mkin in _STAGES:
        for k, v in mkin(inp).items():
            base[nm + "_" + k] = v
    x = np.ascontiguousarray(np.asarray(inp["L0_attn_a_x"], np.float32))
    maps = []
    for i in range(NCORE):
        m = dict(base)
        m["x"] = np.ascontiguousarray(x[i])
        maps.append(m)
    res = run_bass_kernel_spmd(nc, maps, core_ids=list(range(NCORE)))
    return np.stack([r["y"] for r in res.results], axis=0).astype(np.float32)



def kernel(**inp):
    return kernel_fused(inp)


def kernel_multilaunch(**inp):
    x = np.ascontiguousarray(np.asarray(inp["L0_attn_a_x"], np.float32))
    x = run_l0_attn_a(x, inp)
    x = run_l0_ffn(x, inp)
    x = run_l0_attn_b(x, inp)
    x = run_l0_ffn(x, inp)
    x = run_attn("mla", x, inp)
    x = run_l1_ffn(x, inp)
    x = run_attn("sig", x, inp)
    x = run_l1_ffn(x, inp)
    return np.asarray(x, np.float32)


# version 09
# speedup vs baseline: 1.0307x; 1.0209x over previous
import contextlib
import numpy as np
import concourse.bass as bass
import concourse.mybir as mybir
from concourse.bass_utils import run_bass_kernel_spmd

F32 = mybir.dt.float32
BF16 = mybir.dt.bfloat16
I32 = mybir.dt.int32
AF = mybir.ActivationFunctionType
OP = mybir.AluOpType
AX = mybir.AxisListType

NCORE = 8
T = 4096
D = 1024
NT = T // 128
ND = D // 128


class Prog:

    def __init__(self):
        self.nc = bass.Bass("TRN2", target_bir_lowering=False)
        self.q = {"tensor": [], "vector": [], "scalar": [], "sync": [], "gpsimd": []}
        self.cnt = 0
        self.st = contextlib.ExitStack()
        self._names = {}
        self._groups = {}
        self.pref = ""
        self.stage = ""


    def name(self, base):
        i = self._names.get(base, 0)
        self._names[base] = i + 1
        return f"{base}{i}"

    def dram(self, name, shape, dtype, kind="ExternalInput"):
        return self.nc.dram_tensor(self.pref + name, list(shape), dtype, kind=kind).ap()

    def _alloc(self, cm, grp):
        h = cm.__enter__()
        if grp is None:
            self.st.callback(cm.__exit__, None, None, None)
        else:
            self._groups.setdefault(grp, []).append(cm)
        return h.ap() if hasattr(h, "ap") else h

    def sb(self, shape, dtype, base="sb", grp=None):
        g = self.stage + ":" + (grp or "main")
        return self._alloc(self.nc.sbuf_tensor(self.name(self.pref + base), list(shape), dtype), g)

    def ps(self, shape, dtype=F32, base="ps", grp=None):
        g = self.stage + ":" + (grp or "main")
        return self._alloc(self.nc.psum_tensor(self.name(self.pref + base), list(shape), dtype), g)

    def fence(self):
        self._dep_init()
        self._fence = dict(self.ecnt)

    def free_grp(self, grp):
        self.fence()
        for cm in reversed(self._groups.pop(self.stage + ":" + grp, [])):
            cm.__exit__(None, None, None)

    def end_stage(self):
        self.fence()
        ks = [k for k in self._groups if k.startswith(self.stage + ":")]
        for k in reversed(ks):
            for cm in reversed(self._groups.pop(k)):
                cm.__exit__(None, None, None)

    def _dep_init(self):
        if not hasattr(self, "_Wc"):
            self._Wc = {}
            self._Rc = {}
            self._Wd = {}
            self._Rd = {}
            self._guar = {}
            self.ccnt = 0
            self.dcnt = 0
            self._fc = 0
            self._fd = 0

    @staticmethod
    def _aps(xs):
        out = []
        for x in xs:
            if x is None:
                continue
            n = getattr(x, "name", None)
            if isinstance(n, str):
                out.append(n)
        return out

    def emit(self, eng, fn, inc=1, outs=(), ins=(), accum=False, wait=True):
        self._dep_init()
        ik = self._aps(ins)
        ok = self._aps(outs)
        if eng == "sync":
            na = self._fc
            nd = max(self._fd, self.dcnt)
            for k in ik + ok:
                na = max(na, self._Wc.get(k, 0))
                nd = max(nd, self._Wd.get(k, 0))
            for k in ok:
                na = max(na, self._Rc.get(k, 0))
                nd = max(nd, self._Rd.get(k, 0))
            self._push(eng, fn, inc, na, nd)
            self.dcnt += inc
            for k in ik:
                self._Rd[k] = self.dcnt
            for k in ok:
                self._Wd[k] = self.dcnt
        else:
            na = self.ccnt if wait else 0
            nd = self.dcnt if wait else 0
            self._push(eng, fn, inc, na, nd)
            self.ccnt += inc
            for k in ik:
                self._Rc[k] = self.ccnt
            for k in ok:
                self._Wc[k] = self.ccnt

    def _push(self, eng, fn, inc, na, nd):
        g = self._guar.setdefault(eng, {"a": 0, "d": 0})
        wa = wd = 0
        if na > g["a"]:
            wa = g["a"] = na
        if nd > g["d"]:
            wd = g["d"] = nd
        self.q[eng].append((wa, wd, fn, inc))

    def fence(self):
        self._dep_init()
        self._fc = self.ccnt
        self._fd = self.dcnt

    def dma(self, out, in_, wait=True):
        self.emit("sync", lambda e, o=out, i=in_: e.dma_start(out=o, in_=i), inc=16,
                  outs=[out], ins=[in_])

    def mm(self, out, lhsT, rhs, start=True, stop=True, wait=True, sgc=False):
        self.emit("tensor",
                  lambda e, o=out, l=lhsT, r=rhs, s0=start, s1=stop, g=sgc:
                  e.matmul(o, l, r, start=s0, stop=s1, skip_group_check=g),
                  outs=[out], ins=[lhsT, rhs], accum=(not start), wait=wait)

    def trans(self, out, in_, ident):
        self.emit("tensor", lambda e, o=out, i=in_, d=ident: e.transpose(o, i, d),
                  outs=[out], ins=[in_, ident])

    def act(self, out, in_, func, bias=None, scale=None):
        if isinstance(bias, float):
            bias = np.float32(bias)
        if isinstance(scale, float):
            scale = np.float32(scale)

        def f(e, o=out, i=in_, fu=func, b=bias, s=scale):
            kw = {}
            if b is not None:
                kw["bias"] = b
            if s is not None:
                kw["scale"] = s
            return e.activation(o, i, fu, **kw)
        self.emit("scalar", f, outs=[out], ins=[in_, bias, scale])

    def scopy(self, out, in_):
        self.emit("scalar", lambda e, o=out, i=in_: e.copy(o, i), outs=[out], ins=[in_])

    def tt(self, out, a, b, op):
        self.emit("vector", lambda e, o=out, x=a, y=b, p=op: e.tensor_tensor(o, x, y, p),
                  outs=[out], ins=[a, b])

    def ts(self, out, a, s1, op0, s2=None, op1=None):
        def f(e, o=out, x=a, a1=s1, a2=s2, p0=op0, p1=op1):
            if a2 is None:
                return e.tensor_scalar(o, x, a1, None, p0)
            return e.tensor_scalar(o, x, a1, a2, p0, p1)
        self.emit("vector", f, outs=[out], ins=[a, s1, s2])

    def stt(self, out, in0, scalar, in1, op0, op1):
        self.emit("vector",
                  lambda e, o=out, x=in0, s=scalar, y=in1, p0=op0, p1=op1:
                  e.scalar_tensor_tensor(o, x, s, y, p0, p1),
                  outs=[out], ins=[in0, scalar, in1])

    def red(self, out, in_, op=OP.add):
        self.emit("vector", lambda e, o=out, i=in_, p=op: e.tensor_reduce(o, i, AX.X, p),
                  outs=[out], ins=[in_])

    def vcopy(self, out, in_):
        self.emit("vector", lambda e, o=out, i=in_: e.tensor_copy(o, i), outs=[out], ins=[in_])

    def recip(self, out, in_):
        self.emit("vector", lambda e, o=out, i=in_: e.reciprocal(o, i), outs=[out], ins=[in_])

    def memset(self, ap, const):
        self.emit("vector", lambda e, o=ap, c=const: e.memset(o, c), outs=[ap])

    def build(self):
        nc = self.nc
        self._dep_init()
        semA = self.st.enter_context(nc.semaphore("semA"))
        semD = self.st.enter_context(nc.semaphore("semD"))
        block = self.st.enter_context(nc.Block())
        tc, td = self.ccnt, self.dcnt
        q = self.q

        def runner(eng_name):
            def run(eng):
                for wa, wd, fn, inc in q[eng_name]:
                    if wd > 0:
                        eng.wait_ge(semD, wd)
                    if wa > 0:
                        eng.wait_ge(semA, wa)
                    inst = fn(eng)
                    inst.then_inc(semD if eng_name == "sync" else semA, inc)
                if eng_name == "vector":
                    eng.wait_ge(semD, td)
                    eng.wait_ge(semA, tc)
            return run

        block.sync(runner("sync"))
        block.tensor(runner("tensor"))
        block.scalar(runner("scalar"))
        block.vector(runner("vector"))
        if q["gpsimd"]:
            block.gpsimd(runner("gpsimd"))
        for k in reversed(list(self._groups)):
            for cm in reversed(self._groups.pop(k)):
                cm.__exit__(None, None, None)
        self.st.close()
        return nc



def load_consts(p):
    c = {}
    ones = p.dram("c_ones", [1, 128], F32)
    ident = p.dram("c_ident", [128, 128], BF16)
    c["ones_sb"] = p.sb([1, 128], F32, "ones")
    c["id_sb"] = p.sb([128, 128], BF16, "ident")
    p.dma(c["ones_sb"][:, :], ones[:, :])
    p.dma(c["id_sb"][:, :], ident[:, :])
    return c


def const_inputs():
    return {
        "c_ones": np.ones((1, 128), np.float32),
        "c_ident": np.eye(128, dtype=np.float32).astype(bf16_np()),
    }


_BF16 = None


def bf16_np():
    global _BF16
    if _BF16 is None:
        import ml_dtypes
        _BF16 = ml_dtypes.bfloat16
    return _BF16


def replicate_row(p, c, dst_sb, row_sb, n, scale=None):
    pr = p.ps([128, 512], F32, "prep")
    for j in range(0, n, 512):
        w = min(512, n - j)
        p.mm(pr[:, :w], c["ones_sb"][:, :], row_sb[:, j:j + w])
        if scale is None:
            p.scopy(dst_sb[:, j:j + w], pr[:, :w])
        else:
            p.act(dst_sb[:, j:j + w], pr[:, :w], AF.Copy, scale=scale)


def load_row(p, hbm_vec, n, dtype=F32):
    r = p.sb([1, n], dtype, "row")
    src = hbm_vec.rearrange("(one n) -> one n", one=1) if len(hbm_vec.shape) == 1 else hbm_vec
    p.dma(r[:, :], src)
    return r


def layernorm_tile(p, dst, xt, tmp, m, v, width=None, gamma_rep=None, beta_rep=None, eps=1e-5):
    W = width or xt.shape[-1]
    p.red(m[:, 0:1], xt[:, :W])
    p.ts(m[:, 0:1], m[:, 0:1], 1.0 / W, OP.mult)
    p.act(tmp[:, :W], xt[:, :W], AF.Square)
    p.red(v[:, 0:1], tmp[:, :W])
    p.ts(v[:, 0:1], v[:, 0:1], 1.0 / W, OP.mult)
    p.stt(v[:, 0:1], m[:, 0:1], m[:, 0:1], v[:, 0:1], OP.mult, OP.subtract)
    p.act(v[:, 0:1], v[:, 0:1], AF.Ln, bias=eps, scale=-1.0)
    p.act(v[:, 0:1], v[:, 0:1], AF.Exp, scale=-0.5)
    p.ts(tmp[:, :W], xt[:, :W], m[:, 0:1], OP.subtract)
    if gamma_rep is None:
        p.ts(dst[:, :W], tmp[:, :W], v[:, 0:1], OP.mult)
    else:
        p.ts(tmp[:, :W], tmp[:, :W], v[:, 0:1], OP.mult)
        p.stt(dst[:, :W], tmp[:, :W], 1.0, gamma_rep[:, :W], OP.mult, OP.mult)
        p.tt(dst[:, :W], dst[:, :W], beta_rep[:, :W], OP.add)


def transpose_nat_to_T(p, c, dstT_list, nat_bf, tcol, pbf):
    for j in range(len(dstT_list)):
        p.trans(pbf[:, :128], nat_bf[:, j * 128:(j + 1) * 128], c["id_sb"][:, :])
        p.vcopy(dstT_list[j][:, tcol * 128:(tcol + 1) * 128], pbf[:, :128])



def build_l1_ffn(p=None, c=None, xin=None, yout=None):
    standalone = p is None
    if standalone:
        p = Prog()
        c = load_consts(p)
    x_h = xin if xin is not None else p.dram("x", [T, D], F32)
    y_h = yout if yout is not None else p.dram("y", [T, D], F32, kind="ExternalOutput")
    g_h = p.dram("ln_g", [D], F32)
    b_h = p.dram("ln_b", [D], F32)
    q1_h = p.dram("fc1_q", [D, 4096], I32)
    s1_h = p.dram("fc1_s", [8, 4096], F32)
    b1_h = p.dram("fc1_b", [4096], F32)
    q2_h = p.dram("fc2_q", [4096, D], I32)
    s2_h = p.dram("fc2_s", [32, D], F32)
    b2_h = p.dram("fc2_b", [D], F32)

    F = 4096
    NF = F // 128

    w1 = [p.sb([128, F], BF16, "w1") for _ in range(ND)]
    w2 = [p.sb([128, D], BF16, "w2") for _ in range(NF)]
    qs = p.sb([128, 2048], I32, "qscratch")
    qf = p.sb([128, F], F32, "qfscratch")
    srow = p.sb([1, 512], F32, "srow")
    prep = p.ps([128, 512], F32, "prep2")
    for g in range(ND):
        for h in range(2):
            p.dma(qs[:, :2048], q1_h[g * 128:(g + 1) * 128, h * 2048:(h + 1) * 2048])
            p.vcopy(qf[:, h * 2048:(h + 1) * 2048], qs[:, :2048])
        for j in range(F // 512):
            p.dma(srow[:, :512], s1_h[g:g + 1, j * 512:(j + 1) * 512])
            p.mm(prep[:, :], c["ones_sb"][:, :], srow[:, :512])
            p.stt(w1[g][:, j * 512:(j + 1) * 512], qf[:, j * 512:(j + 1) * 512],
                  -128.0, prep[:, :], OP.add, OP.mult)
    for g in range(F // 128):
        p.dma(qs[:, :D], q2_h[g * 128:(g + 1) * 128, :])
        p.vcopy(qf[:, :D], qs[:, :D])
        for j in range(D // 512):
            p.dma(srow[:, :512], s2_h[g:g + 1, j * 512:(j + 1) * 512])
            p.mm(prep[:, :], c["ones_sb"][:, :], srow[:, :512])
            p.stt(w2[g][:, j * 512:(j + 1) * 512], qf[:, j * 512:(j + 1) * 512],
                  -128.0, prep[:, :], OP.add, OP.mult)

    b1t = p.sb([128, NF], F32, "b1t")
    for ft in range(NF):
        p.dma(b1t[:, ft:ft + 1], b1_h[ft * 128:(ft + 1) * 128].rearrange("(p one) -> p one", one=1))
    b2rep = p.sb([128, D], F32, "b2rep")
    grep = p.sb([128, D], BF16, "grep")
    brep = p.sb([128, D], BF16, "brep")
    for (rep, h, sc) in ((b2rep, b2_h, 0.5), (grep, g_h, None), (brep, b_h, None)):
        for j in range(2):
            p.dma(srow[:, :512], h[j * 512:(j + 1) * 512].rearrange("(one n) -> one n", one=1))
            p.mm(prep[:, :], c["ones_sb"][:, :], srow[:, :512])
            if sc is None:
                p.scopy(rep[:, j * 512:(j + 1) * 512], prep[:, :])
            else:
                p.act(rep[:, j * 512:(j + 1) * 512], prep[:, :], AF.Copy, scale=sc)

    xt = qf[:, 0:1024]
    tmp = qf[:, 1024:2048]
    xres = qf[:, 2048:3072]
    ztmp = qf[:, 3072:3584]
    mcol = p.sb([128, 1], F32, "m")
    vcol = p.sb([128, 1], F32, "v")
    lnb = p.sb([128, D], BF16, "lnb")
    pbf = p.ps([128, 128], BF16, "ptr")
    Wc = 256
    ntc = Wc // 128
    lnT = [p.sb([128, Wc], BF16, "lnT") for _ in range(ND)]
    h1T = [p.sb([128, Wc], BF16, "h1T") for _ in range(NF)]
    tbuf = [p.sb([128, D], F32, "tbuf") for _ in range(ntc)]
    ph = p.ps([128, Wc], F32, "ph")
    pz = p.ps([128, 512], F32, "pz")

    for ch in range(T // Wc):
        for tt_ in range(ntc):
            t = ch * ntc + tt_
            p.dma(xt, x_h[t * 128:(t + 1) * 128, :])
            layernorm_tile(p, tbuf[tt_], xt, tmp, mcol, vcol)
            layernorm_tile(p, lnb, tbuf[tt_], tmp, mcol, vcol, gamma_rep=grep, beta_rep=brep)
            transpose_nat_to_T(p, c, lnT, lnb, tt_, pbf)
        for ft in range(NF):
            for k in range(ND):
                p.mm(ph[:, :], w1[k][:, ft * 128:(ft + 1) * 128], lnT[k][:, :],
                     start=(k == 0), stop=(k == ND - 1), wait=(k == 0))
            p.act(h1T[ft][:, :], ph[:, :], AF.Silu, bias=b1t[:, ft:ft + 1])
        for tt_ in range(ntc):
            t = ch * ntc + tt_
            p.dma(xres, x_h[t * 128:(t + 1) * 128, :])
            for dc in range(2):
                dcs = slice(dc * 512, (dc + 1) * 512)
                for k in range(NF):
                    p.mm(pz[:, :], h1T[k][:, tt_ * 128:(tt_ + 1) * 128], w2[k][:, dcs],
                         start=(k == 0), stop=(k == NF - 1), wait=(k == 0))
                p.stt(ztmp, pz[:, :], 0.5, b2rep[:, dcs], OP.mult, OP.add)
                p.tt(ztmp, ztmp, xres[:, dcs], OP.add)
                p.tt(ztmp, ztmp, tbuf[tt_][:, dcs], OP.add)
                p.dma(y_h[t * 128:(t + 1) * 128, dcs], ztmp)
    return p.build() if standalone else None



def bfly_build_M(p, c, uw_h, M_bf, wrow, wrep, X, Xn, t1, t2):
    uw_flat = uw_h.rearrange("s k a b -> s (k a b)")
    cur, nxt = X, Xn
    for s in range(7):
        s_ = 1 << s
        for q in range(4):
            p.dma(wrow[:, :], uw_flat[s:s + 1, q * 2048:(q + 1) * 2048])
            wv = wrow.rearrange("one (k a b) -> one k a b", a=2, b=2)
            for wi, (ai, bi) in enumerate(((0, 0), (0, 1), (1, 0), (1, 1))):
                p.mm(p._prep[:, :], c["ones_sb"][:, :], wv[:, :, ai, bi])
                p.scopy(wrep[wi][:, q * 512:(q + 1) * 512], p._prep[:, :])

        def av(buf):
            return buf.rearrange("p (nb two s) -> p nb two s", two=2, s=s_)
        a_c, b_c = av(cur)[:, :, 0, :], av(cur)[:, :, 1, :]
        a_n, b_n = av(nxt)[:, :, 0, :], av(nxt)[:, :, 1, :]
        wv3 = [w.rearrange("p (nb s) -> p nb s", s=s_) for w in wrep]
        t1v = t1.rearrange("p (nb s) -> p nb s", s=s_)
        t2v = t2.rearrange("p (nb s) -> p nb s", s=s_)
        p.tt(t1v, a_c, wv3[0], OP.mult)
        p.tt(t2v, b_c, wv3[1], OP.mult)
        p.tt(a_n, t1v, t2v, OP.add)
        p.tt(t1v, a_c, wv3[2], OP.mult)
        p.tt(t2v, b_c, wv3[3], OP.mult)
        p.tt(b_n, t1v, t2v, OP.add)
        cur, nxt = nxt, cur
    p.vcopy(M_bf[:, :], cur[:, :])


def bfly_high_cols(p, uw_h, dst):
    out = []
    for s in range(7, 12):
        cols = []
        npair = 4096 >> (s + 1)
        tstep = 1 << (s - 7)
        pairs = [ta for ta in range(32) if (ta // tstep) % 2 == 0]
        for pi, ta in enumerate(pairs):
            i0 = 128 * ta
            k0 = ((i0 >> (s + 1)) << s) | (i0 & ((1 << s) - 1))
            w4 = p.sb([128, 4], F32, "bw")
            p.dma(w4[:, :], uw_h[s:s + 1, k0:k0 + 128, :, :].rearrange("s k a b -> (s k) (a b)"))
            cols.append((ta, ta + tstep, w4))
        out.append(cols)
    return out


def bfly_high_apply(p, bufs_a, bufs_b, cols, W, tmp):
    cur, nxt = bufs_a, bufs_b
    for si in range(5):
        for (ta, tb, w4) in cols[si]:
            p.ts(tmp[:, :W], cur[tb][:, :W], w4[:, 1:2], OP.mult)
            p.stt(nxt[ta][:, :W], cur[ta][:, :W], w4[:, 0:1], tmp[:, :W], OP.mult, OP.add)
            p.ts(tmp[:, :W], cur[ta][:, :W], w4[:, 2:3], OP.mult)
            p.stt(nxt[tb][:, :W], cur[tb][:, :W], w4[:, 3:4], tmp[:, :W], OP.mult, OP.add)
        cur, nxt = nxt, cur
    return cur, nxt


def build_l0_ffn(p=None, c=None, xin=None, yout=None):
    standalone = p is None
    if standalone:
        p = Prog()
        c = load_consts(p)
    F = 4096
    NF = F // 128
    x_h = xin if xin is not None else p.dram("x", [T, D], F32)
    y_h = yout if yout is not None else p.dram("y", [T, D], F32, kind="ExternalOutput")
    wexp_h = p.dram("W_exp", [D, F], F32)
    bexp_h = p.dram("b_exp", [F], F32)
    uw_h = p.dram("up_w", [12, 2048, 2, 2], F32)
    ub_h = p.dram("up_b", [F], F32)
    dw_h = p.dram("down_w", [12, 2048, 2, 2], F32)
    wcon_h = p.dram("W_con", [F, D], F32)
    bcon_h = p.dram("b_con", [D], F32)
    db_h = p.dram("down_b", [D], F32)
    basis_h = p.dram("c_basis", [128, F], F32)

    p._prep = p.ps([128, 512], F32, "prep")

    Mup = p.sb([128, F], BF16, "Mup")
    Mdn = p.sb([128, F], BF16, "Mdn")
    wrow = p.sb([1, 2048], F32, "wrow", grp="bld")
    wrep = [p.sb([128, 2048], BF16, "wrep", grp="bld") for _ in range(4)]
    Xb = p.sb([128, F], F32, "Xb", grp="bld")
    Xb2 = p.sb([128, F], F32, "Xb2", grp="bld")
    t1 = p.sb([128, 2048], BF16, "bt1", grp="bld")
    t2 = p.sb([128, 2048], BF16, "bt2", grp="bld")
    p.dma(Xb[:, :], basis_h[:, :])
    bfly_build_M(p, c, uw_h, Mup, wrow, wrep, Xb, Xb2, t1, t2)
    p.dma(Xb[:, :], basis_h[:, :])
    bfly_build_M(p, c, dw_h, Mdn, wrow, wrep, Xb, Xb2, t1, t2)
    p.free_grp("bld")

    fscr = p.sb([128, 3584], F32, "fscr")
    wexp = [p.sb([128, F], BF16, "wexp") for _ in range(ND)]
    for k in range(ND):
        for hh in range(2):
            p.dma(fscr[:, :2048], wexp_h[k * 128:(k + 1) * 128, hh * 2048:(hh + 1) * 2048])
            p.vcopy(wexp[k][:, hh * 2048:(hh + 1) * 2048], fscr[:, :2048])
    wcon = [p.sb([128, D], BF16, "wcon") for _ in range(NF)]
    for k in range(NF):
        p.dma(fscr[:, :D], wcon_h[k * 128:(k + 1) * 128, :])
        p.vcopy(wcon[k][:, :], fscr[:, :D])

    upcols = bfly_high_cols(p, uw_h, None)
    dncols = bfly_high_cols(p, dw_h, None)

    bexp_c = p.sb([128, NF], F32, "bexpc")
    ub_c = p.sb([128, NF], F32, "ubc")
    for ft in range(NF):
        p.dma(bexp_c[:, ft:ft + 1], bexp_h[ft * 128:(ft + 1) * 128].rearrange("(p one) -> p one", one=1))
        p.dma(ub_c[:, ft:ft + 1], ub_h[ft * 128:(ft + 1) * 128].rearrange("(p one) -> p one", one=1))
    srow = p.sb([1, 512], F32, "srow")
    bc_rep = p.sb([128, D], BF16, "bcrep")
    for j in range(2):
        p.dma(srow[:, :], bcon_h[j * 512:(j + 1) * 512].rearrange("(one n) -> one n", one=1))
        p.mm(p._prep[:, :], c["ones_sb"][:, :], srow[:, :], start=True, stop=False)
        p.dma(srow[:, :], db_h[j * 512:(j + 1) * 512].rearrange("(one n) -> one n", one=1))
        p.mm(p._prep[:, :], c["ones_sb"][:, :], srow[:, :], start=False, stop=True)
        p.scopy(bc_rep[:, j * 512:(j + 1) * 512], p._prep[:, :])

    W = 256
    lnT = [p.sb([128, W], BF16, "lnT") for _ in range(ND)]
    bufA = [p.sb([128, W], BF16, "bufA") for _ in range(NF)]
    bufB = [p.sb([128, W], BF16, "bufB") for _ in range(NF)]
    htmp = p.sb([128, W], BF16, "htmp")
    xt = fscr[:, 0:1024]
    tmp = fscr[:, 1024:2048]
    xres = fscr[:, 2048:3072]
    ztmp = fscr[:, 3072:3584]
    mcol = p.sb([128, 1], F32, "m")
    vcol = p.sb([128, 1], F32, "v")
    lnb = p.sb([128, D], BF16, "lnb")
    pbf = p.ps([128, 128], BF16, "ptr")
    ph = p.ps([128, W], F32, "ph")
    pz = p.ps([128, 512], F32, "pz")

    for ch in range(T // W):
        for tt_ in range(W // 128):
            t = ch * (W // 128) + tt_
            p.dma(xt, x_h[t * 128:(t + 1) * 128, :])
            layernorm_tile(p, lnb, xt, tmp, mcol, vcol)
            transpose_nat_to_T(p, c, lnT, lnb, tt_, pbf)
        for ft in range(NF):
            for k in range(ND):
                p.mm(ph[:, :], wexp[k][:, ft * 128:(ft + 1) * 128], lnT[k][:, :],
                     start=(k == 0), stop=(k == ND - 1), wait=(k == 0))
            p.ts(bufB[ft][:, :], ph[:, :], bexp_c[:, ft:ft + 1], OP.add)
        for ft in range(NF):
            p.mm(ph[:, :], Mup[:, ft * 128:(ft + 1) * 128], bufB[ft][:, :])
            p.scopy(bufA[ft][:, :], ph[:, :])
        cur, nxt = bfly_high_apply(p, bufA, bufB, upcols, W, htmp)
        for ft in range(NF):
            p.act(nxt[ft][:, :], cur[ft][:, :], AF.Gelu, bias=ub_c[:, ft:ft + 1])
        cur, nxt = nxt, cur
        for ft in range(NF):
            p.mm(ph[:, :], Mdn[:, ft * 128:(ft + 1) * 128], cur[ft][:, :])
            p.scopy(nxt[ft][:, :], ph[:, :])
        cur, nxt = bfly_high_apply(p, nxt, cur, dncols, W, htmp)
        for tt_ in range(W // 128):
            t = ch * (W // 128) + tt_
            p.dma(xres, x_h[t * 128:(t + 1) * 128, :])
            for dc in range(2):
                dcs = slice(dc * 512, (dc + 1) * 512)
                for k in range(NF):
                    p.mm(pz[:, :], cur[k][:, tt_ * 128:(tt_ + 1) * 128], wcon[k][:, dcs],
                         start=(k == 0), stop=(k == NF - 1), wait=(k == 0))
                p.tt(ztmp, pz[:, :], bc_rep[:, dcs], OP.add)
                p.tt(ztmp, ztmp, xres[:, dcs], OP.add)
                p.dma(y_h[t * 128:(t + 1) * 128, dcs], ztmp)
    return p.build() if standalone else None


def run_l0_ffn(x, inp):
    nc = get_prog("l0_ffn", build_l0_ffn)
    cmn = const_inputs()
    basis = np.zeros((128, 4096), np.float32)
    for j in range(128):
        basis[j, j::128] = 1.0
    maps = []
    for i in range(NCORE):
        m = dict(cmn)
        m["c_basis"] = basis
        m["x"] = np.ascontiguousarray(x[i])
        m["W_exp"] = inp["L0_ffn_W_exp"]
        m["b_exp"] = inp["L0_ffn_b_exp"]
        m["up_w"] = inp["L0_ffn_up_weights"]
        m["up_b"] = inp["L0_ffn_up_bias"]
        m["down_w"] = inp["L0_ffn_down_weights"]
        m["W_con"] = inp["L0_ffn_W_con"]
        m["b_con"] = inp["L0_ffn_b_con"]
        m["down_b"] = inp["L0_ffn_down_bias"]
        maps.append(m)
    res = run_bass_kernel_spmd(nc, maps, core_ids=list(range(NCORE)))
    return np.stack([r["y"] for r in res.results], axis=0)



def load_w_bf16(p, dst_tiles, w_h, rows, cols, fscr):
    for k in range(rows // 128):
        for hh in range(0, cols, 2048):
            wdt = min(2048, cols - hh)
            p.dma(fscr[:, :wdt], w_h[k * 128:(k + 1) * 128, hh:hh + wdt])
            p.vcopy(dst_tiles[k][:, hh:hh + wdt], fscr[:, :wdt])


def load_w_bf16_T(p, c, dst_tiles, w_h, n, fscr, wbscr, pbf):
    nt = n // 128
    for k in range(nt):
        p.dma(fscr[:, :n], w_h[k * 128:(k + 1) * 128, :])
        p.vcopy(wbscr[:, :n], fscr[:, :n])
        for j in range(nt):
            p.trans(pbf[:, :128], wbscr[:, j * 128:(j + 1) * 128], c["id_sb"][:, :])
            p.vcopy(dst_tiles[j][:, k * 128:(k + 1) * 128], pbf[:, :128])


def build_attn(kind, p=None, c=None, xin=None, yout=None):
    standalone = p is None
    if standalone:
        p = Prog()
        c = load_consts(p)
    x_h = xin if xin is not None else p.dram("x", [T, D], F32)
    y_h = yout if yout is not None else p.dram("y", [T, D], F32, kind="ExternalOutput")
    H = 16
    HD = 64
    VA = HD + 1
    qT_hbm = p.dram("sc_qT", [D, T], BF16, kind="Internal")
    kT_hbm = p.dram("sc_kT", [D, T], BF16, kind="Internal")
    v_hbm = p.dram("sc_v", [T, H * VA], BF16, kind="Internal")
    if kind == "mla":
        wdkv_h = p.dram("W_DKV", [D, 256], F32)
        wuk_h = p.dram("W_UK", [256, D], F32)
        wuv_h = p.dram("W_UV", [256, D], F32)
        wuq_h = p.dram("W_UQ", [D, D], F32)
        wo_h = p.dram("W_O", [D, D], F32)
        mask_h = p.dram("c_diag", [128, 128], BF16)
    else:
        g_h = p.dram("ln_g", [D], F32)
        b_h = p.dram("ln_b", [D], F32)
        wq_h = p.dram("Wq", [D, D], F32)
        wk_h = p.dram("Wk", [D, D], F32)
        wv_h = p.dram("Wv", [D, D], F32)
        wo_h = p.dram("Wo", [D, D], F32)
        bq_h = p.dram("bq", [D], F32)
        bk_h = p.dram("bk", [D], F32)
        bv_h = p.dram("bv", [D], F32)
        bo_h = p.dram("bo", [D], F32)
        mu_h = p.dram("mu", [1, 1], F32)

    p._prep = p.ps([128, 512], F32, "prep")
    onat = [p.sb([128, D], BF16, "onat") for _ in range(NT)]
    wo = [p.sb([128, D], BF16, "wo") for _ in range(ND)]

    fscr = p.sb([128, 2048], F32, "fscr", grp="p1")
    if kind == "mla":
        wbscr = None
        wuq = [p.sb([128, D], BF16, "wuq", grp="p1") for _ in range(ND)]
        wdkv = [p.sb([128, 256], BF16, "wdkv", grp="p1") for _ in range(ND)]
        wuk = [p.sb([128, D], BF16, "wuk", grp="p1") for _ in range(2)]
        wuv = [p.sb([128, D], BF16, "wuv", grp="p1") for _ in range(2)]
        load_w_bf16(p, wuq, wuq_h, D, D, fscr)
        load_w_bf16(p, wdkv, wdkv_h, D, 256, fscr)
        load_w_bf16(p, wuk, wuk_h, 256, D, fscr)
        load_w_bf16(p, wuv, wuv_h, 256, D, fscr)
        load_w_bf16(p, wo, wo_h, D, D, fscr)
        ckvT = [p.sb([128, T], BF16, "ckvT", grp="p1") for _ in range(2)]
        grep = brep = None
        qb = kb = vb = None
    else:
        pbf0 = p.ps([128, 128], BF16, "ptr0", grp="p1")
        wbscr = p.sb([128, D], BF16, "wbscr", grp="p1")
        wqT = [p.sb([128, D], BF16, "wqT", grp="p1") for _ in range(ND)]
        wkT = [p.sb([128, D], BF16, "wkT", grp="p1") for _ in range(ND)]
        wvT = [p.sb([128, D], BF16, "wvT", grp="p1") for _ in range(ND)]
        load_w_bf16_T(p, c, wqT, wq_h, D, fscr, wbscr, pbf0)
        load_w_bf16_T(p, c, wkT, wk_h, D, fscr, wbscr, pbf0)
        load_w_bf16_T(p, c, wvT, wv_h, D, fscr, wbscr, pbf0)
        load_w_bf16_T(p, c, wo, wo_h, D, fscr, wbscr, pbf0)
        qb = p.sb([128, ND], F32, "qb", grp="p1")
        kb = p.sb([128, ND], F32, "kb", grp="p1")
        for ftl in range(ND):
            p.dma(qb[:, ftl:ftl + 1], bq_h[ftl * 128:(ftl + 1) * 128].rearrange("(p one) -> p one", one=1))
            p.dma(kb[:, ftl:ftl + 1], bk_h[ftl * 128:(ftl + 1) * 128].rearrange("(p one) -> p one", one=1))
        grep = p.sb([128, D], BF16, "grep", grp="p1")
        brep = p.sb([128, D], BF16, "brep", grp="p1")
        bvrep = p.sb([128, D], F32, "bvrep", grp="p1")
        srow = p.sb([1, 512], F32, "srow", grp="p1")
        for (rep, hh) in ((grep, g_h), (brep, b_h), (bvrep, bv_h)):
            for j in range(2):
                p.dma(srow[:, :], hh[j * 512:(j + 1) * 512].rearrange("(one n) -> one n", one=1))
                p.mm(p._prep[:, :], c["ones_sb"][:, :], srow[:, :])
                p.scopy(rep[:, j * 512:(j + 1) * 512], p._prep[:, :])

    lnT = [p.sb([128, 512], BF16, "lnTc", grp="p1") for _ in range(ND)]
    xt = p.sb([128, D], F32, "xt", grp="p1")
    tmp = p.sb([128, D], F32, "lntmp", grp="p1")
    t32 = p.sb([128, D], F32, "t32", grp="p1")
    mcol = p.sb([128, 1], F32, "m", grp="p1")
    vcol = p.sb([128, 1], F32, "v", grp="p1")
    lnb = p.sb([128, D], BF16, "lnb", grp="p1")
    pbf = p.ps([128, 128], BF16, "ptr", grp="p1")
    ph = p.ps([128, 512], F32, "ph", grp="p1")
    ev = p.sb([128, 512], BF16, "ev", grp="p1")
    vbuf = p.sb([128, H * VA], BF16, "vbuf", grp="p1")

    for ch in range(8):
        tok = slice(ch * 512, (ch + 1) * 512)
        for tt_ in range(4):
            t = ch * 4 + tt_
            p.dma(xt[:, :], x_h[t * 128:(t + 1) * 128, :])
            if kind == "mla":
                layernorm_tile(p, lnb, xt, tmp, mcol, vcol)
            else:
                layernorm_tile(p, t32, xt, tmp, mcol, vcol)
                layernorm_tile(p, lnb, t32, tmp, mcol, vcol,
                               gamma_rep=grep, beta_rep=brep, eps=1e-6)
            transpose_nat_to_T(p, c, lnT, lnb, tt_, pbf)
        if kind == "mla":
            for ntl in range(ND):
                for k in range(ND):
                    p.mm(ph[:, :], wuq[k][:, ntl * 128:(ntl + 1) * 128], lnT[k][:, :],
                         start=(k == 0), stop=(k == ND - 1), wait=(k == 0))
                p.scopy(ev[:, :], ph[:, :])
                p.dma(qT_hbm[ntl * 128:(ntl + 1) * 128, tok], ev[:, :])
            for ntl in range(2):
                for k in range(ND):
                    p.mm(ph[:, :], wdkv[k][:, ntl * 128:(ntl + 1) * 128], lnT[k][:, :],
                         start=(k == 0), stop=(k == ND - 1), wait=(k == 0))
                p.scopy(ckvT[ntl][:, tok], ph[:, :])
        else:
            for (wt, bias, dst_h) in ((wqT, qb, qT_hbm), (wkT, kb, kT_hbm)):
                for ntl in range(ND):
                    for k in range(ND):
                        p.mm(ph[:, :], wt[k][:, ntl * 128:(ntl + 1) * 128], lnT[k][:, :],
                             start=(k == 0), stop=(k == ND - 1), wait=(k == 0))
                    p.ts(ev[:, :], ph[:, :], bias[:, ntl:ntl + 1], OP.add)
                    p.dma(dst_h[ntl * 128:(ntl + 1) * 128, tok], ev[:, :])
            for tt_ in range(4):
                t = ch * 4 + tt_
                for nc2 in range(2):
                    for k in range(ND):
                        p.mm(ph[:, :], lnT[k][:, tt_ * 128:(tt_ + 1) * 128],
                             wvT[k][:, nc2 * 512:(nc2 + 1) * 512],
                             start=(k == 0), stop=(k == ND - 1), wait=(k == 0))
                    for h8 in range(8):
                        hd_ = nc2 * 8 + h8
                        p.tt(vbuf[:, hd_ * VA:hd_ * VA + HD], ph[:, h8 * 64:(h8 + 1) * 64],
                             bvrep[:, hd_ * 64:(hd_ + 1) * 64], OP.add)
                p.memset(vbuf.rearrange("p (h va) -> p h va", va=VA)[:, :, HD:HD + 1], 1.0)
                p.dma(v_hbm[t * 128:(t + 1) * 128, :], vbuf[:, :])
    if kind == "mla":
        for ntl in range(ND):
            for tc in range(8):
                for k in range(2):
                    p.mm(ph[:, :], wuk[k][:, ntl * 128:(ntl + 1) * 128],
                         ckvT[k][:, tc * 512:(tc + 1) * 512],
                         start=(k == 0), stop=(k == 1), wait=(k == 0))
                p.scopy(ev[:, :], ph[:, :])
                p.dma(kT_hbm[ntl * 128:(ntl + 1) * 128, tc * 512:(tc + 1) * 512], ev[:, :])
        for t in range(NT):
            for nc2 in range(2):
                for k in range(2):
                    p.mm(ph[:, :], ckvT[k][:, t * 128:(t + 1) * 128],
                         wuv[k][:, nc2 * 512:(nc2 + 1) * 512],
                         start=(k == 0), stop=(k == 1), wait=(k == 0))
                for h8 in range(8):
                    hd_ = nc2 * 8 + h8
                    p.vcopy(vbuf[:, hd_ * VA:hd_ * VA + HD], ph[:, h8 * 64:(h8 + 1) * 64])
            p.memset(vbuf.rearrange("p (h va) -> p h va", va=VA)[:, :, HD:HD + 1], 1.0)
            p.dma(v_hbm[t * 128:(t + 1) * 128, :], vbuf[:, :])
    p.free_grp("p1")

    kTh = p.sb([64, T], BF16, "kTh", grp="p2")
    qTh = p.sb([64, T], BF16, "qTh", grp="p2")
    vth = [p.sb([128, VA], BF16, "vth", grp="p2") for _ in range(NT)]
    PTs = p.sb([128, 512], BF16, "PTs", grp="p2")
    pS = p.ps([128, 512], F32, "pS", grp="p2")
    lrec = p.sb([128, 1], F32, "lrec", grp="p2")
    if kind == "mla":
        mask_sb = p.sb([128, 128], BF16, "msk", grp="p2")
        p.dma(mask_sb[:, :], mask_h[:, :])
        po = [p.ps([128, VA], F32, "po", grp="p2") for _ in range(4)]
        scale = 0.125
    else:
        macol = p.sb([128, 1], F32, "macol", grp="p2")
        murow = p.sb([1, 1], F32, "murow", grp="p2")
        p.dma(murow[:, :], mu_h[:, :])
        p.mm(p._prep[:, 0:1], c["ones_sb"][:, :], murow[:, :])
        p.act(macol[:, :], p._prep[:, 0:1], AF.Copy, scale=-1.0)
        sig = p.sb([128, 512], BF16, "sig", grp="p2")
        pot = p.ps([65, 512], F32, "pot", grp="p2")
        otu = p.sb([65, 512], BF16, "otu", grp="p2")
        pon = p.ps([128, VA], BF16, "pon", grp="p2")
        scale = 0.125

    for h in range(H):
        hrow = (h % 2) * 64
        ktile = h // 2
        p.dma(kTh[:, :], kT_hbm[ktile * 128 + hrow:ktile * 128 + hrow + 64, :])
        p.dma(qTh[:, :], qT_hbm[ktile * 128 + hrow:ktile * 128 + hrow + 64, :])
        for t in range(NT):
            p.dma(vth[t][:, :], v_hbm[t * 128:(t + 1) * 128, h * VA:(h + 1) * VA])
        for ic in range(8):
            njt = 4 * ic + 4 if kind == "mla" else NT
            if kind == "mla":
                for jt in range(njt):
                    p.mm(pS[:, :], kTh[:, jt * 128:(jt + 1) * 128], qTh[:, ic * 512:(ic + 1) * 512], sgc=True)
                    p.act(PTs[:, :], pS[:, :], AF.Exp, scale=scale)
                    if jt >= 4 * ic:
                        it4 = jt - 4 * ic
                        p.tt(PTs[:, it4 * 128:(it4 + 1) * 128], PTs[:, it4 * 128:(it4 + 1) * 128],
                             mask_sb[:, :], OP.mult)
                    first_pv = True
                    for it4 in range(4):
                        itile = 4 * ic + it4
                        if jt <= itile:
                            p.mm(po[it4][:, :], PTs[:, it4 * 128:(it4 + 1) * 128], vth[jt][:, :],
                                 start=(jt == 0), stop=(jt == itile), wait=first_pv, sgc=True)
                            first_pv = False
                for it4 in range(4):
                    itile = 4 * ic + it4
                    p.recip(lrec[:, :], po[it4][:, HD:HD + 1])
                    p.ts(onat[itile][:, h * HD:(h + 1) * HD], po[it4][:, 0:HD], lrec[:, 0:1], OP.mult)
            else:
                for jt in range(njt):
                    p.mm(pS[:, :], kTh[:, jt * 128:(jt + 1) * 128], qTh[:, ic * 512:(ic + 1) * 512], sgc=True)
                    p.act(PTs[:, :], pS[:, :], AF.Exp, scale=scale)
                    p.act(sig[:, :], pS[:, :], AF.Sigmoid, bias=macol[:, 0:1], scale=scale)
                    p.tt(PTs[:, :], PTs[:, :], sig[:, :], OP.mult)
                    p.mm(pot[:, :], vth[jt][:, :], PTs[:, :],
                         start=(jt == 0), stop=(jt == njt - 1), wait=True, sgc=True)
                p.vcopy(otu[:, :], pot[:, :])
                for it4 in range(4):
                    itile = 4 * ic + it4
                    p.trans(pon[:, :], otu[:, it4 * 128:(it4 + 1) * 128], c["id_sb"][0:VA, 0:VA])
                    p.recip(lrec[:, :], pon[0:128, HD:HD + 1])
                    p.ts(onat[itile][:, h * HD:(h + 1) * HD], pon[0:128, 0:HD], lrec[:, 0:1], OP.mult)
    p.free_grp("p2")

    xres = p.sb([128, D], F32, "xres", grp="p3")
    ztmp = p.sb([128, 512], F32, "zt3", grp="p3")
    otp = p.sb([128, 128], BF16, "otp", grp="p3")
    pbf3 = p.ps([128, 128], BF16, "ptr3", grp="p3")
    py = [p.ps([128, 512], F32, "py", grp="p3") for _ in range(2)]
    borep = None
    if kind != "mla":
        borep = p.sb([128, D], F32, "borep", grp="p3")
        srow3 = p.sb([1, 512], F32, "srow3", grp="p3")
        for j in range(2):
            p.dma(srow3[:, :], bo_h[j * 512:(j + 1) * 512].rearrange("(one n) -> one n", one=1))
            p.mm(p._prep[:, :], c["ones_sb"][:, :], srow3[:, :])
            p.scopy(borep[:, j * 512:(j + 1) * 512], p._prep[:, :])
    for t in range(NT):
        p.dma(xres[:, :], x_h[t * 128:(t + 1) * 128, :])
        for hv in range(ND):
            p.trans(pbf3[:, :], onat[t][:, hv * 128:(hv + 1) * 128], c["id_sb"][:, :])
            p.vcopy(otp[:, :], pbf3[:, :])
            for dc in range(2):
                p.mm(py[dc][:, :], otp[:, :], wo[hv][:, dc * 512:(dc + 1) * 512],
                     start=(hv == 0), stop=(hv == ND - 1), wait=(dc == 0), sgc=True)
        for dc in range(2):
            dcs = slice(dc * 512, (dc + 1) * 512)
            if borep is not None:
                p.tt(ztmp[:, :], py[dc][:, :], borep[:, dcs], OP.add)
                p.tt(ztmp[:, :], ztmp[:, :], xres[:, dcs], OP.add)
            else:
                p.tt(ztmp[:, :], py[dc][:, :], xres[:, dcs], OP.add)
            p.dma(y_h[t * 128:(t + 1) * 128, dcs], ztmp[:, :])
    return p.build() if standalone else None


def run_attn(kind, x, inp):
    nc = get_prog("attn_" + kind, lambda: build_attn(kind))
    cmn = const_inputs()
    maps = []
    for i in range(NCORE):
        m = dict(cmn)
        m["x"] = np.ascontiguousarray(x[i])
        if kind == "mla":
            jj = np.arange(128)
            m["c_diag"] = (jj[:, None] <= jj[None, :]).astype(bf16_np())
            m["W_DKV"] = inp["L1_attn_a_W_DKV"]
            m["W_UK"] = inp["L1_attn_a_W_UK"]
            m["W_UV"] = inp["L1_attn_a_W_UV"]
            m["W_UQ"] = inp["L1_attn_a_W_UQ"]
            m["W_O"] = inp["L1_attn_a_W_O"]
        else:
            m["ln_g"] = inp["L1_attn_b_ln_gamma"]
            m["ln_b"] = inp["L1_attn_b_ln_beta"]
            m["Wq"] = inp["L1_attn_b_Wq"]
            m["Wk"] = inp["L1_attn_b_Wk"]
            m["Wv"] = inp["L1_attn_b_Wv"]
            m["Wo"] = inp["L1_attn_b_Wo"]
            m["bq"] = inp["L1_attn_b_bq"]
            m["bk"] = inp["L1_attn_b_bk"]
            m["bv"] = inp["L1_attn_b_bv"]
            m["bo"] = inp["L1_attn_b_bo"]
            m["mu"] = np.asarray(inp["L1_attn_b_mu"], np.float32).reshape(1, 1)
        maps.append(m)
    res = run_bass_kernel_spmd(nc, maps, core_ids=list(range(NCORE)))
    return np.stack([r["y"] for r in res.results], axis=0)



def build_l0_attn_b(p=None, c=None, xin=None, yout=None):
    standalone = p is None
    if standalone:
        p = Prog()
        c = load_consts(p)
    x_h = xin if xin is not None else p.dram("x", [T, D], F32)
    y_h = yout if yout is not None else p.dram("y", [T, D], F32, kind="ExternalOutput")
    tmk_h = p.dram("tmk", [1, 1, D], F32)
    tmv_h = p.dram("tmv", [1, 1, D], F32)
    tmr_h = p.dram("tmr", [1, 1, D], F32)
    tmg_h = p.dram("tmg", [1, 1, D], F32)
    td_h = p.dram("td", [16], F32)
    tf_h = p.dram("tf", [16], F32)
    wr_h = p.dram("W_r", [D, D], F32)
    wk_h = p.dram("W_k", [D, D], F32)
    wv_h = p.dram("W_v", [D, D], F32)
    wg_h = p.dram("W_g", [D, D], F32)
    wo_h = p.dram("W_o", [D, D], F32)
    lnw_h = p.dram("ln_w", [D], F32)
    lnb_h = p.dram("ln_b", [D], F32)
    e1_h = p.dram("c_e1", [128, 128], F32)
    su_h = p.dram("c_su", [128, 128], F32)
    eye_h = p.dram("c_eye", [128, 128], F32)
    iota_h = p.dram("c_iota", [128, 2], F32)

    rT_hbm = p.dram("sc_rT", [D, T], BF16, kind="Internal")
    kT_hbm = p.dram("sc_kT", [D, T], BF16, kind="Internal")
    kn_hbm = p.dram("sc_kn", [T, D], BF16, kind="Internal")
    vn_hbm = p.dram("sc_vn", [T, D], BF16, kind="Internal")
    gn_hbm = p.dram("sc_gn", [T, D], BF16, kind="Internal")

    p._prep = p.ps([128, 512], F32, "prep")
    H = 16
    K = 64

    ynat = [p.sb([128, D], BF16, "ynat") for _ in range(NT)]
    woT = [p.sb([128, D], BF16, "woT") for _ in range(ND)]
    lnwrep = p.sb([128, D], F32, "lnwrep")
    lnbrep = p.sb([128, D], F32, "lnbrep")
    srow0 = p.sb([1, 512], F32, "srow0")
    for (rep, hh) in ((lnwrep, lnw_h), (lnbrep, lnb_h)):
        for j in range(2):
            p.dma(srow0[:, :], hh[j * 512:(j + 1) * 512].rearrange("(one n) -> one n", one=1))
            p.mm(p._prep[:, :], c["ones_sb"][:, :], srow0[:, :])
            p.scopy(rep[:, j * 512:(j + 1) * 512], p._prep[:, :])

    fscr = p.sb([128, D], F32, "fscr", grp="p1")
    wbscr = p.sb([128, D], BF16, "wbscr", grp="p1")
    pbf0 = p.ps([128, 128], BF16, "ptr0", grp="p1")
    wrT = [p.sb([128, D], BF16, "wrT", grp="p1") for _ in range(ND)]
    wkT = [p.sb([128, D], BF16, "wkT", grp="p1") for _ in range(ND)]
    wvT = [p.sb([128, D], BF16, "wvT", grp="p1") for _ in range(ND)]
    wgT = [p.sb([128, D], BF16, "wgT", grp="p1") for _ in range(ND)]
    load_w_bf16_T(p, c, wrT, wr_h, D, fscr, wbscr, pbf0)
    load_w_bf16_T(p, c, wkT, wk_h, D, fscr, wbscr, pbf0)
    load_w_bf16_T(p, c, wvT, wv_h, D, fscr, wbscr, pbf0)
    load_w_bf16_T(p, c, wgT, wg_h, D, fscr, wbscr, pbf0)
    load_w_bf16_T(p, c, woT, wo_h, D, fscr, wbscr, pbf0)
    tmcols = {}
    for nm, hh in (("r", tmr_h), ("k", tmk_h), ("v", tmv_h), ("g", tmg_h)):
        col = p.sb([128, ND], F32, "tmc", grp="p1")
        for ftl in range(ND):
            p.dma(col[:, ftl:ftl + 1],
                  hh.rearrange("a b d -> (a b) d")[0:1, ftl * 128:(ftl + 1) * 128]
                  .rearrange("one (pp o) -> (one pp) o", o=1))
        tmcols[nm] = col

    lnT = [p.sb([128, 512], BF16, "lnTc", grp="p1") for _ in range(ND)]
    xmT = [p.sb([128, 512], BF16, "xmT", grp="p1") for _ in range(ND)]
    dcol = p.sb([128, 512], F32, "dcol", grp="p1")
    pcol = p.sb([128, ND], BF16, "pcol", grp="p1")
    xt = p.sb([128, D], F32, "xt", grp="p1")
    tmp = p.sb([128, D], F32, "lntmp", grp="p1")
    mcol = p.sb([128, 1], F32, "m", grp="p1")
    vcol = p.sb([128, 1], F32, "v", grp="p1")
    lnb2 = p.sb([128, D], BF16, "lnb2", grp="p1")
    pbf = p.ps([128, 128], BF16, "ptr", grp="p1")
    ph = p.ps([128, 512], F32, "ph", grp="p1")
    ev = p.sb([128, 512], BF16, "ev", grp="p1")
    p.memset(pcol[:, :], 0.0)

    for ch in range(8):
        tok = slice(ch * 512, (ch + 1) * 512)
        for tt_ in range(4):
            t = ch * 4 + tt_
            p.dma(xt[:, :], x_h[t * 128:(t + 1) * 128, :])
            layernorm_tile(p, lnb2, xt, tmp, mcol, vcol)
            transpose_nat_to_T(p, c, lnT, lnb2, tt_, pbf)
        for nm, dsts in (("r", ("rT",)), ("k", ("kT", "kn")), ("v", ("vn",)), ("g", ("gn",))):
            tmc = tmcols[nm]
            for k in range(ND):
                p.tt(dcol[:, 1:512], lnT[k][:, 1:512], lnT[k][:, 0:511], OP.subtract)
                p.stt(xmT[k][:, 1:512], dcol[:, 1:512], tmc[:, k:k + 1], lnT[k][:, 0:511],
                      OP.mult, OP.add)
                p.tt(dcol[:, 0:1], lnT[k][:, 0:1], pcol[:, k:k + 1], OP.subtract)
                p.stt(xmT[k][:, 0:1], dcol[:, 0:1], tmc[:, k:k + 1], pcol[:, k:k + 1],
                      OP.mult, OP.add)
            if nm == "g":
                for k in range(ND):
                    p.vcopy(pcol[:, k:k + 1], lnT[k][:, 511:512])
            wT = {"r": wrT, "k": wkT, "v": wvT, "g": wgT}[nm]
            for d in dsts:
                if d in ("rT", "kT"):
                    hbm = rT_hbm if d == "rT" else kT_hbm
                    for ntl in range(ND):
                        for k in range(ND):
                            p.mm(ph[:, :], wT[k][:, ntl * 128:(ntl + 1) * 128], xmT[k][:, :],
                                 start=(k == 0), stop=(k == ND - 1), wait=(k == 0))
                        p.scopy(ev[:, :], ph[:, :])
                        p.dma(hbm[ntl * 128:(ntl + 1) * 128, tok], ev[:, :])
                else:
                    hbm = {"kn": kn_hbm, "vn": vn_hbm, "gn": gn_hbm}[d]
                    for tt_ in range(4):
                        t = ch * 4 + tt_
                        for nc2 in range(2):
                            for k in range(ND):
                                p.mm(ph[:, :], xmT[k][:, tt_ * 128:(tt_ + 1) * 128],
                                     wT[k][:, nc2 * 512:(nc2 + 1) * 512],
                                     start=(k == 0), stop=(k == ND - 1), wait=(k == 0))
                            if d == "gn":
                                p.act(ev[:, :], ph[:, :], AF.Silu)
                            else:
                                p.scopy(ev[:, :], ph[:, :])
                            p.dma(hbm[t * 128:(t + 1) * 128, nc2 * 512:(nc2 + 1) * 512], ev[:, :])
    p.free_grp("p1")

    e1c = p.sb([128, 128], F32, "e1c", grp="p2")
    suc = p.sb([128, 128], F32, "suc", grp="p2")
    eyec = p.sb([128, 128], F32, "eyec", grp="p2")
    iotac = p.sb([128, 2], F32, "iotac", grp="p2")
    p.dma(e1c[:, :], e1_h[:, :])
    p.dma(suc[:, :], su_h[:, :])
    p.dma(eyec[:, :], eye_h[:, :])
    p.dma(iotac[:, :], iota_h[:, :])
    tdrow = p.sb([1, 16], F32, "tdrow", grp="p2")
    tfrow = p.sb([1, 16], F32, "tfrow", grp="p2")
    p.dma(tdrow[:, :], td_h.rearrange("(one h) -> one h", one=1))
    p.dma(tfrow[:, :], tf_h.rearrange("(one h) -> one h", one=1))
    ldrep = p.sb([128, 16], F32, "ldrep", grp="p2")
    urep = p.sb([128, 16], F32, "urep", grp="p2")
    p.mm(p._prep[:, 0:16], c["ones_sb"][:, :], tdrow[:, :])
    p.act(ldrep[:, :], p._prep[:, 0:16], AF.Exp)
    p.ts(ldrep[:, :], ldrep[:, :], -1.0, OP.mult)
    p.mm(p._prep[:, 0:16], c["ones_sb"][:, :], tfrow[:, :])
    p.scopy(urep[:, :], p._prep[:, 0:16])

    wmT = p.sb([128, 128], BF16, "wmT", grp="p2")
    wmt1 = p.sb([128, 128], F32, "wmt1", grp="p2")
    wbcol = p.sb([128, 1], F32, "wbcol", grp="p2")
    wkcol = p.sb([128, 1], F32, "wkcol", grp="p2")
    wscol = p.sb([64, 1], F32, "wscol", grp="p2")
    s_st = p.sb([64, 128], F32, "s_st", grp="p2")
    s_bf = p.sb([64, K], BF16, "s_bf", grp="p2")
    kwk = p.sb([128, K], BF16, "kwk", grp="p2")
    PTc = p.sb([128, 128], BF16, "PTc", grp="p2")
    y1 = p.sb([128, K], F32, "y1", grp="p2")
    y2 = p.sb([128, K], F32, "y2", grp="p2")
    gm = p.sb([128, 1], F32, "gm", grp="p2")
    gv = p.sb([128, 1], F32, "gv", grp="p2")
    rTh = p.sb([64, T], BF16, "rTh", grp="p2")
    kTh = p.sb([64, T], BF16, "kTh", grp="p2")
    knh = [p.sb([128, K], BF16, "knh", grp="p2") for _ in range(NT)]
    vnh = [p.sb([128, K], BF16, "vnh", grp="p2") for _ in range(NT)]
    pA = p.ps([128, 128], F32, "pA", grp="p2")
    pO = p.ps([128, K], F32, "pO", grp="p2")
    pC = p.ps([128, K], F32, "pC", grp="p2")
    pS3 = p.ps([64, K], F32, "pS3", grp="p2")

    for h in range(H):
        hrow = (h % 2) * 64
        ktile = h // 2
        p.dma(rTh[:, :], rT_hbm[ktile * 128 + hrow:ktile * 128 + hrow + 64, :])
        p.dma(kTh[:, :], kT_hbm[ktile * 128 + hrow:ktile * 128 + hrow + 64, :])
        for t in range(NT):
            p.dma(knh[t][:, :], kn_hbm[t * 128:(t + 1) * 128, h * K:(h + 1) * K])
            p.dma(vnh[t][:, :], vn_hbm[t * 128:(t + 1) * 128, h * K:(h + 1) * K])
        ldc = ldrep[:, h:h + 1]
        p.ts(wmt1[:, :], e1c[:, :], ldc, OP.mult)
        p.act(wmt1[:, :], wmt1[:, :], AF.Exp)
        p.tt(wmt1[:, :], wmt1[:, :], suc[:, :], OP.mult)
        p.stt(wmT[:, :], eyec[:, :], urep[:, h:h + 1], wmt1[:, :], OP.mult, OP.add)
        p.ts(wbcol[:, :], iotac[:, 0:1], ldc, OP.mult)
        p.act(wbcol[:, :], wbcol[:, :], AF.Exp)
        p.ts(wkcol[:, :], iotac[:, 1:2], ldc, OP.mult)
        p.act(wkcol[:, :], wkcol[:, :], AF.Exp)
        p.act(wscol[:, :], ldrep[0:64, h:h + 1], AF.Exp, scale=128.0)
        p.memset(s_st[:, 0:K], 0.0)
        p.memset(s_bf[:, :], 0.0)
        for t in range(NT):
            p.mm(pA[:, :], kTh[:, t * 128:(t + 1) * 128], rTh[:, t * 128:(t + 1) * 128], sgc=True)
            p.tt(PTc[:, :], pA[:, :], wmT[:, :], OP.mult)
            p.mm(pO[:, :], PTc[:, :], vnh[t][:, :], sgc=True)
            p.mm(pC[:, :], rTh[:, t * 128:(t + 1) * 128], s_bf[:, :], sgc=True)
            p.ts(y1[:, :], pC[:, :], wbcol[:, 0:1], OP.mult)
            p.tt(y1[:, :], y1[:, :], pO[:, :], OP.add)
            p.ts(kwk[:, :], knh[t][:, :], wkcol[:, 0:1], OP.mult)
            p.mm(pS3[:, :], kwk[:, :], vnh[t][:, :], sgc=True)
            p.ts(s_st[:, 0:K], s_st[:, 0:K], wscol[:, 0:1], OP.mult)
            p.tt(s_st[:, 0:K], s_st[:, 0:K], pS3[:, :], OP.add)
            p.vcopy(s_bf[:, :], s_st[:, 0:K])
            p.ts(y2[:, :], y1[:, :], 0.125, OP.mult)
            p.red(gm[:, 0:1], y2[:, :])
            p.ts(gm[:, 0:1], gm[:, 0:1], 1.0 / K, OP.mult)
            p.act(y1[:, :], y2[:, :], AF.Square)
            p.red(gv[:, 0:1], y1[:, :])
            p.ts(gv[:, 0:1], gv[:, 0:1], 1.0 / K, OP.mult)
            p.stt(gv[:, 0:1], gm[:, 0:1], gm[:, 0:1], gv[:, 0:1], OP.mult, OP.subtract)
            p.act(gv[:, 0:1], gv[:, 0:1], AF.Ln, bias=1e-5, scale=-1.0)
            p.act(gv[:, 0:1], gv[:, 0:1], AF.Exp, scale=-0.5)
            p.ts(y2[:, :], y2[:, :], gm[:, 0:1], OP.subtract)
            p.ts(y2[:, :], y2[:, :], gv[:, 0:1], OP.mult)
            p.tt(y2[:, :], y2[:, :], lnwrep[:, h * K:(h + 1) * K], OP.mult)
            p.tt(ynat[t][:, h * K:(h + 1) * K], y2[:, :], lnbrep[:, h * K:(h + 1) * K], OP.add)
    p.free_grp("p2")

    xres = p.sb([128, D], F32, "xres", grp="p3")
    gt = p.sb([128, D], BF16, "gt", grp="p3")
    yg = p.sb([128, D], BF16, "yg", grp="p3")
    ztmp = p.sb([128, 512], F32, "zt3", grp="p3")
    otp = p.sb([128, 128], BF16, "otp", grp="p3")
    pbf3 = p.ps([128, 128], BF16, "ptr3", grp="p3")
    py = [p.ps([128, 512], F32, "py", grp="p3") for _ in range(2)]
    for t in range(NT):
        p.dma(xres[:, :], x_h[t * 128:(t + 1) * 128, :])
        p.dma(gt[:, :], gn_hbm[t * 128:(t + 1) * 128, :])
        p.tt(yg[:, :], ynat[t][:, :], gt[:, :], OP.mult)
        for hv in range(ND):
            p.trans(pbf3[:, :], yg[:, hv * 128:(hv + 1) * 128], c["id_sb"][:, :])
            p.vcopy(otp[:, :], pbf3[:, :])
            for dc in range(2):
                p.mm(py[dc][:, :], otp[:, :], woT[hv][:, dc * 512:(dc + 1) * 512],
                     start=(hv == 0), stop=(hv == ND - 1), wait=(dc == 0), sgc=True)
        for dc in range(2):
            dcs = slice(dc * 512, (dc + 1) * 512)
            p.tt(ztmp[:, :], py[dc][:, :], xres[:, dcs], OP.add)
            p.dma(y_h[t * 128:(t + 1) * 128, dcs], ztmp[:, :])
    return p.build() if standalone else None


def run_l0_attn_b(x, inp):
    nc = get_prog("l0_attn_b", build_l0_attn_b)
    cmn = const_inputs()
    jj = np.arange(128)
    e1 = np.maximum(jj[None, :] - jj[:, None] - 1, 0).astype(np.float32)
    su = (jj[None, :] > jj[:, None]).astype(np.float32)
    iota = np.stack([jj.astype(np.float32), (127 - jj).astype(np.float32)], axis=1)
    iota = np.ascontiguousarray(iota)
    maps = []
    for i in range(NCORE):
        m = dict(cmn)
        m["c_e1"] = e1
        m["c_su"] = su
        m["c_eye"] = np.eye(128, dtype=np.float32)
        m["c_iota"] = iota
        m["x"] = np.ascontiguousarray(x[i])
        m["tmk"] = inp["L0_attn_b_time_mix_k"]
        m["tmv"] = inp["L0_attn_b_time_mix_v"]
        m["tmr"] = inp["L0_attn_b_time_mix_r"]
        m["tmg"] = inp["L0_attn_b_time_mix_g"]
        m["td"] = inp["L0_attn_b_time_decay"]
        m["tf"] = inp["L0_attn_b_time_faaaa"]
        m["W_r"] = inp["L0_attn_b_W_r"]
        m["W_k"] = inp["L0_attn_b_W_k"]
        m["W_v"] = inp["L0_attn_b_W_v"]
        m["W_g"] = inp["L0_attn_b_W_g"]
        m["W_o"] = inp["L0_attn_b_W_o"]
        m["ln_w"] = inp["L0_attn_b_ln_w"]
        m["ln_b"] = inp["L0_attn_b_ln_b"]
        maps.append(m)
    res = run_bass_kernel_spmd(nc, maps, core_ids=list(range(NCORE)))
    return np.stack([r["y"] for r in res.results], axis=0)



def build_l0_attn_a(p=None, c=None, xin=None, yout=None):
    standalone = p is None
    if standalone:
        p = Prog()
        c = load_consts(p)
    x_h = xin if xin is not None else p.dram("x", [T, D], F32)
    y_h = yout if yout is not None else p.dram("y", [T, D], F32, kind="ExternalOutput")
    wsel_h = p.dram("W_sel", [D, 160], F32)
    dtw_h = p.dram("dt_W", [32, 512], F32)
    dtb_h = p.dram("dt_b", [512], F32)
    wk_h = p.dram("Wk", [D, 512], F32)
    wv_h = p.dram("Wv", [D, 512], F32)
    wr_h = p.dram("Wr", [D, 512], F32)
    wg_h = p.dram("Wg", [D, 512], F32)
    wo_h = p.dram("Wo", [512, D], F32)
    bd_h = p.dram("base_decay", [8, 64], F32)
    tf_h = p.dram("time_first", [8, 64], F32)
    tri_h = p.dram("c_tri", [128, 128], F32)
    sl_h = p.dram("c_sl", [128, 128], F32)
    msk_h = p.dram("c_msk", [128, 128], BF16)
    rm_h = p.dram("c_rm", [128, 4], F32)
    sel8_h = p.dram("c_sel8", [8, 512], BF16)

    ld_hbm = p.dram("sc_ld", [T, 512], F32, kind="Internal")
    kn_hbm = p.dram("sc_kn", [T, 512], BF16, kind="Internal")
    kT_hbm = p.dram("sc_kT", [512, T], BF16, kind="Internal")
    vn_hbm = p.dram("sc_vn", [T, 512], BF16, kind="Internal")
    rg_hbm = p.dram("sc_rg", [T, 512], BF16, kind="Internal")
    ex_hbm = p.dram("sc_ex", [T, 512], BF16, kind="Internal")
    bs_hbm = p.dram("sc_bs", [T], F32, kind="Internal")

    wo = [p.sb([128, D], BF16, "wo") for _ in range(4)]

    prep = p.ps([128, 512], F32, "prep", grp="p1")
    fscr = p.sb([128, 512], F32, "fscr", grp="p1")
    wsel = [p.sb([128, 160], BF16, "wsel", grp="p1") for _ in range(ND)]
    wk = [p.sb([128, 512], BF16, "wk", grp="p1") for _ in range(ND)]
    wv = [p.sb([128, 512], BF16, "wv", grp="p1") for _ in range(ND)]
    wr = [p.sb([128, 512], BF16, "wr", grp="p1") for _ in range(ND)]
    wg = [p.sb([128, 512], BF16, "wg", grp="p1") for _ in range(ND)]
    for k in range(ND):
        p.dma(fscr[:, 0:160], wsel_h[k * 128:(k + 1) * 128, :])
        p.vcopy(wsel[k][:, :], fscr[:, 0:160])
        for (dst, hh) in ((wk, wk_h), (wv, wv_h), (wr, wr_h), (wg, wg_h)):
            p.dma(fscr[:, :], hh[k * 128:(k + 1) * 128, :])
            p.vcopy(dst[k][:, :], fscr[:, :])
    for k in range(4):
        p.dma(fscr[:, :], wo_h[k * 128:(k + 1) * 128, 0:512])
        p.vcopy(wo[k][:, 0:512], fscr[:, :])
        p.dma(fscr[:, :], wo_h[k * 128:(k + 1) * 128, 512:1024])
        p.vcopy(wo[k][:, 512:1024], fscr[:, :])
    dtw = p.sb([32, 512], BF16, "dtw", grp="p1")
    p.dma(fscr[0:32, :], dtw_h[:, :])
    p.vcopy(dtw[:, :], fscr[0:32, :])
    srow = p.sb([1, 512], F32, "srow", grp="p1")
    dtbrep = p.sb([128, 512], F32, "dtbrep", grp="p1")
    nbdrep = p.sb([128, 512], F32, "nbdrep", grp="p1")
    tfrep = p.sb([128, 512], BF16, "tfrep", grp="p1")
    p.dma(srow[:, :], dtb_h.rearrange("(one n) -> one n", one=1))
    p.mm(prep[:, :], c["ones_sb"][:, :], srow[:, :])
    p.scopy(dtbrep[:, :], prep[:, :])
    p.dma(srow[:, :], bd_h.rearrange("h k -> (h k)").rearrange("(one n) -> one n", one=1))
    p.mm(prep[:, :], c["ones_sb"][:, :], srow[:, :])
    p.act(nbdrep[:, :], prep[:, :], AF.Copy, scale=-1.0)
    p.dma(srow[:, :], tf_h.rearrange("h k -> (h k)").rearrange("(one n) -> one n", one=1))
    p.mm(prep[:, :], c["ones_sb"][:, :], srow[:, :])
    p.scopy(tfrep[:, :], prep[:, :])
    sel8 = p.sb([8, 512], BF16, "sel8", grp="p1")
    p.dma(sel8[:, :], sel8_h[:, :])

    lnT = [p.sb([128, 512], BF16, "lnTc", grp="p1") for _ in range(ND)]
    dtT = p.sb([32, T], BF16, "dtT", grp="p1")
    xt = p.sb([128, D], F32, "xt", grp="p1")
    tmp = p.sb([128, D], F32, "lntmp", grp="p1")
    mcol = p.sb([128, 1], F32, "m", grp="p1")
    vcol = p.sb([128, 1], F32, "v", grp="p1")
    lnb2 = p.sb([128, D], BF16, "lnb2", grp="p1")
    pbf = p.ps([128, 128], BF16, "ptr", grp="p1")
    ph = p.ps([128, 512], F32, "ph", grp="p1")
    pse = p.ps([128, 160], F32, "pse", grp="p1")
    ptr32 = p.ps([32, 128], BF16, "ptr32", grp="p1")
    bsc = p.sb([128, 1], F32, "bsc", grp="p1")
    dtrb = p.sb([128, 32], BF16, "dtrb", grp="p1")
    ktile = p.sb([128, 512], BF16, "ktile", grp="p1")
    vtile = p.sb([128, 512], BF16, "vtile", grp="p1")
    rs = p.sb([128, 512], BF16, "rs", grp="p1")
    gs = p.sb([128, 512], BF16, "gs", grp="p1")
    e1b = p.sb([128, 512], BF16, "e1b", grp="p1")
    sqf = p.sb([128, 512], F32, "sqf", grp="p1")
    rs8 = p.sb([128, 8], BF16, "rs8", grp="p1")
    rs8T = p.sb([8, 128], BF16, "rs8T", grp="p1")
    x32 = p.sb([128, 512], F32, "x32", grp="p1")
    ptr8 = p.ps([8, 128], BF16, "ptr8", grp="p1")

    for ch in range(8):
        tok = slice(ch * 512, (ch + 1) * 512)
        for tt_ in range(4):
            t = ch * 4 + tt_
            p.dma(xt[:, :], x_h[t * 128:(t + 1) * 128, :])
            layernorm_tile(p, lnb2, xt, tmp, mcol, vcol)
            transpose_nat_to_T(p, c, lnT, lnb2, tt_, pbf)
        for tt_ in range(4):
            t = ch * 4 + tt_
            tsl = slice(tt_ * 128, (tt_ + 1) * 128)
            for k in range(ND):
                p.mm(pse[:, :], lnT[k][:, tsl], wsel[k][:, :],
                     start=(k == 0), stop=(k == ND - 1), wait=(k == 0))
            p.red(bsc[:, 0:1], pse[:, 32:96])
            p.ts(bsc[:, 0:1], bsc[:, 0:1], 1.0 / 64.0, OP.mult)
            p.dma(bs_hbm[t * 128:(t + 1) * 128].rearrange("(pp o) -> pp o", o=1), bsc[:, 0:1])
            p.scopy(dtrb[:, :], pse[:, 0:32])
            p.trans(ptr32[:, :], dtrb[:, :], c["id_sb"][:, :])
            p.vcopy(dtT[:, t * 128:(t + 1) * 128], ptr32[:, :])
            for k in range(ND):
                p.mm(ph[:, :], lnT[k][:, tsl], wk[k][:, :],
                     start=(k == 0), stop=(k == ND - 1), wait=(k == 0))
            p.scopy(ktile[:, :], ph[:, :])
            p.dma(kn_hbm[t * 128:(t + 1) * 128, :], ktile[:, :])
            for k in range(ND):
                p.mm(ph[:, :], lnT[k][:, tsl], wv[k][:, :],
                     start=(k == 0), stop=(k == ND - 1), wait=(k == 0))
            p.scopy(vtile[:, :], ph[:, :])
            p.dma(vn_hbm[t * 128:(t + 1) * 128, :], vtile[:, :])
            for k in range(ND):
                p.mm(ph[:, :], lnT[k][:, tsl], wr[k][:, :],
                     start=(k == 0), stop=(k == ND - 1), wait=(k == 0))
            p.act(rs[:, :], ph[:, :], AF.Sigmoid)
            for k in range(ND):
                p.mm(ph[:, :], lnT[k][:, tsl], wg[k][:, :],
                     start=(k == 0), stop=(k == ND - 1), wait=(k == 0))
            p.act(gs[:, :], ph[:, :], AF.Sigmoid)
            p.tt(rs[:, :], rs[:, :], gs[:, :], OP.mult)
            p.dma(rg_hbm[t * 128:(t + 1) * 128, :], rs[:, :])
            p.tt(e1b[:, :], ktile[:, :], vtile[:, :], OP.mult)
            p.tt(e1b[:, :], e1b[:, :], tfrep[:, :], OP.mult)
            p.act(sqf[:, :], ktile[:, :], AF.Square)
            for h8 in range(8):
                p.red(mcol[:, 0:1], sqf[:, h8 * 64:(h8 + 1) * 64])
                p.ts(rs8[:, h8:h8 + 1], mcol[:, 0:1], bsc[:, 0:1], OP.mult)
            p.trans(ptr8[:, :], rs8[:, :], c["id_sb"][:, :])
            p.vcopy(rs8T[:, :], ptr8[:, :])
            p.mm(ph[:, :], rs8T[:, :], sel8[:, :], sgc=True)
            p.tt(x32[:, :], vtile[:, :], ph[:, :], OP.mult)
            p.tt(e1b[:, :], e1b[:, :], x32[:, :], OP.add)
            p.dma(ex_hbm[t * 128:(t + 1) * 128, :], e1b[:, :])
            p.mm(ph[:, :], dtT[:, t * 128:(t + 1) * 128], dtw[:, :], sgc=True)
            p.tt(x32[:, :], ph[:, :], dtbrep[:, :], OP.add)
            p.act(x32[:, :], x32[:, :], AF.Exp)
            p.act(x32[:, :], x32[:, :], AF.Ln, bias=1.0)
            p.tt(x32[:, :], x32[:, :], nbdrep[:, :], OP.mult)
            p.dma(ld_hbm[t * 128:(t + 1) * 128, :], x32[:, :])
        for ntl in range(4):
            for k in range(ND):
                p.mm(ph[:, :], wk[k][:, ntl * 128:(ntl + 1) * 128], lnT[k][:, :],
                     start=(k == 0), stop=(k == ND - 1), wait=(k == 0))
            p.scopy(x32[:, :], ph[:, :])
            p.vcopy(rs[:, :], x32[:, :])
            p.dma(kT_hbm[ntl * 128:(ntl + 1) * 128, tok], rs[:, :])
    p.free_grp("p1")

    tric = p.sb([128, 128], F32, "tric", grp="p2")
    slc = p.sb([128, 128], F32, "slc", grp="p2")
    mskc = p.sb([128, 128], BF16, "mskc", grp="p2")
    rmc = p.sb([128, 4], F32, "rmc", grp="p2")
    p.dma(tric[:, :], tri_h[:, :])
    p.dma(slc[:, :], sl_h[:, :])
    p.dma(mskc[:, :], msk_h[:, :])
    p.dma(rmc[:, :], rm_h[:, :])
    ldt = p.sb([128, 512], F32, "ldt", grp="p2")
    knt = p.sb([128, 512], BF16, "knt", grp="p2")
    vnt = p.sb([128, 512], BF16, "vnt", grp="p2")
    vp = p.sb([128, 512], BF16, "vp", grp="p2")
    ext = p.sb([128, 512], BF16, "ext", grp="p2")
    rgt = p.sb([128, 512], BF16, "rgt", grp="p2")
    kTt = [p.sb([128, 128], BF16, "kTt", grp="p2") for _ in range(4)]
    bscol = p.sb([128, 1], F32, "bscol", grp="p2")
    xrest = p.sb([128, D], F32, "xrest", grp="p2")
    qTh = [p.sb([64, 128], F32, "qTh", grp="p2") for _ in range(8)]
    uTh = [p.sb([64, 128], F32, "uTh", grp="p2") for _ in range(8)]
    qe = p.sb([128, 128], F32, "qe", grp="p2")
    Ecol = [p.sb([128, 4], F32, "Ecol", grp="p2") for _ in range(4)]
    er = p.sb([128, 512], F32, "er", grp="p2")
    wt = p.sb([128, 512], BF16, "wt", grp="p2")
    wtc = p.sb([128, 128], BF16, "wtc", grp="p2")
    S = [p.sb([128, 128], F32, "S", grp="p2") for _ in range(4)]
    Sstk = [p.sb([64, 256], F32, "Sstk", grp="p2") for _ in range(8)]
    PTm = p.sb([128, 128], BF16, "PTm", grp="p2")
    ya = p.sb([128, 64], F32, "ya", grp="p2")
    yb = p.sb([128, 64], F32, "yb", grp="p2")
    yacc = p.sb([128, 512], F32, "yacc", grp="p2")
    ysf = p.sb([128, 512], BF16, "ysf", grp="p2")
    otp = p.sb([128, 128], BF16, "otp", grp="p2")
    pL = p.ps([128, 512], F32, "pL", grp="p2")
    pT = p.ps([128, 128], F32, "pT", grp="p2")
    pPT = p.ps([128, 128], F32, "pPT", grp="p2")
    pO = p.ps([128, 512], F32, "pO", grp="p2")
    pC = p.ps([128, 256], F32, "pC", grp="p2")
    ptr2 = p.ps([128, 128], BF16, "ptr2", grp="p2")
    py = [p.ps([128, 512], F32, "py", grp="p2") for _ in range(2)]

    for kc in range(4):
        p.memset(S[kc][:, :], 0.0)
    for t in range(NT):
        p.dma(ldt[:, :], ld_hbm[t * 128:(t + 1) * 128, :])
        p.dma(knt[:, :], kn_hbm[t * 128:(t + 1) * 128, :])
        p.dma(vnt[:, :], vn_hbm[t * 128:(t + 1) * 128, :])
        p.dma(ext[:, :], ex_hbm[t * 128:(t + 1) * 128, :])
        p.dma(rgt[:, :], rg_hbm[t * 128:(t + 1) * 128, :])
        p.dma(bscol[:, 0:1], bs_hbm[t * 128:(t + 1) * 128].rearrange("(pp o) -> pp o", o=1))
        p.dma(xrest[:, :], x_h[t * 128:(t + 1) * 128, :])
        for kc in range(4):
            p.dma(kTt[kc][:, :], kT_hbm[kc * 128:(kc + 1) * 128, t * 128:(t + 1) * 128])
        for kc in range(4):
            p.mm(pL[:, kc * 128:(kc + 1) * 128], ldt[:, kc * 128:(kc + 1) * 128], tric[:, :],
                 sgc=True, wait=(kc == 0), start=(kc == 0), stop=(kc == 3))
        for kc in range(4):
            psl = pL[:, kc * 128:(kc + 1) * 128]
            p.act(qe[:, :], psl, AF.Exp)
            for hf in range(2):
                p.tt(qTh[2 * kc + hf][:, :], kTt[kc][hf * 64:(hf + 1) * 64, :],
                     qe[hf * 64:(hf + 1) * 64, :], OP.mult)
            p.ts(qe[:, :], psl, -1.0, OP.mult)
            p.ts(qe[:, :], qe[:, :], 80.0, OP.min)
            p.act(qe[:, :], qe[:, :], AF.Exp)
            for hf in range(2):
                p.tt(uTh[2 * kc + hf][:, :], kTt[kc][hf * 64:(hf + 1) * 64, :],
                     qe[hf * 64:(hf + 1) * 64, :], OP.mult)
            for cc in range(4):
                p.act(Ecol[kc][:, cc:cc + 1], psl[:, 32 * cc + 31:32 * cc + 32], AF.Exp)
        p.mm(pL[:, :], slc[:, :], ldt[:, :], sgc=True)
        p.act(er[:, :], pL[:, :], AF.Exp)
        p.tt(wt[:, :], knt[:, :], er[:, :], OP.mult)
        p.ts(vp[:, :], vnt[:, :], bscol[:, 0:1], OP.mult)
        for cc in range(4):
            for kc in range(4):
                p.vcopy(Sstk[2 * kc][:, cc * 64:(cc + 1) * 64], S[kc][0:64, 0:64])
                p.vcopy(Sstk[2 * kc + 1][:, cc * 64:(cc + 1) * 64], S[kc][64:128, 64:128])
            for kc in range(4):
                p.ts(wtc[:, :], wt[:, kc * 128:(kc + 1) * 128], rmc[:, cc:cc + 1], OP.mult)
                p.mm(pT[:, :], wtc[:, :], vp[:, kc * 128:(kc + 1) * 128], sgc=True)
                p.ts(S[kc][:, :], S[kc][:, :], Ecol[kc][:, cc:cc + 1], OP.mult)
                p.tt(S[kc][:, :], S[kc][:, :], pT[:, :], OP.add)
        for h in range(8):
            p.mm(pC[:, :], qTh[h][:, :], Sstk[h][:, :], sgc=True)
            p.mm(pPT[:, :], uTh[h][:, :], qTh[h][:, :], sgc=True)
            p.tt(PTm[:, :], pPT[:, :], mskc[:, :], OP.mult)
            p.mm(pO[:, h * 64:(h + 1) * 64], PTm[:, :], vp[:, h * 64:(h + 1) * 64], sgc=True,
                 start=(h == 0), stop=(h == 7))
            p.ts(ya[:, :], pC[:, 0:64], rmc[:, 0:1], OP.mult)
            p.stt(yb[:, :], pC[:, 64:128], rmc[:, 1:2], ya[:, :], OP.mult, OP.add)
            p.stt(ya[:, :], pC[:, 128:192], rmc[:, 2:3], yb[:, :], OP.mult, OP.add)
            p.stt(yb[:, :], pC[:, 192:256], rmc[:, 3:4], ya[:, :], OP.mult, OP.add)
            p.tt(yacc[:, h * 64:(h + 1) * 64], yb[:, :], pO[:, h * 64:(h + 1) * 64], OP.add)
        p.tt(yacc[:, :], yacc[:, :], ext[:, :], OP.add)
        p.tt(ysf[:, :], yacc[:, :], rgt[:, :], OP.mult)
        for kc in range(4):
            p.trans(ptr2[:, :], ysf[:, kc * 128:(kc + 1) * 128], c["id_sb"][:, :])
            p.vcopy(otp[:, :], ptr2[:, :])
            for dc in range(2):
                p.mm(py[dc][:, :], otp[:, :], wo[kc][:, dc * 512:(dc + 1) * 512],
                     start=(kc == 0), stop=(kc == 3), wait=(dc == 0), sgc=True)
        for dc in range(2):
            dcs = slice(dc * 512, (dc + 1) * 512)
            p.tt(er[:, :], py[dc][:, :], xrest[:, dcs], OP.add)
            p.dma(y_h[t * 128:(t + 1) * 128, dcs], er[:, :])
    return p.build() if standalone else None


def run_l0_attn_a(x, inp):
    nc = get_prog("l0_attn_a", build_l0_attn_a)
    cmn = const_inputs()
    jj = np.arange(128)
    same = (jj[:, None] // 32) == (jj[None, :] // 32)
    tri = ((jj[:, None] <= jj[None, :]) & same).astype(np.float32)
    sl = ((jj[:, None] > jj[None, :]) & same).astype(np.float32)
    msk = ((jj[:, None] < jj[None, :]) & same).astype(bf16_np())
    rm = np.zeros((128, 4), np.float32)
    for cc in range(4):
        rm[cc * 32:(cc + 1) * 32, cc] = 1.0
    sel8 = np.zeros((8, 512), np.float32)
    for h8 in range(8):
        sel8[h8, h8 * 64:(h8 + 1) * 64] = 1.0
    maps = []
    for i in range(NCORE):
        m = dict(cmn)
        m["c_tri"] = tri
        m["c_sl"] = sl
        m["c_msk"] = msk
        m["c_rm"] = rm
        m["c_sel8"] = sel8.astype(bf16_np())
        m["x"] = np.ascontiguousarray(x[i])
        m["W_sel"] = inp["L0_attn_a_W_sel"]
        m["dt_W"] = inp["L0_attn_a_dt_W"]
        m["dt_b"] = inp["L0_attn_a_dt_b"]
        m["Wk"] = inp["L0_attn_a_Wk"]
        m["Wv"] = inp["L0_attn_a_Wv"]
        m["Wr"] = inp["L0_attn_a_Wr"]
        m["Wg"] = inp["L0_attn_a_Wg"]
        m["Wo"] = inp["L0_attn_a_Wo"]
        m["base_decay"] = inp["L0_attn_a_base_decay"]
        m["time_first"] = inp["L0_attn_a_time_first"]
        maps.append(m)
    res = run_bass_kernel_spmd(nc, maps, core_ids=list(range(NCORE)))
    return np.stack([r["y"] for r in res.results], axis=0)


_PROG_CACHE = {}


def get_prog(key, builder):
    if key not in _PROG_CACHE:
        _PROG_CACHE[key] = builder()
    return _PROG_CACHE[key]


def run_l1_ffn(x, inp):
    nc = get_prog("l1_ffn", build_l1_ffn)
    cmn = const_inputs()
    maps = []
    for i in range(NCORE):
        m = dict(cmn)
        m["x"] = np.ascontiguousarray(x[i])
        m["ln_g"] = inp["L1_ffn_ln_gamma"]
        m["ln_b"] = inp["L1_ffn_ln_beta"]
        m["fc1_q"] = inp["L1_ffn_fc1_q"]
        m["fc1_s"] = inp["L1_ffn_fc1_scales"]
        m["fc1_b"] = inp["L1_ffn_fc1_bias"]
        m["fc2_q"] = inp["L1_ffn_fc2_q"]
        m["fc2_s"] = inp["L1_ffn_fc2_scales"]
        m["fc2_b"] = inp["L1_ffn_fc2_bias"]
        maps.append(m)
    res = run_bass_kernel_spmd(nc, maps, core_ids=list(range(NCORE)))
    return np.stack([r["y"] for r in res.results], axis=0)





def inputs_l0_attn_a(inp):
    jj = np.arange(128)
    same = (jj[:, None] // 32) == (jj[None, :] // 32)
    rm = np.zeros((128, 4), np.float32)
    for cc in range(4):
        rm[cc * 32:(cc + 1) * 32, cc] = 1.0
    sel8 = np.zeros((8, 512), np.float32)
    for h8 in range(8):
        sel8[h8, h8 * 64:(h8 + 1) * 64] = 1.0
    return {
        "c_tri": ((jj[:, None] <= jj[None, :]) & same).astype(np.float32),
        "c_sl": ((jj[:, None] > jj[None, :]) & same).astype(np.float32),
        "c_msk": ((jj[:, None] < jj[None, :]) & same).astype(bf16_np()),
        "c_rm": rm, "c_sel8": sel8.astype(bf16_np()),
        "W_sel": inp["L0_attn_a_W_sel"], "dt_W": inp["L0_attn_a_dt_W"],
        "dt_b": inp["L0_attn_a_dt_b"], "Wk": inp["L0_attn_a_Wk"],
        "Wv": inp["L0_attn_a_Wv"], "Wr": inp["L0_attn_a_Wr"],
        "Wg": inp["L0_attn_a_Wg"], "Wo": inp["L0_attn_a_Wo"],
        "base_decay": inp["L0_attn_a_base_decay"], "time_first": inp["L0_attn_a_time_first"],
    }


def inputs_l0_ffn(inp):
    basis = np.zeros((128, 4096), np.float32)
    for j in range(128):
        basis[j, j::128] = 1.0
    return {
        "c_basis": basis,
        "W_exp": inp["L0_ffn_W_exp"], "b_exp": inp["L0_ffn_b_exp"],
        "up_w": inp["L0_ffn_up_weights"], "up_b": inp["L0_ffn_up_bias"],
        "down_w": inp["L0_ffn_down_weights"], "W_con": inp["L0_ffn_W_con"],
        "b_con": inp["L0_ffn_b_con"], "down_b": inp["L0_ffn_down_bias"],
    }


def inputs_l0_attn_b(inp):
    jj = np.arange(128)
    return {
        "c_e1": np.maximum(jj[None, :] - jj[:, None] - 1, 0).astype(np.float32),
        "c_su": (jj[None, :] > jj[:, None]).astype(np.float32),
        "c_eye": np.eye(128, dtype=np.float32),
        "c_iota": np.ascontiguousarray(
            np.stack([jj.astype(np.float32), (127 - jj).astype(np.float32)], axis=1)),
        "tmk": inp["L0_attn_b_time_mix_k"], "tmv": inp["L0_attn_b_time_mix_v"],
        "tmr": inp["L0_attn_b_time_mix_r"], "tmg": inp["L0_attn_b_time_mix_g"],
        "td": inp["L0_attn_b_time_decay"], "tf": inp["L0_attn_b_time_faaaa"],
        "W_r": inp["L0_attn_b_W_r"], "W_k": inp["L0_attn_b_W_k"],
        "W_v": inp["L0_attn_b_W_v"], "W_g": inp["L0_attn_b_W_g"],
        "W_o": inp["L0_attn_b_W_o"], "ln_w": inp["L0_attn_b_ln_w"],
        "ln_b": inp["L0_attn_b_ln_b"],
    }


def inputs_attn(kind, inp):
    if kind == "mla":
        jj = np.arange(128)
        return {
            "c_diag": (jj[:, None] <= jj[None, :]).astype(bf16_np()),
            "W_DKV": inp["L1_attn_a_W_DKV"], "W_UK": inp["L1_attn_a_W_UK"],
            "W_UV": inp["L1_attn_a_W_UV"], "W_UQ": inp["L1_attn_a_W_UQ"],
            "W_O": inp["L1_attn_a_W_O"],
        }
    return {
        "ln_g": inp["L1_attn_b_ln_gamma"], "ln_b": inp["L1_attn_b_ln_beta"],
        "Wq": inp["L1_attn_b_Wq"], "Wk": inp["L1_attn_b_Wk"],
        "Wv": inp["L1_attn_b_Wv"], "Wo": inp["L1_attn_b_Wo"],
        "bq": inp["L1_attn_b_bq"], "bk": inp["L1_attn_b_bk"],
        "bv": inp["L1_attn_b_bv"], "bo": inp["L1_attn_b_bo"],
        "mu": np.asarray(inp["L1_attn_b_mu"], np.float32).reshape(1, 1),
    }


def inputs_l1_ffn(inp):
    return {
        "ln_g": inp["L1_ffn_ln_gamma"], "ln_b": inp["L1_ffn_ln_beta"],
        "fc1_q": inp["L1_ffn_fc1_q"], "fc1_s": inp["L1_ffn_fc1_scales"],
        "fc1_b": inp["L1_ffn_fc1_bias"], "fc2_q": inp["L1_ffn_fc2_q"],
        "fc2_s": inp["L1_ffn_fc2_scales"], "fc2_b": inp["L1_ffn_fc2_bias"],
    }



_STAGES = [
    ("s0", lambda **kw: build_l0_attn_a(**kw), inputs_l0_attn_a),
    ("s1", lambda **kw: build_l0_ffn(**kw), inputs_l0_ffn),
    ("s2", lambda **kw: build_l0_attn_b(**kw), inputs_l0_attn_b),
    ("s3", lambda **kw: build_l0_ffn(**kw), inputs_l0_ffn),
    ("s4", lambda **kw: build_attn("mla", **kw), lambda inp: inputs_attn("mla", inp)),
    ("s5", lambda **kw: build_l1_ffn(**kw), inputs_l1_ffn),
    ("s6", lambda **kw: build_attn("sig", **kw), lambda inp: inputs_attn("sig", inp)),
    ("s7", lambda **kw: build_l1_ffn(**kw), inputs_l1_ffn),
]


def build_fused():
    p = Prog()
    c = load_consts(p)
    x_h = p.dram("x", [T, D], F32)
    y_h = p.dram("y", [T, D], F32, kind="ExternalOutput")
    bufs = [p.dram("xchain0", [T, D], F32, kind="Internal"),
            p.dram("xchain1", [T, D], F32, kind="Internal")]
    cur = x_h
    for i, (nm, fn, _) in enumerate(_STAGES):
        out = y_h if i == len(_STAGES) - 1 else bufs[i % 2]
        p.stage = nm
        p.pref = nm + "_"
        fn(p=p, c=c, xin=cur, yout=out)
        p.end_stage()
        cur = out
    p.stage = ""
    p.pref = ""
    return p.build()


def kernel_fused(inp):
    nc = get_prog("fused", build_fused)
    base = const_inputs()
    for nm, _, mkin in _STAGES:
        for k, v in mkin(inp).items():
            base[nm + "_" + k] = v
    x = np.ascontiguousarray(np.asarray(inp["L0_attn_a_x"], np.float32))
    maps = []
    for i in range(NCORE):
        m = dict(base)
        m["x"] = np.ascontiguousarray(x[i])
        maps.append(m)
    res = run_bass_kernel_spmd(nc, maps, core_ids=list(range(NCORE)))
    return np.stack([r["y"] for r in res.results], axis=0).astype(np.float32)



def kernel(**inp):
    return kernel_fused(inp)


def kernel_multilaunch(**inp):
    x = np.ascontiguousarray(np.asarray(inp["L0_attn_a_x"], np.float32))
    x = run_l0_attn_a(x, inp)
    x = run_l0_ffn(x, inp)
    x = run_l0_attn_b(x, inp)
    x = run_l0_ffn(x, inp)
    x = run_attn("mla", x, inp)
    x = run_l1_ffn(x, inp)
    x = run_attn("sig", x, inp)
    x = run_l1_ffn(x, inp)
    return np.asarray(x, np.float32)
